# version 04
# baseline (speedup 1.0000x reference)
import numpy as np
import ml_dtypes
from contextlib import ExitStack
import concourse.bass as bass
import concourse.mybir as mybir
from concourse.bass_utils import run_bass_kernel_spmd

F32 = mybir.dt.float32
BF16 = mybir.dt.bfloat16
AF = mybir.ActivationFunctionType
ALU = mybir.AluOpType
AX = mybir.AxisListType
BF = ml_dtypes.bfloat16

D = 2048
L = 2048
DIN = 4096
NH = 64
HP = 64
DS = 128
DCONV = 4
CHUNK = 256
NCH = L // CHUNK
CONV_DIM = DIN + 2 * DS
DPROJ = 2 * DIN + 2 * DS + NH
NOC = DPROJ // 128
EPS = 1e-5
BIGM = 32768.0
NEXP = 6
EE = 1024
TFF = 1024
TEST_NO_YOFF = False


class Prog:
    ENG_SELF = {"vector": "sV", "scalar": "sA", "gpsimd": "sG"}
    ENGS = ("tensor", "vector", "scalar", "gpsimd", "sync")

    def __init__(self, nc):
        self.nc = nc
        self.sem = {}
        self.count = {}
        self.waited = {e: {} for e in self.ENGS}
        for s in ("sV", "sA", "sG", "sP", "dS", "dG", "dC", "dCg", "dU",
                  "dUa", "dUb", "dWa", "dWb", "dXa", "dXb", "dZOa", "dZOb",
                  "dGNa", "dGNb", "dOUT"):
            self.sem[s] = nc.alloc_semaphore(s)
            self.count[s] = 0
        self.q = None

    def begin(self):
        self.q = {e: [] for e in self.ENGS}

    def flush(self, name=None):
        with self.nc.Block(name) as blk:
            for e in self.ENGS:
                fns = self.q[e]
                if not fns:
                    continue
                def body(eng, fns=fns):
                    for f in fns:
                        f(eng)
                getattr(blk, e)(body)
        self.q = None

    def op(self, eng, fn, waits=(), inc="SELF", inc_n=1):
        if inc == "SELF":
            inc = self.ENG_SELF.get(eng)
        wl = []
        se = self.ENG_SELF.get(eng)
        if se is not None:
            wl.append((se, self.count[se]))
        for wv in waits:
            if wv is not None:
                wl.append(wv)
        W = self.waited[eng]
        wl2 = []
        for s, v in wl:
            if v > 0 and W.get(s, 0) < v:
                wl2.append((s, v))
                W[s] = v
        ev = None
        if inc:
            self.count[inc] += inc_n
            ev = (inc, self.count[inc])
        sems = self.sem
        def closure(e, wl2=wl2, fn=fn, inc=inc, inc_n=inc_n):
            for s, v in wl2:
                e.wait_ge(sems[s], v)
            inst = fn(e)
            if inc:
                inst.then_inc(sems[inc], inc_n)
        self.q[eng].append(closure)
        return ev

    DMA_SEMS = ("dS", "dG", "dC", "dCg", "dU", "dUa", "dUb", "dWa", "dWb",
                "dXa", "dXb", "dZOa", "dZOb", "dGNa", "dGNb", "dOUT")

    def dma(self, queue, out, in_, waits=(), sem=None):
        semn = sem or ("dS" if queue == "sync" else "dG")
        return self.op(queue, lambda e, o=out, i=in_: e.dma_start(o, i),
                       waits=waits, inc=semn, inc_n=16)

    def dma_snapshot(self):
        return tuple((s, self.count[s]) for s in self.DMA_SEMS
                     if self.count[s] > 0)

    def final_wait(self):
        snap = self.dma_snapshot()
        def closure(e):
            for s, v in snap:
                e.wait_ge(self.sem[s], v)
        self.q["sync"].append(closure)


def build(debug=False):
    nc = bass.Bass("TRN2", target_bir_lowering=False)
    P = Prog(nc)
    t = {}

    def din(name, shape, dt=F32):
        t[name] = nc.dram_tensor(name, list(shape), dt, kind="ExternalInput").ap()

    din("u_fm", [D, L])
    din("w_inT", [D, DPROJ])
    din("w_outT", [DIN, D])
    din("conv_w_pm", [128, 34 * 4])
    din("conv_b_pm", [128, 34])
    din("dtb_col", [NH, 1])
    din("alog_col", [NH, 1])
    din("dp_pm", [128, 32])
    din("nw_pm", [128, 32])
    din("wg", [2, D, 8])
    din("bg_col", [2, 8, 1])
    din("wup", [NEXP, D, EE])
    din("bup_pm", [NEXP, 128, 8])
    din("wsw", [NEXP, EE, EE])
    din("bsw_pm", [NEXP, 128, 8])
    din("wdn", [NEXP, EE, D])
    din("bdn", [NEXP, D])
    din("wsu", [D, 2 * EE])
    din("bsu_pm", [128, 16])
    din("wsa", [EE, EE])
    din("bsa_pm", [128, 8])
    din("wsd", [2 * EE, D])
    din("bsd_pm", [128, 16])
    din("m1a", [128, 256])
    din("m1b", [128, 256])
    din("u01_bf", [128, 512], BF16)
    din("negdiag_bf", [128, 128], BF16)
    din("sel_bf", [128, NH * 128], BF16)
    din("sel8", [8, 8 * 128])
    din("id_bf", [128, 128], BF16)
    din("id_f32", [128, 128])
    din("ones_col_bf", [128, 1], BF16)
    din("i4096_col_bf", [128, 1], BF16)
    din("i2048_col_bf", [128, 1], BF16)
    din("eps11", [1, 1])
    din("bg2", [8, 2])
    din("ones_col_f32", [128, 1])
    din("ones_row_f32", [1, 128])

    skind = "ExternalOutput" if debug else "Internal"
    d_zbf = nc.dram_tensor("d_zbf", [32, 128, L], BF16, kind=skind).ap()
    d_xbc = nc.dram_tensor("d_xbc", [34, 128, L], BF16, kind=skind).ap()
    d_gn = nc.dram_tensor("d_gn", [32, 128, L], BF16, kind=skind).ap()
    d_x1 = nc.dram_tensor("d_x1", [16, 128, L], F32, kind=skind).ap()
    d_x1bf = nc.dram_tensor("d_x1bf", [16, 128, L], BF16, kind=skind).ap()
    d_x2bf = nc.dram_tensor("d_x2bf", [16, 128, L], BF16, kind=skind).ap()
    d_out = nc.dram_tensor("out", [16, 128, L], F32, kind="ExternalOutput").ap()
    if debug:
        d_dt = nc.dram_tensor("d_dt", [NH, L], F32, kind="ExternalOutput").ap()
        d_y = nc.dram_tensor("d_y", [NCH, 128, 32 * 256], F32,
                             kind="ExternalOutput").ap()
        d_dbgE = nc.dram_tensor("d_dbgE", [6, 128, 384], BF16,
                                kind="ExternalOutput").ap()
        d_dbgS = nc.dram_tensor("d_dbgS", [6, 128, 384], BF16,
                                kind="ExternalOutput").ap()
        d_dbgX = nc.dram_tensor("d_dbgX", [3, 128, 256], BF16,
                                kind="ExternalOutput").ap()
        d_dbgP = nc.dram_tensor("d_dbgP", [2, 128, 256], F32,
                                kind="ExternalOutput").ap()
        d_x3 = nc.dram_tensor("d_x3", [16, 128, L], F32,
                              kind="ExternalOutput").ap()
        d_dbgG = nc.dram_tensor("d_dbgG", [2, 8, 1024], F32,
                                kind="ExternalOutput").ap()
        d_dbgWf = nc.dram_tensor("d_dbgWf", [2, 8, 1024], F32,
                                 kind="ExternalOutput").ap()
        d_dbgXt = nc.dram_tensor("d_dbgXt", [16, 128, 1024], BF16,
                                 kind="ExternalOutput").ap()
        d_ms = nc.dram_tensor("d_ms", [NCH, 1, 256], F32,
                              kind="ExternalOutput").ap()

    g = {}
    def gal(name, shape, dt=F32):
        g[name] = nc.alloc_sbuf_tensor("g_" + name, list(shape), dt)
        return g[name]
    for name, shape, dt in [
        ("conv_w_pm", [128, 136], F32), ("conv_b_pm", [128, 34], F32),
        ("dtb_col", [NH, 1], F32), ("alog_col", [NH, 1], F32),
        ("dp_pm", [128, 32], F32), ("nw_pm", [128, 32], F32),
        ("m1a", [128, 256], F32), ("m1b", [128, 256], F32),
        ("u01_bf", [128, 512], BF16),
        ("negdiag_bf", [128, 128], BF16),
        ("id_bf", [128, 128], BF16), ("id_f32", [128, 128], F32),
        ("ones_col_bf", [128, 1], BF16), ("ones_col_f32", [128, 1], F32),
        ("i4096_col_bf", [128, 1], BF16), ("eps11", [1, 1], F32),
        ("i2048_col_bf", [128, 1], BF16), ("bg2", [8, 2], F32),
        ("ones_row_f32", [1, 128], F32),
        ("sel8", [8, 8 * 128], F32),
        ("bg_col", [2 * 8, 1], F32),
        ("bsu_pm", [128, 16], F32), ("bsa_pm", [128, 8], F32),
        ("bsd_pm", [128, 16], F32),
        ("bup_pm", [128, NEXP * 8], F32), ("bsw_pm", [128, NEXP * 8], F32),
        ("wg_bf", [128, 2 * 16 * 8], BF16),
        ("acol", [NH, 1], F32),
    ]:
        gal(name, shape, dt)

    P.begin()
    for name in ("conv_w_pm", "conv_b_pm", "dtb_col", "alog_col", "dp_pm",
                 "nw_pm", "m1a", "m1b", "u01_bf", "negdiag_bf",
                 "id_bf", "id_f32", "ones_col_bf", "ones_col_f32",
                 "i4096_col_bf", "eps11", "i2048_col_bf", "bg2",
                 "ones_row_f32", "sel8", "bsu_pm", "bsa_pm", "bsd_pm"):
        P.dma("sync", g[name][:], t[name], sem="dC")
    P.dma("sync", g["bg_col"][:], t["bg_col"].rearrange("r n one -> (r n) one"),
          sem="dC")
    for n in range(NEXP):
        P.dma("sync", g["bup_pm"][:, n * 8:(n + 1) * 8], t["bup_pm"][n], sem="dC")
        P.dma("sync", g["bsw_pm"][:, n * 8:(n + 1) * 8], t["bsw_pm"][n], sem="dC")
    for r in range(2):
        for k in range(16):
            P.dma("gpsimd", g["wg_bf"][:, (r * 16 + k) * 8:(r * 16 + k + 1) * 8],
                  t["wg"][r, k * 128:(k + 1) * 128, :], sem="dCg")
    P.flush("p0")
    ev_const = ("dC", P.count["dC"])
    ev_constg = ("dCg", P.count["dCg"])

    _mix_stack = ExitStack()
    g["dt_sb"] = _mix_stack.enter_context(
        nc.sbuf_tensor("g_dt_sb", [NH, L], F32))

    with ExitStack() as st:
        def al(name, shape, dt=F32):
            return st.enter_context(nc.sbuf_tensor(name, list(shape), dt))
        ubuf = al("ubuf", [128, 16 * L], BF16)
        wbuf = [al(f"wbuf{i}", [128, 16 * 512], BF16) for i in range(2)]
        ebuf = [al(f"ebuf{i}", [128, 2051], F32) for i in range(2)]
        xsb = [al(f"xsb{i}", [128, L], F32) for i in range(2)]
        stg = [al(f"stg{i}", [128, L], BF16) for i in range(2)]
        sp1 = al("sp1", [NH, L], F32)
        sp2 = al("sp2", [NH, L], F32)
        pA = st.enter_context(nc.psum_tensor("pA", [128, L], F32))
        pB = st.enter_context(nc.psum_tensor("pB", [128, L], F32))

        P.begin()
        ev_u = None
        for k in range(16):
            ev_u = P.dma("gpsimd", ubuf[:, k * L:(k + 1) * L],
                         t["u_fm"][k * 128:(k + 1) * 128, :], waits=(ev_constg,),
                         sem="dU")
        ev_pad = [P.op("gpsimd", lambda e, i=i: e.memset(ebuf[i][:, 0:3], 0.0))
                  for i in range(2)]

        ogs = []
        o = 0
        while o < 66:
            ogs.append(list(range(o, min(o + 4, 66))))
            o += 4
        ogs.append([66])

        ev_wdma = {}
        ev_pe_og = {}
        ev_evac = {}
        ev_outd = {}
        ev_conv = {}
        ev_silu = {}

        for gi, og in enumerate(ogs):
            c0 = og[0] * 128
            c1 = min(og[-1] * 128 + 128, DPROJ)
            wcols = c1 - c0
            wb = wbuf[gi % 2]
            wdep = [ev_constg]
            if gi >= 2:
                wdep.append(ev_pe_og.get(gi - 2))
            ev = None
            for k in range(16):
                ev = P.dma("gpsimd", wb[:, k * 512:k * 512 + wcols],
                           t["w_inT"][k * 128:(k + 1) * 128, c0:c1],
                           waits=wdep, sem="dWa" if gi % 2 == 0 else "dWb")
            ev_wdma[gi] = ev

            for o in og:
                rows = 64 if o == 66 else 128
                ps = pA if (o % 2 == 0) else pB
                off = o * 128 - c0
                mmdep = [ev_wdma[gi], ev_u]
                if o >= 2:
                    mmdep.append(ev_evac.get(o - 2))
                evp = None
                for k in range(16):
                    for tt_ in range(4):
                        evp = P.op(
                            "tensor",
                            lambda e, ps=ps, rows=rows, k=k, tt_=tt_, wb=wb, off=off: e.matmul(
                                ps[0:rows, tt_ * 512:(tt_ + 1) * 512],
                                wb[:, k * 512 + off:k * 512 + off + rows],
                                ubuf[:, k * L + tt_ * 512:k * L + (tt_ + 1) * 512],
                                start=(k == 0), stop=(k == 15)),
                            waits=mmdep if (k == 0 and tt_ == 0) else (),
                            inc="sP" if (k == 15 and tt_ == 3) else None)
                ev_pe_og[gi] = evp

                par = o % 2
                if o < 32:
                    eva = P.op("scalar",
                               lambda e, ps=ps, par=par: e.activation(
                                   stg[par][:], ps[:], AF.Silu),
                               waits=(evp, ev_outd.get(par)))
                    ev_evac[o] = eva
                    ev_outd[par] = P.dma("sync", d_zbf[o], stg[par][:],
                                         waits=(eva,),
                                         sem="dZOa" if par == 0 else "dZOb")
                elif o < 66:
                    cx = o - 32
                    eva = P.op("scalar",
                               lambda e, ps=ps, par=par: e.activation(
                                   ebuf[par][:, 3:2051], ps[:], AF.Copy),
                               waits=(evp, ev_evac.get(o - 2), ev_pad[par],
                                      ev_conv.get(o - 2)))
                    ev_evac[o] = eva
                    e0 = P.op("vector",
                              lambda e, par=par, cx=cx: e.tensor_scalar(
                                  xsb[par][:], ebuf[par][:, 0:2048],
                                  g["conv_w_pm"][:, cx * 4:cx * 4 + 1], None,
                                  ALU.mult),
                              waits=(eva, ev_silu.get(o - 2)))
                    for kk in range(1, 4):
                        e0 = P.op("vector",
                                  lambda e, par=par, cx=cx, kk=kk: e.scalar_tensor_tensor(
                                      xsb[par][:], ebuf[par][:, kk:kk + 2048],
                                      g["conv_w_pm"][:, cx * 4 + kk:cx * 4 + kk + 1],
                                      xsb[par][:], ALU.mult, ALU.add))
                    ev_conv[o] = e0
                    eva2 = P.op("scalar",
                                lambda e, par=par, cx=cx: e.activation(
                                    stg[par][:], xsb[par][:], AF.Silu,
                                    bias=g["conv_b_pm"][:, cx:cx + 1]),
                                waits=(e0, ev_outd.get(par)))
                    ev_silu[o] = eva2
                    ev_outd[par] = P.dma("sync", d_xbc[cx], stg[par][:],
                                         waits=(eva2,),
                                         sem="dZOa" if par == 0 else "dZOb")
                else:
                    e1 = P.op("scalar",
                              lambda e, ps=ps: e.activation(
                                  sp1[:], ps[0:64, :], AF.Abs,
                                  bias=g["dtb_col"][:, 0:1]),
                              waits=(evp,))
                    e2 = P.op("scalar",
                              lambda e: e.activation(sp2[:], sp1[:], AF.Exp,
                                                     scale=-1.0))
                    e3 = P.op("scalar",
                              lambda e: e.activation(sp1[:], sp2[:], AF.Ln,
                                                     bias=1.0))
                    e4 = P.op("scalar",
                              lambda e, ps=ps: e.activation(
                                  sp2[:], ps[0:64, :], AF.Relu,
                                  bias=g["dtb_col"][:, 0:1]))
                    ev_evac[o] = e4
                    e5 = P.op("vector",
                              lambda e: e.tensor_tensor(g["dt_sb"][:], sp1[:],
                                                        sp2[:], ALU.add),
                              waits=(e3, e4))
                    e6 = P.op("scalar",
                              lambda e: e.activation(sp1[:, 0:1],
                                                     g["alog_col"][:, 0:1],
                                                     AF.Exp),
                              waits=(ev_const,))
                    e7 = P.op("vector",
                              lambda e: e.tensor_scalar(g["acol"][:, 0:1],
                                                        sp1[:, 0:1], -1.0,
                                                        None, ALU.mult),
                              waits=(e6,))
                    if debug:
                        P.dma("sync", d_dt, g["dt_sb"][:], waits=(e5,))
        P.final_wait() if debug else None
        P.flush("p1")
    with ExitStack() as st:
        def al(name, shape, dt=F32):
            return st.enter_context(nc.sbuf_tensor(name, list(shape), dt))
        xbc_sb = [al(f"xbc{i}", [128, 34 * 256], BF16) for i in range(2)]
        z_sb = [al(f"zsb{i}", [128, 32 * 256], BF16) for i in range(2)]
        y_sb = al("y_sb", [128, 32 * 256], F32)
        gn_sb = al("gn_sb", [128, 32 * 256], BF16)
        sq_sb = al("sq_sb", [128, 32 * 256], BF16)
        h_f = al("h_f", [128, NH * HP], F32)
        h_b = al("h_b", [128, NH * HP], BF16)
        sel_sb = al("sel_sb", [128, NH * 128], BF16)
        dA_f = al("dA_f", [NH, 256], F32)
        dA_t = al("dA_t", [128, 128], F32)
        cs_fm = al("cs_fm", [NH, 256], F32)
        cs_t = al("cs_t", [128, 128], F32)

        hilo = al("hilo", [128, 256], BF16)
        b2 = al("b2", [128, 128], F32)
        ldt = al("ldt", [128, 128], F32)
        ecs_bf = al("ecs_bf", [NH, 256], BF16)
        wt1 = al("wt1", [NH, 256], F32)
        wt2 = al("wt2", [NH, 256], F32)
        w_t = al("w_t", [128, 128], F32)
        dt_t = al("dt_t", [128, 128], F32)
        cd_col = al("cd_col", [NH, 1], F32)
        cd_row = al("cd_row", [1, NH], F32)
        cd_all = al("cd_all", [128, NH], F32)

        cbt_bf = al("cbt_bf", [128, 384], BF16)
        bt_bf = al("bt_bf", [128, 256], BF16)
        xt_bf = [al(f"xt_bf{i}", [128, 256], BF16) for i in range(2)]
        E_bf = [al(f"E_bf{i}", [128, 384], BF16) for i in range(2)]
        ce_bf = [al(f"ce_bf{i}", [128, 256], BF16) for i in range(2)]
        sc_bf = [al(f"sc_bf{i}", [128, 384], BF16) for i in range(2)]
        xw_bf = [al(f"xw_bf{i}", [128, 128], BF16) for i in range(2)]
        ms_row = al("ms_row", [1, 256], F32)
        zero_bf = al("zero_bf", [128, 256], BF16)
        dbgP_sb = al("dbgP_sb", [128, 256], F32)
        rstd_row = al("rstd_row", [1, 256], F32)

        pseg = [st.enter_context(nc.psum_tensor(f"pseg{i}", [128, 512], F32))
                for i in range(2)]
        pyp = st.enter_context(nc.psum_tensor("pyp", [128, 512], F32))
        ptr = [st.enter_context(nc.psum_tensor(f"ptr{i}", [128, 1024], BF16))
               for i in range(2)]
        pcum = st.enter_context(nc.psum_tensor("pcum", [128, 512], F32))
        pcbt = st.enter_context(nc.psum_tensor("pcbt", [128, 512], F32))
        pce = st.enter_context(nc.psum_tensor("pce", [128, 512], F32))

        P.begin()
        P.op("vector", lambda e: e.memset(h_f[:], 0.0))
        P.op("vector", lambda e: e.memset(zero_bf[:], 0.0))
        ev_hb0 = P.op("gpsimd", lambda e: e.memset(h_b[:], 0.0))
        for hh in range(NH):
            P.dma("sync", sel_sb[:, hh * 128:(hh + 1) * 128],
                  t["sel_bf"][:, hh * 128:(hh + 1) * 128], sem="dC")

        M = {}
        snap = lambda: {s_: P.count[s_] for s_ in ("sP", "sV", "sA", "sG", "dS", "dG")}
        M[-1] = M[-2] = {s_: 0 for s_ in ("sP", "sV", "sA", "sG", "dS", "dG")}
        ev_gn_dma = {}

        def emit_chunk(c):
            par_c = c % 2
            xb = xbc_sb[par_c]
            zb = z_sb[par_c]
            csl = slice(c * 256, (c + 1) * 256)
            ddep = (("sP", M[c - 1]["sP"]), ("sV", M[c - 1]["sV"]),
                    ("sA", M[c - 1]["sA"]))
            if c == 0:
                ddep = ddep + (("dZOa", P.count["dZOa"]),
                               ("dZOb", P.count["dZOb"]))
            dxs = "dXa" if par_c == 0 else "dXb"
            ev_x = P.dma("sync", xb[:].rearrange("p (t j) -> p t j", j=256),
                         d_xbc[:, :, csl].rearrange("t p j -> p t j"), waits=ddep,
                         sem=dxs)
            ev_z = P.dma("sync", zb[:].rearrange("p (t j) -> p t j", j=256),
                         d_zbf[:, :, csl].rearrange("t p j -> p t j"), waits=ddep,
                         sem=dxs)

            adep = (("sP", M[c - 1]["sP"]), ("sV", M[c - 1]["sV"]),
                    ("sA", M[c - 1]["sA"]), ("sG", M[c - 1]["sG"]))
            e_dA = P.op("vector", lambda e: e.tensor_scalar(
                dA_f[:], g["dt_sb"][:, csl], g["acol"][:, 0:1], None, ALU.mult),
                waits=adep)
            e_trA = P.op("tensor", lambda e: e.transpose(
                pcum[0:128, 0:64], dA_f[:, 0:128], g["id_f32"][0:64, 0:64]),
                waits=adep + (e_dA,), inc="sP")
            e_trA2 = P.op("tensor", lambda e: e.transpose(
                pcum[0:128, 64:128], dA_f[:, 128:256], g["id_f32"][0:64, 0:64]),
                inc="sP")
            e_vA = P.op("vector", lambda e: e.tensor_copy(dA_t[:], pcum[:, 0:128]),
                        waits=(e_trA2,))
            e_trD = P.op("tensor", lambda e: e.transpose(
                pcum[0:128, 128:192], g["dt_sb"][:, csl.start:csl.start + 128],
                g["id_f32"][0:64, 0:64]), waits=(e_vA,), inc="sP")
            e_trD2 = P.op("tensor", lambda e: e.transpose(
                pcum[0:128, 192:256], g["dt_sb"][:, csl.start + 128:csl.stop],
                g["id_f32"][0:64, 0:64]), inc="sP")
            e_vD = P.op("vector", lambda e: e.tensor_copy(dt_t[:], pcum[:, 128:256]),
                        waits=(e_trD2,))
            e_cf1 = P.op("tensor", lambda e: e.matmul(
                pcum[0:64, 256:512], dA_t[:, 0:64], g["m1a"][:], start=True,
                stop=False), waits=(e_vD,), inc="sP")
            e_cf2 = P.op("tensor", lambda e: e.matmul(
                pcum[0:64, 256:512], dA_t[:, 64:128], g["m1b"][:], start=False,
                stop=True), inc="sP")
            e_csfm = P.op("vector", lambda e: e.tensor_copy(
                cs_fm[:], pcum[0:64, 256:512]), waits=(e_cf2,))
            e_ct1 = P.op("tensor", lambda e: e.matmul(
                pcum[0:128, 0:64], g["m1a"][:, 0:128], dA_t[:, 0:64],
                start=True, stop=True), waits=(e_csfm,), inc="sP")
            e_ct2 = P.op("tensor", lambda e: e.matmul(
                pcum[0:128, 64:128], g["m1a"][:, 128:256], dA_t[:, 0:64],
                start=True, stop=False), inc="sP")
            e_ct3 = P.op("tensor", lambda e: e.matmul(
                pcum[0:128, 64:128], g["m1b"][:, 128:256], dA_t[:, 64:128],
                start=False, stop=True), inc="sP")
            e_cst = P.op("vector", lambda e: e.tensor_copy(cs_t[:], pcum[:, 0:128]),
                         waits=(e_ct3,))
            e_ldt = P.op("scalar", lambda e: e.activation(
                ldt[:], dt_t[:], AF.Ln), waits=(e_vD,))
            e_b2 = P.op("vector", lambda e: e.tensor_tensor(
                b2[:], ldt[:], cs_t[:], ALU.subtract), waits=(e_ldt, e_cst))
            e_hi = P.op("scalar", lambda e: e.activation(
                hilo[0:64, :], cs_fm[:], AF.Copy), waits=(e_csfm,))
            e_lo = P.op("vector", lambda e: e.tensor_tensor(
                hilo[64:128, :], cs_fm[:], hilo[0:64, :], ALU.subtract),
                waits=(e_hi,))
            e_ecs = P.op("scalar", lambda e: e.activation(
                ecs_bf[:], cs_fm[:], AF.Exp))
            e_w1 = P.op("vector", lambda e: e.tensor_scalar(
                wt1[:], cs_fm[:], cs_fm[:, 255:256], None, ALU.subtract))
            e_w2 = P.op("scalar", lambda e: e.activation(
                wt2[:], wt1[:], AF.Exp, scale=-1.0), waits=(e_w1,))
            e_w3 = P.op("vector", lambda e: e.tensor_tensor(
                wt1[:], wt2[:], g["dt_sb"][:, csl], ALU.mult), waits=(e_w2,))
            e_tw1 = P.op("tensor", lambda e: e.transpose(
                pcum[0:128, 128:192], wt1[:, 0:128], g["id_f32"][0:64, 0:64]),
                waits=(e_w3, e_cst), inc="sP")
            e_tw2 = P.op("tensor", lambda e: e.transpose(
                pcum[0:128, 192:256], wt1[:, 128:256], g["id_f32"][0:64, 0:64]),
                inc="sP")
            e_vw = P.op("vector", lambda e: e.tensor_copy(w_t[:], pcum[:, 128:256]),
                        waits=(e_tw2,))
            e_cd1 = P.op("scalar", lambda e: e.activation(
                cd_col[:, 0:1], cs_fm[:, 255:256], AF.Exp))
            e_cd2 = P.op("tensor", lambda e: e.transpose(
                pcum[0:1, 256:320], cd_col[:, 0:1], g["id_f32"][0:64, 0:64]),
                waits=(("sA", e_cd1[1]), e_vw), inc="sP")
            e_cd3 = P.op("vector", lambda e: e.tensor_copy(
                cd_row[:], pcum[0:1, 256:320]), waits=(e_cd2,))
            e_cd4 = P.op("tensor", lambda e: e.matmul(
                pcum[0:128, 320:384], g["ones_row_f32"][:], cd_row[:],
                start=True, stop=True), waits=(e_cd3,), inc="sP")
            e_cd5 = P.op("vector", lambda e: e.tensor_copy(
                cd_all[:], pcum[:, 320:384]), waits=(e_cd4,))
            e_cb1 = P.op("tensor", lambda e: e.matmul(
                pcbt[:, 0:256], xb[:, 32 * 256:32 * 256 + 128],
                xb[:, 33 * 256:34 * 256], start=True, stop=True),
                waits=(ev_x,), inc="sP")
            e_cb2 = P.op("tensor", lambda e: e.matmul(
                pcbt[:, 256:512], xb[:, 32 * 256 + 128:33 * 256],
                xb[:, 33 * 256:34 * 256], start=True, stop=True), inc="sP")
            e_cbt0 = P.op("scalar", lambda e: e.activation(
                cbt_bf[:, 0:256], pcbt[:, 0:256], AF.Copy), waits=(e_cb2,))
            e_cbt = P.op("scalar", lambda e: e.activation(
                cbt_bf[:, 256:384], pcbt[:, 384:512], AF.Copy))
            e_bt1 = P.op("tensor", lambda e: e.transpose(
                ptr[0][:, 0:128], xb[:, 32 * 256:32 * 256 + 128], g["id_bf"][:]),
                waits=(ev_x,), inc="sP")
            e_bt2 = P.op("tensor", lambda e: e.transpose(
                ptr[0][:, 128:256], xb[:, 32 * 256 + 128:33 * 256], g["id_bf"][:]),
                inc="sP")
            e_btv = P.op("scalar", lambda e: e.activation(
                bt_bf[:], ptr[0][:, 0:256], AF.Copy), waits=(e_bt2,))

            dbg_ev = {}
            ev_seg = {}
            ev_brcE = {}
            ev_E = {}
            ev_sc = {}
            ev_ce = {}
            ev_xw = {}
            ev_ydiag = {}
            ev_yoff = {}
            ev_states = {}
            ev_hupd = {}
            ev_hbf = {}
            ev_xtv = {}
            ev_evac = {}
            ev_g = {}
            ev_sq = {}

            def emit_xtr(f):
                pb = ptr[f % 2]
                w = [ev_x, ev_xtv.get(f - 2)]
                if f <= 1:
                    w.append(e_btv)
                e1 = P.op("tensor", lambda e: e.transpose(
                    pb[:, 256:384], xb[:, f * 256:f * 256 + 128], g["id_bf"][:]),
                    waits=w, inc="sP")
                e2 = P.op("tensor", lambda e: e.transpose(
                    pb[:, 384:512], xb[:, f * 256 + 128:(f + 1) * 256],
                    g["id_bf"][:]), inc="sP")
                ev_xtv[f] = P.op("scalar", lambda e: e.activation(
                    xt_bf[f % 2][:], pb[:, 256:512], AF.Copy),
                    waits=(e2, ev_ydiag.get(2 * (f - 2) + 1),
                           ev_xw.get(2 * (f - 2) + 1)))
                if debug and c == 0 and f < 3:
                    dbg_ev[("X", f)] = P.dma("sync", d_dbgX[f], xt_bf[f % 2][:],
                                             waits=(ev_xtv[f],))

            def emit_seg(h):
                par = h % 2
                ps = pseg[par]
                w = [e_lo, ev_E.get(h - 2)]
                rhs2 = hilo[:].rearrange("p (r c) -> p r c", r=1).broadcast_to(
                    [128, 2, 256])
                P.op("tensor", lambda e, rhs2=rhs2: e.matmul(
                    ps[:, 0:512], sel_sb[:, h * 128:(h + 1) * 128], rhs2,
                    start=True, stop=False), waits=w, inc=None)
                ev_seg[h] = P.op("tensor", lambda e: e.matmul(
                    ps[:, 0:512], g["negdiag_bf"][:], g["u01_bf"][:],
                    start=False, stop=True), inc="sP")

            def emit_brcE(h):
                par = h % 2
                pc = pce if par == 0 else pcum
                w = [e_ecs, ev_ce.get(h - 2), ev_hupd.get(h - 2)]
                if par == 1:
                    w.append(e_cd5)
                ev_brcE[h] = P.op("tensor", lambda e, pc=pc: e.matmul(
                    pc[:, 0:256], sel_sb[0:64, h * 128:(h + 1) * 128], ecs_bf[:],
                    start=True, stop=True), waits=w, inc="sP")

            def emit_E(h):
                par = h % 2
                ps = pseg[par]
                w = [ev_seg[h], ev_sc.get(h - 2), e_b2,
                     dbg_ev.get(("E", h - 2))]
                P.op("scalar", lambda e: e.activation(
                    E_bf[par][:, 0:256], ps[:, 0:256], AF.Exp,
                    bias=b2[:, h:h + 1]), waits=w)
                ev_E[h] = P.op("scalar", lambda e: e.activation(
                    E_bf[par][:, 256:384], ps[:, 384:512], AF.Exp,
                    bias=b2[:, 64 + h:64 + h + 1]))
                if debug and c == 0 and h < 6:
                    dbg_ev[("E", h)] = P.dma("sync", d_dbgE[h], E_bf[par][:],
                                             waits=(ev_E[h],))

            def emit_ce(h):
                par = h % 2
                pc = pce if par == 0 else pcum
                w = [ev_brcE[h], ev_yoff.get(h - 2)]
                ev_ce[h] = P.op("vector", lambda e, pc=pc: e.tensor_tensor(
                    ce_bf[par][:], xb[:, 33 * 256:34 * 256], pc[:, 0:256],
                    ALU.mult), waits=w)

            def emit_scores(h):
                par = h % 2
                w = [ev_E[h], e_cbt, ev_ydiag.get(h - 2),
                     dbg_ev.get(("S", h - 2))]
                ev_sc[h] = P.op("vector", lambda e: e.tensor_tensor(
                    sc_bf[par][:, 0:384], E_bf[par][:, 0:384],
                    cbt_bf[:, 0:384], ALU.mult), waits=w)
                if debug and c == 0 and h < 6:
                    dbg_ev[("S", h)] = P.dma("sync", d_dbgS[h], sc_bf[par][:],
                                             waits=(ev_sc[h],))

            def emit_xw(h):
                par = h % 2
                f = h // 2
                s = (h % 2) * 64
                w = [ev_xtv.get(f), e_vw, ev_states.get(h - 2)]
                P.op("gpsimd", lambda e: e.tensor_scalar(
                    xw_bf[par][:, 0:64], xt_bf[f % 2][:, s:s + 64],
                    w_t[:, h:h + 1], None, ALU.mult), waits=w)
                ev_xw[h] = P.op("gpsimd", lambda e: e.tensor_scalar(
                    xw_bf[par][:, 64:128], xt_bf[f % 2][:, 128 + s:128 + s + 64],
                    w_t[:, 64 + h:64 + h + 1], None, ALU.mult))

            def emit_compute(h):
                par = h % 2
                scpar = h % 2
                f = h // 2
                s = (h % 2) * 64
                w = [ev_sc[h], ev_xtv.get(f)]
                if h % 2 == 0:
                    w.append(ev_evac.get(f - 1))
                tp = None if h % 2 == 0 else (0, 64)
                P.op("tensor", lambda e, tp=tp: e.matmul(
                    pyp[s:s + 64, 0:256], xt_bf[f % 2][:, s:s + 64],
                    sc_bf[scpar][:, 0:256], start=True, stop=False,
                    tile_position=tp), waits=w, inc=None)
                ev_ydiag[h] = P.op("tensor", lambda e, tp=tp: e.matmul(
                    pyp[s:s + 64, 128:256], xt_bf[f % 2][:, 128 + s:128 + s + 64],
                    sc_bf[scpar][:, 256:384], start=False, stop=False,
                    tile_position=tp), inc="sP")
                ev_yoff[h] = P.op("tensor", lambda e, tp=tp: e.matmul(
                    pyp[s:s + 64, 0:256], h_b[:, h * 64:(h + 1) * 64],
                    ce_bf[par][:], start=False, stop=(h % 2 == 1),
                    tile_position=tp),
                    waits=(ev_ce[h], ev_hb0), inc="sP") if not TEST_NO_YOFF else P.op(
                    "tensor", lambda e, tp=tp: e.matmul(
                    pyp[s:s + 64, 0:256], g["id_bf"][:, 0:64], zero_bf[:],
                    start=False, stop=(h % 2 == 1), tile_position=tp),
                    waits=(ev_ce[h],), inc="sP")
                pc = pce if par == 0 else pcum
                w3 = [ev_xw[h], e_btv, ev_ce[h], ev_hupd.get(h - 2)]
                P.op("tensor", lambda e, pc=pc: e.matmul(
                    pc[:, 256:320], bt_bf[:, 0:128], xw_bf[par][:, 0:64],
                    start=False, stop=False), waits=w3, inc=None)
                ev_states[h] = P.op("tensor", lambda e, pc=pc: e.matmul(
                    pc[:, 256:320], bt_bf[:, 128:256], xw_bf[par][:, 64:128],
                    start=False, stop=True), inc="sP")

            def emit_hupd(h):
                par = h % 2
                pc = pce if par == 0 else pcum
                w = [ev_states[h], ev_yoff[h], e_cd5]
                ev_hupd[h] = P.op("vector", lambda e, pc=pc: e.scalar_tensor_tensor(
                    h_f[:, h * 64:(h + 1) * 64], h_f[:, h * 64:(h + 1) * 64],
                    cd_all[:, h:h + 1], pc[:, 256:320], ALU.mult, ALU.add),
                    waits=w)
                ev_hbf[h] = P.op("gpsimd", lambda e: e.tensor_copy(
                    h_b[:, h * 64:(h + 1) * 64], h_f[:, h * 64:(h + 1) * 64]),
                    waits=(ev_hupd[h],))

            def emit_evac(f):
                if debug and c == 0 and f < 2:
                    epv = P.op("scalar", lambda e: e.activation(
                        dbgP_sb[:], pyp[:, 0:256], AF.Copy),
                        waits=(ev_yoff[2 * f + 1],))
                    dbg_ev[("P", f)] = P.dma("sync", d_dbgP[f], dbgP_sb[:],
                                             waits=(epv,))
                w = [ev_yoff[2 * f + 1], ("sV", M[c - 1]["sV"]),
                     ("sA", M[c - 1]["sA"]), ev_x, dbg_ev.get(("P", f))]
                ev_evac[f] = P.op("vector", lambda e: e.scalar_tensor_tensor(
                    y_sb[:, f * 256:(f + 1) * 256], xb[:, f * 256:(f + 1) * 256],
                    g["dp_pm"][:, f:f + 1], pyp[:, 0:256], ALU.mult, ALU.add),
                    waits=w)
                ev_g[f] = P.op("gpsimd", lambda e: e.tensor_tensor(
                    y_sb[:, f * 256:(f + 1) * 256], y_sb[:, f * 256:(f + 1) * 256],
                    zb[:, f * 256:(f + 1) * 256], ALU.mult),
                    waits=(ev_z, ev_evac[f]))
                ev_sq[f] = P.op("scalar", lambda e: e.activation(
                    sq_sb[:, f * 256:(f + 1) * 256], y_sb[:, f * 256:(f + 1) * 256],
                    AF.Square), waits=(ev_g[f],))

            emit_xtr(0)
            emit_xtr(1)
            for h in range(NH + 1):
                if h < NH:
                    emit_seg(h)
                    emit_brcE(h)
                    emit_E(h)
                    emit_ce(h)
                    emit_scores(h)
                    emit_xw(h)
                if h >= 1:
                    hh = h - 1
                    emit_compute(hh)
                    emit_hupd(hh)
                    if hh % 2 == 1:
                        emit_evac(hh // 2)
                        if (hh // 2 + 2) <= 31:
                            emit_xtr(hh // 2 + 2)

            e_st = None
            for f in range(32):
                e_st = P.op("tensor", lambda e, f=f: e.matmul(
                    pcbt[0:1, 0:256], g["i4096_col_bf"][:],
                    sq_sb[:, f * 256:(f + 1) * 256],
                    start=(f == 0), stop=(f == 31)),
                    waits=(ev_sq[f],), inc="sP" if f == 31 else None)
            e_ln = P.op("scalar", lambda e: e.activation(
                ms_row[:], pcbt[0:1, 0:256], AF.Ln, bias=g["eps11"][0:1, 0:1]),
                waits=(e_st,))
            e_rstd = P.op("scalar", lambda e: e.activation(
                rstd_row[:], ms_row[:], AF.Exp, scale=-0.5))
            e_rs = P.op("tensor", lambda e: e.matmul(
                pcbt[:, 256:512], g["ones_row_f32"][:], rstd_row[:],
                start=True, stop=True), waits=(e_rstd,), inc="sP")
            evdy = None
            if debug:
                evdy = P.dma("sync", d_y[c], y_sb[:],
                             waits=tuple(ev_g[f] for f in range(32)))
                P.dma("sync", d_ms[c], ms_row[:], waits=(("sA", e_rstd[1]),))
            e_gnl = None
            for f in range(32):
                e_gnl = P.op("vector", lambda e, f=f: e.scalar_tensor_tensor(
                    gn_sb[:, f * 256:(f + 1) * 256], y_sb[:, f * 256:(f + 1) * 256],
                    g["nw_pm"][:, f:f + 1], pcbt[:, 256:512], ALU.mult, ALU.mult),
                    waits=(e_rs, ev_g[f], ev_gn_dma.get(c - 1)))
            ev_gn_dma[c] = P.dma("sync",
                                 d_gn[:, :, csl].rearrange("t p j -> p t j"),
                                 gn_sb[:].rearrange("p (t j) -> p t j", j=256),
                                 waits=(e_gnl,),
                                 sem="dGNa" if par_c == 0 else "dGNb")
            M[c] = snap()

        for c in range(NCH):
            emit_chunk(c)
        if debug:
            P.final_wait()
        P.flush("p2a")
    _mix_stack.close()

    with ExitStack() as st:
        def al(name, shape, dt=F32):
            return st.enter_context(nc.sbuf_tensor(name, list(shape), dt))
        gn_t = al("gn_t", [128, 32 * 1024], BF16)
        wo_bf = [al(f"wo_bf{i}", [128, 32 * 128], BF16) for i in range(2)]
        u_st = [al(f"u_st{i}", [128, 1024], F32) for i in range(2)]
        r_sb = al("r_sb", [128, 16 * 1024], F32)
        sqb = al("sqb", [128, 1024], BF16)
        murow = al("murow", [1, 1024], F32)
        rstd2 = al("rstd2", [1, 1024], F32)
        mubc = al("mubc", [128, 1024], F32)
        rsbc = mubc
        xst = [al(f"xst{i}", [128, 1024], F32) for i in range(2)]
        pd = [st.enter_context(nc.psum_tensor(f"pd{i}", [128, 1024], F32))
              for i in range(2)]
        pms2 = st.enter_context(nc.psum_tensor("pms2", [128, 1024], F32))

        P.begin()
        for tt_ in range(2):
            tsl = slice(tt_ * 1024, (tt_ + 1) * 1024)
            tdep = (("sP", P.count["sP"]), ("sV", P.count["sV"]),
                    ("sA", P.count["sA"]), ("sG", P.count["sG"])
                    ) + P.dma_snapshot()
            ev_gn = P.dma("sync", gn_t[:].rearrange("p (f j) -> p f j", j=1024),
                          d_gn[:, :, tsl].rearrange("f p j -> p f j"), waits=tdep,
                          sem="dXa" if tt_ == 0 else "dXb")
            ev_wo = {}
            ev_wo[0] = P.dma("gpsimd",
                             wo_bf[0][:].rearrange("p (f j) -> p f j", j=128),
                             t["w_outT"][:, 0:128].rearrange("(f p) c -> p f c", p=128),
                             waits=tdep, sem="dWa")
            ev_u = {}
            ev_mm = {}
            ev_r = {}
            ev_stat = {}
            for dp in range(8):
                for dd in (2 * dp, 2 * dp + 1):
                    if dd + 1 < 16:
                        ev_wo[dd + 1] = P.dma(
                            "gpsimd",
                            wo_bf[(dd + 1) % 2][:].rearrange("p (f j) -> p f j", j=128),
                            t["w_outT"][:, (dd + 1) * 128:(dd + 2) * 128].rearrange(
                                "(f p) c -> p f c", p=128),
                            waits=(ev_mm.get(dd - 1),),
                            sem="dWa" if (dd + 1) % 2 == 0 else "dWb")
                    ev_u[dd] = P.dma("sync", u_st[dd % 2][:],
                                     t["u_fm"][dd * 128:(dd + 1) * 128, tsl],
                                     waits=(ev_r.get(dd - 2),),
                                     sem="dUa" if dd % 2 == 0 else "dUb")
                    ps = pd[dd % 2]
                    evm = None
                    for fi in range(32):
                        for nn in range(2):
                            evm = P.op(
                                "tensor",
                                lambda e, ps=ps, dd=dd, fi=fi, nn=nn: e.matmul(
                                    ps[:, nn * 512:(nn + 1) * 512],
                                    wo_bf[dd % 2][:, fi * 128:(fi + 1) * 128],
                                    gn_t[:, fi * 1024 + nn * 512:
                                         fi * 1024 + (nn + 1) * 512],
                                    start=(fi == 0), stop=(fi == 31)),
                                waits=(ev_wo[dd], ev_gn, ev_r.get(dd - 2))
                                if fi == 0 and nn == 0 else (),
                                inc="sP" if (fi == 31 and nn == 1) else None)
                    ev_mm[dd] = evm
                    ev_r[dd] = P.op("vector", lambda e, ps=ps, dd=dd: e.tensor_tensor(
                        r_sb[:, dd * 1024:(dd + 1) * 1024], ps[:, 0:1024],
                        u_st[dd % 2][:], ALU.add), waits=(evm, ev_u[dd]))
                    for nn in range(2):
                        ev_stat[dd] = P.op(
                            "tensor",
                            lambda e, dd=dd, nn=nn: e.matmul(
                                pms2[0:1, nn * 512:(nn + 1) * 512],
                                g["ones_col_f32"][:],
                                r_sb[:, dd * 1024 + nn * 512:dd * 1024 + (nn + 1) * 512],
                                start=(dd == 0), stop=(dd == 15)),
                            waits=(ev_r[dd],) if nn == 0 else (),
                            inc="sP" if (dd == 15 and nn == 1) else None)
            e_mu = P.op("vector", lambda e: e.tensor_scalar(
                murow[:], pms2[0:1, 0:1024], 1.0 / 2048.0, None, ALU.mult),
                waits=(ev_stat[15],))
            e_mb = P.op("tensor", lambda e: e.matmul(
                pms2[:, 0:512], g["ones_row_f32"][:], murow[:, 0:512],
                start=True, stop=True), waits=(e_mu,), inc="sP")
            e_mb2 = P.op("tensor", lambda e: e.matmul(
                pms2[:, 512:1024], g["ones_row_f32"][:], murow[:, 512:1024],
                start=True, stop=True), inc="sP")
            e_mbv = P.op("vector", lambda e: e.tensor_copy(mubc[:], pms2[:, 0:1024]),
                         waits=(e_mb2,))
            ev_xc = {}
            ev_sq2 = {}
            ev_vst = None
            for dd in range(16):
                ev_xc[dd] = P.op("vector", lambda e, dd=dd: e.tensor_tensor(
                    r_sb[:, dd * 1024:(dd + 1) * 1024],
                    r_sb[:, dd * 1024:(dd + 1) * 1024], mubc[:], ALU.subtract),
                    waits=(e_mbv,))
                ev_sq2[dd] = P.op("scalar", lambda e, dd=dd: e.activation(
                    sqb[:], r_sb[:, dd * 1024:(dd + 1) * 1024], AF.Square),
                    waits=(ev_xc[dd], ev_vst))
                for nn in range(2):
                    ev_vst = P.op(
                        "tensor",
                        lambda e, dd=dd, nn=nn: e.matmul(
                            pms2[0:1, nn * 512:(nn + 1) * 512],
                            g["i2048_col_bf"][:],
                            sqb[:, nn * 512:(nn + 1) * 512],
                            start=(dd == 0), stop=(dd == 15)),
                        waits=(ev_sq2[dd], e_mbv) if nn == 0 else (),
                        inc="sP" if nn == 1 else None)
            e_ln2 = P.op("scalar", lambda e: e.activation(
                rstd2[:], pms2[0:1, 0:1024], AF.Ln, bias=g["eps11"][0:1, 0:1]),
                waits=(ev_vst,))
            e_rst2 = P.op("scalar", lambda e: e.activation(
                rstd2[:], rstd2[:], AF.Exp, scale=-0.5))
            e_rb = P.op("tensor", lambda e: e.matmul(
                pms2[:, 0:512], g["ones_row_f32"][:], rstd2[:, 0:512],
                start=True, stop=True), waits=(e_rst2,), inc="sP")
            e_rb2 = P.op("tensor", lambda e: e.matmul(
                pms2[:, 512:1024], g["ones_row_f32"][:], rstd2[:, 512:1024],
                start=True, stop=True), inc="sP")
            e_rbv = P.op("vector", lambda e: e.tensor_copy(rsbc[:], pms2[:, 0:1024]),
                         waits=(e_rb2,))
            ev_od = {}
            for dd in range(16):
                ex = P.op("vector", lambda e, dd=dd: e.tensor_tensor(
                    xst[dd % 2][:], r_sb[:, dd * 1024:(dd + 1) * 1024], rsbc[:],
                    ALU.mult), waits=(e_rbv, ev_od.get(dd - 2)))
                e1 = P.dma("sync", d_x1[dd, :, tsl], xst[dd % 2][:], waits=(ex,),
                           sem="dZOa" if dd % 2 == 0 else "dZOb")
                ev_od[dd] = P.dma("gpsimd", d_x1bf[dd, :, tsl], xst[dd % 2][:],
                                  waits=(ex,),
                                  sem="dZOa" if dd % 2 == 0 else "dZOb")
        P.flush("p2b")

    with ExitStack() as st:
        def al(name, shape, dt=F32):
            return st.enter_context(nc.sbuf_tensor(name, list(shape), dt))
        X1 = al("X1", [128, 16 * 1024], F32)
        xtb = al("xtb", [128, 16 * 1024], BF16)
        wsb = al("wsb", [128, 16 * 1024], BF16)
        hbf = al("hbf", [128, 8 * 1024], BF16)
        h2w = al("h2w", [128, 8 * 1024], BF16)
        wbc_sb = al("wbc_sb", [128, 1024], F32)
        x1s = [al(f"x1s{i}", [128, 1024], F32) for i in range(2)]
        sqs = al("sqs", [128, 1024], F32)
        gfm = al("gfm", [8, 1024], F32)
        gt_t = al("gt_t", [128, 64], F32)
        w_tt = al("w_tt", [128, 64], F32)
        w_fm = al("w_fm", [8, 1024], F32)
        mcol = al("mcol", [128, 4], F32)
        bdn_sb = al("bdn_sb", [NEXP, 2048], F32)
        pu = [st.enter_context(nc.psum_tensor(f"pu{i}", [128, 1024], F32))
              for i in range(2)]
        pdn = [st.enter_context(nc.psum_tensor(f"pdn{i}", [128, 1024], F32))
               for i in range(2)]

        P.begin()
        ev_bdn = None
        for n in range(NEXP):
            ev_bdn = P.dma("sync", bdn_sb[n:n + 1, :], t["bdn"][n:n + 1, :],
                           sem="dC")
        carry_mml = [{}]

        def sA_now():
            return ("sA", P.count["sA"])

        def sV_now():
            return ("sV", P.count["sV"])

        for rr in range(2):
            for tt_ in range(2):
                tsl = slice(tt_ * 1024, (tt_ + 1) * 1024)
                tdep = (("sP", P.count["sP"]), ("sV", P.count["sV"]),
                        ("sA", P.count["sA"]), ("sG", P.count["sG"])
                        ) + P.dma_snapshot()
                src = d_x1bf if rr == 0 else d_x2bf
                ev_xt = P.dma("sync",
                              xtb[:].rearrange("p (d j) -> p d j", j=1024),
                              src[:, :, tsl].rearrange("d p j -> p d j"),
                              waits=tdep, sem="dXa" if tt_ == 0 else "dXb")
                evg = None
                for k in range(16):
                    for nn in range(2):
                        evg = P.op("tensor", lambda e, k=k, nn=nn, rr=rr: e.matmul(
                            pu[0][0:8, nn * 512:(nn + 1) * 512],
                            g["wg_bf"][:, (rr * 16 + k) * 8:(rr * 16 + k + 1) * 8],
                            xtb[:, k * 1024 + nn * 512:k * 1024 + (nn + 1) * 512],
                            start=(k == 0), stop=(k == 15)),
                            waits=(ev_xt,) + tdep if (k == 0 and nn == 0) else (),
                            inc="sP" if (k == 15 and nn == 1) else None)
                e_gf = P.op("scalar", lambda e, rr=rr: e.activation(
                    gfm[:], pu[0][0:8, 0:1024], AF.Identity,
                    bias=g["bg2"][:, rr:rr + 1]), waits=(evg,))
                ev_gt = None
                for j in range(8):
                    ev_gt = P.op("tensor", lambda e, j=j: e.transpose(
                        pdn[0][0:128, j * 8:(j + 1) * 8],
                        gfm[:, j * 128:(j + 1) * 128], g["id_f32"][0:8, 0:8]),
                        waits=(e_gf,) if j == 0 else (), inc="sP")
                e_gtv = P.op("vector", lambda e: e.tensor_copy(
                    gt_t[:], pdn[0][:, 0:64]), waits=(ev_gt,))
                e_wt = None
                for j in range(8):
                    gsl = gt_t[:, j * 8:(j + 1) * 8]
                    wsl = w_tt[:, j * 8:(j + 1) * 8]
                    P.op("vector", lambda e, gsl=gsl: e.tensor_reduce(
                        mcol[:, 0:1], gsl, AX.X, ALU.min), waits=(e_gtv,))
                    P.op("vector", lambda e, gsl=gsl, wsl=wsl: e.tensor_scalar(
                        wsl, gsl, mcol[:, 0:1], None, ALU.is_equal))
                    P.op("vector", lambda e, gsl=gsl, wsl=wsl: e.scalar_tensor_tensor(
                        wsl, wsl, 32768.0, gsl, ALU.mult, ALU.add))
                    P.op("vector", lambda e, wsl=wsl: e.tensor_reduce(
                        mcol[:, 1:2], wsl, AX.X, ALU.min))
                    P.op("vector", lambda e, gsl=gsl, wsl=wsl: e.tensor_scalar(
                        wsl, gsl, mcol[:, 1:2], None, ALU.is_gt))
                    e6 = P.op("vector", lambda e, gsl=gsl: e.tensor_reduce(
                        mcol[:, 2:3], gsl, AX.X, ALU.max, negate=True))
                    e7 = P.op("scalar", lambda e, gsl=gsl, j=j: e.activation(
                        sqs[:, j * 8:(j + 1) * 8], gsl, AF.Exp,
                        bias=mcol[:, 2:3]), waits=(e6,))
                    P.op("vector", lambda e, wsl=wsl, j=j: e.tensor_tensor(
                        wsl, wsl, sqs[:, j * 8:(j + 1) * 8], ALU.mult),
                        waits=(e7,))
                    P.op("vector", lambda e, wsl=wsl: e.tensor_reduce(
                        mcol[:, 3:4], wsl, AX.X, ALU.add))
                    P.op("vector", lambda e: e.reciprocal(
                        mcol[:, 3:4], mcol[:, 3:4]))
                    e_wt = P.op("vector", lambda e, wsl=wsl: e.tensor_scalar(
                        wsl, wsl, mcol[:, 3:4], None, ALU.mult))
                ev_wf = None
                for j in range(8):
                    ev_wf = P.op("tensor", lambda e, j=j: e.transpose(
                        pdn[0][0:8, j * 128:(j + 1) * 128],
                        w_tt[:, j * 8:(j + 1) * 8], g["id_f32"][:]),
                        waits=(e_wt, e_gtv) if j == 0 else (), inc="sP")
                e_wfv = P.op("vector", lambda e: e.tensor_copy(
                    w_fm[:], pdn[0][0:8, 0:1024]), waits=(ev_wf,))
                if debug and tt_ == 0:
                    P.dma("sync", d_dbgG[rr], gfm[:], waits=(e_gf,))
                    ev_dwf = P.dma("sync", d_dbgWf[rr], w_fm[:], waits=(e_wfv,))
                    if rr == 1:
                        P.dma("sync",
                              d_dbgXt.rearrange("d p j -> p d j"),
                              xtb[:].rearrange("p (d j) -> p d j", j=1024),
                              waits=(ev_xt,))

                R = [wsb[:, 0:8192], wsb[:, 8192:16384]]
                ev_evp = {}
                ev_hb = {}
                ev_h2 = {}
                ev_wbcv = {}
                ev_x1dma = {}
                phases = []

                def ph_up(tag, src_fn, bias_fn, half):
                    def streams(reg_base):
                        return [(reg_base + k * 512, 512,
                                 src_fn(k, half)) for k in range(16)]
                    def mms(Rv, evs, sA0):
                        if half == 0 and isinstance(tag, int):
                            e_wb1 = P.op("tensor", lambda e, n=tag: e.matmul(
                                pdn[1][:, 0:512],
                                g["sel8"][:, n * 128:(n + 1) * 128],
                                w_fm[0:8, 0:512], start=True, stop=True),
                                waits=(e_wfv, ev_evp.get(("prev15",)),
                                       ), inc="sP")
                            e_wb2 = P.op("tensor", lambda e, n=tag: e.matmul(
                                pdn[1][:, 512:1024],
                                g["sel8"][:, n * 128:(n + 1) * 128],
                                w_fm[0:8, 512:1024], start=True, stop=True),
                                inc="sP")
                            ev_wbcv[tag] = P.op("vector", lambda e: e.tensor_copy(
                                wbc_sb[:], pdn[1][:, 0:1024]),
                                waits=(e_wb2, ev_h2.get(("wbc_last",))))
                        evm = None
                        for o in range(half * 4, half * 4 + 4):
                            for k in range(16):
                                for nn in range(2):
                                    evm = P.op(
                                        "tensor",
                                        lambda e, o=o, k=k, nn=nn, Rv=Rv, half=half: e.matmul(
                                            pu[o % 2][:, nn * 512:(nn + 1) * 512],
                                            Rv[:, k * 512 + (o - half * 4) * 128:
                                               k * 512 + (o - half * 4) * 128 + 128],
                                            xtb[:, k * 1024 + nn * 512:
                                                k * 1024 + (nn + 1) * 512],
                                            start=(k == 0), stop=(k == 15)),
                                        waits=(evs, ev_xt,
                                               ("sA", ev_hb[(tag, o - 2)][1])
                                               if (tag, o - 2) in ev_hb else sA0)
                                        if (k == 0 and nn == 0) else (),
                                        inc="sP" if (k == 15 and nn == 1) else None)
                            ev_hb[(tag, o)] = P.op(
                                "scalar", lambda e, o=o: e.activation(
                                    hbf[:, o * 1024:(o + 1) * 1024],
                                    pu[o % 2][:, 0:1024], AF.Identity,
                                    bias=bias_fn(o)),
                                waits=(evm,))
                        return evm
                    return streams, mms

                def ph_sw(tag, src_fn, bias_fn, use_wbc):
                    def streams(reg_base):
                        return [(reg_base + k2 * 1024, 1024, src_fn(k2))
                                for k2 in range(8)]
                    def mms(Rv, evs, sA0):
                        evm = None
                        for o2 in range(8):
                            for k2 in range(8):
                                for nn in range(2):
                                    evm = P.op(
                                        "tensor",
                                        lambda e, o2=o2, k2=k2, nn=nn, Rv=Rv: e.matmul(
                                            pu[o2 % 2][:, nn * 512:(nn + 1) * 512],
                                            Rv[:, k2 * 1024 + o2 * 128:
                                               k2 * 1024 + o2 * 128 + 128],
                                            hbf[:, k2 * 1024 + nn * 512:
                                                k2 * 1024 + (nn + 1) * 512],
                                            start=(k2 == 0), stop=(k2 == 7)),
                                        waits=(evs, ("sA", ev_hb[(tag, 7)][1]),
                                               ev_h2.get((tag, o2 - 2)))
                                        if (k2 == 0 and nn == 0) else (),
                                        inc="sP" if (k2 == 7 and nn == 1) else None)
                            e_si = P.op("scalar", lambda e, o2=o2: e.activation(
                                sqs[:], pu[o2 % 2][:, 0:1024], AF.Silu,
                                bias=bias_fn(o2)),
                                waits=(evm, ev_h2.get((tag, o2 - 1))))
                            if use_wbc:
                                e_t1 = P.op("vector", lambda e, o2=o2: e.tensor_tensor(
                                    sqs[:], sqs[:], hbf[:, o2 * 1024:(o2 + 1) * 1024],
                                    ALU.mult), waits=(e_si,))
                                ev_h2[(tag, o2)] = P.op(
                                    "vector", lambda e, o2=o2: e.tensor_tensor(
                                        h2w[:, o2 * 1024:(o2 + 1) * 1024], sqs[:],
                                        wbc_sb[:], ALU.mult),
                                    waits=(ev_wbcv.get(tag),))
                                if o2 == 7:
                                    ev_h2[("wbc_last",)] = ev_h2[(tag, 7)]
                            else:
                                ev_h2[(tag, o2)] = P.op(
                                    "vector", lambda e, o2=o2: e.tensor_tensor(
                                        h2w[:, o2 * 1024:(o2 + 1) * 1024], sqs[:],
                                        hbf[:, o2 * 1024:(o2 + 1) * 1024], ALU.mult),
                                    waits=(e_si,))
                        return evm
                    return streams, mms

                def ph_dn(tag, src_fn, half, first, bias_close, sd_half):
                    def streams(reg_base):
                        return [(reg_base + k3 * 1024, 1024, src_fn(k3, half))
                                for k3 in range(8)]
                    def mms(Rv, evs, sA0):
                        evm = None
                        for dd in range(half * 8, half * 8 + 8):
                            for k3 in range(8):
                                for nn in range(2):
                                    is_last = (k3 == 7 and nn == 1)
                                    stp = is_last and not bias_close
                                    evm = P.op(
                                        "tensor",
                                        lambda e, dd=dd, k3=k3, nn=nn, Rv=Rv,
                                        half=half, stp=stp: e.matmul(
                                            pdn[dd % 2][:, nn * 512:(nn + 1) * 512],
                                            Rv[:, k3 * 1024 + (dd - half * 8) * 128:
                                               k3 * 1024 + (dd - half * 8) * 128 + 128],
                                            h2w[:, k3 * 1024 + nn * 512:
                                                k3 * 1024 + (nn + 1) * 512],
                                            start=(k3 == 0), stop=stp),
                                        waits=(evs, ("sV", ev_h2[(tag, 7)][1]),
                                               ev_evp.get((tag, dd - 2))
                                               if dd - 2 >= half * 8 else
                                               ev_evp.get(("prev", 14 + dd)),
                                               ev_evp.get(("prev", dd)))
                                        if (k3 == 0 and nn == 0) else (),
                                        inc="sP" if (is_last and not bias_close)
                                        else None)
                            if bias_close:
                                for nn in range(2):
                                    evm = P.op("tensor", lambda e, dd=dd, nn=nn: e.matmul(
                                        pdn[dd % 2][:, nn * 512:(nn + 1) * 512],
                                        bdn_sb[:, dd * 128:(dd + 1) * 128],
                                        w_fm[0:6, nn * 512:(nn + 1) * 512],
                                        start=False, stop=(nn == 1)),
                                        waits=(e_wfv, ev_bdn) if nn == 0 else (),
                                        inc="sP" if nn == 1 else None)
                            if first:
                                ev_x1dma[dd] = P.dma(
                                    "sync", x1s[dd % 2][:], d_x1[dd, :, tsl],
                                    waits=(ev_evp.get((tag, dd - 2)),),
                                    sem="dUa" if dd % 2 == 0 else "dUb")
                                ev_evp[(tag, dd)] = P.op(
                                    "vector", lambda e, dd=dd: e.tensor_tensor(
                                        X1[:, dd * 1024:(dd + 1) * 1024],
                                        pdn[dd % 2][:, 0:1024], x1s[dd % 2][:],
                                        ALU.add),
                                    waits=(evm, ev_x1dma[dd]))
                            elif sd_half == 1:
                                ev_evp[(tag, dd)] = P.op(
                                    "vector", lambda e, dd=dd: e.scalar_tensor_tensor(
                                        X1[:, dd * 1024:(dd + 1) * 1024],
                                        pdn[dd % 2][:, 0:1024],
                                        g["bsd_pm"][:, dd:dd + 1],
                                        X1[:, dd * 1024:(dd + 1) * 1024],
                                        ALU.add, ALU.add),
                                    waits=(evm, ev_evp.get(("prev", dd))))
                            else:
                                ev_evp[(tag, dd)] = P.op(
                                    "vector", lambda e, dd=dd: e.tensor_tensor(
                                        X1[:, dd * 1024:(dd + 1) * 1024],
                                        X1[:, dd * 1024:(dd + 1) * 1024],
                                        pdn[dd % 2][:, 0:1024], ALU.add),
                                    waits=(evm, ev_evp.get(("prev", dd))))
                        return evm
                    return streams, mms

                def close_tag(tag):
                    for dd in range(16):
                        ev_evp[("prev", dd)] = ev_evp[(tag, dd)]
                    ev_evp[("prev15",)] = ev_evp[(tag, 15)]

                for n in range(NEXP):
                    up_src = (lambda n: lambda k, half: t["wup"][
                        n, k * 128:(k + 1) * 128, half * 512:(half + 1) * 512])(n)
                    sw_src = (lambda n: lambda k2: t["wsw"][
                        n, k2 * 128:(k2 + 1) * 128, :])(n)
                    dn_src = (lambda n: lambda k3, half: t["wdn"][
                        n, k3 * 128:(k3 + 1) * 128,
                        half * 1024:(half + 1) * 1024])(n)
                    upb = (lambda n: lambda o: g["bup_pm"][
                        :, n * 8 + o:n * 8 + o + 1])(n)
                    swb = (lambda n: lambda o2: g["bsw_pm"][
                        :, n * 8 + o2:n * 8 + o2 + 1])(n)
                    phases.append((ph_up(n, up_src, upb, 0), n))
                    phases.append((ph_up(n, up_src, upb, 1), n))
                    phases.append((ph_sw(n, sw_src, swb, True), n))
                    phases.append((ph_dn(n, dn_src, 0, n == 0, False, None), n))
                    phases.append((ph_dn(n, dn_src, 1, n == 0, False, None), n))
                for sh_ in range(2):
                    tag = ("s", sh_)
                    su_src = (lambda sh_: lambda k, half: t["wsu"][
                        k * 128:(k + 1) * 128,
                        sh_ * 1024 + half * 512:sh_ * 1024 + (half + 1) * 512])(sh_)
                    sa_src = lambda k2: t["wsa"][k2 * 128:(k2 + 1) * 128, :]
                    sd_src = (lambda sh_: lambda k3, half: t["wsd"][
                        (sh_ * 8 + k3) * 128:(sh_ * 8 + k3 + 1) * 128,
                        half * 1024:(half + 1) * 1024])(sh_)
                    sub = (lambda sh_: lambda o: g["bsu_pm"][
                        :, sh_ * 8 + o:sh_ * 8 + o + 1])(sh_)
                    sab = lambda o2: g["bsa_pm"][:, o2:o2 + 1]
                    phases.append((ph_up(tag, su_src, sub, 0), tag))
                    phases.append((ph_up(tag, su_src, sub, 1), tag))
                    phases.append((ph_sw(tag, sa_src, sab, False), tag))
                    phases.append((ph_dn(tag, sd_src, 0, False,
                                          sh_ == 1, sh_), tag))
                    phases.append((ph_dn(tag, sd_src, 1, False,
                                          sh_ == 1, sh_), tag))

                nph = len(phases)
                ev_stream = {}
                ev_mml = dict(carry_mml[0])

                def emit_stream(i):
                    (streams, _), _tg = phases[i]
                    reg = i % 2
                    evd = None
                    w0 = (ev_mml.get(i - 2),) + tdep
                    for j, (off, width, src) in enumerate(streams(reg * 8192)):
                        evd = P.dma("gpsimd", wsb[:, off:off + width], src,
                                    waits=w0 if j == 0 else (),
                                    sem="dWa" if reg == 0 else "dWb")
                    ev_stream[i] = evd

                emit_stream(0)
                emit_stream(1)
                tag_done = set()
                for i in range(nph):
                    (streams, mms), tg = phases[i]
                    sA0 = ("sA", P.count["sA"])
                    ev_mml[i] = mms(R[i % 2], ev_stream[i], sA0)
                    if i + 2 < nph:
                        emit_stream(i + 2)
                    if (i % 5) == 4:
                        close_tag(tg)
                carry_mml[0] = {-2: ev_mml[nph - 2], -1: ev_mml[nph - 1]}

                if rr == 0:
                    for dd in range(16):
                        P.dma("gpsimd", d_x2bf[dd, :, tsl],
                              X1[:, dd * 1024:(dd + 1) * 1024],
                              waits=(ev_evp[(("s", 1), dd)],),
                              sem="dZOa" if dd % 2 == 0 else "dZOb")
                if rr == 1:
                    if debug:
                        for dd in range(16):
                            P.dma("sync", d_x3[dd, :, tsl],
                                  X1[:, dd * 1024:(dd + 1) * 1024],
                                  waits=(ev_evp[(("s", 1), dd)],))
                    ev_fx = {}
                    ev_fr = {}
                    ev_fst = None
                    for dd in range(16):
                        ev_fx[dd] = P.dma("sync", x1s[dd % 2][:],
                                          d_x1[dd, :, tsl],
                                          waits=(ev_fr.get(dd - 2),
                                                 ev_evp[(0, 14 + dd)] if dd < 2
                                                 else None),
                                          sem="dUa" if dd % 2 == 0 else "dUb")
                        ev_fr[dd] = P.op("vector", lambda e, dd=dd: e.tensor_tensor(
                            X1[:, dd * 1024:(dd + 1) * 1024],
                            X1[:, dd * 1024:(dd + 1) * 1024], x1s[dd % 2][:],
                            ALU.add), waits=(ev_fx[dd], ev_evp[(("s", 1), dd)],
                                             ("dS", P.count["dS"]) if debug else None))
                        for nn in range(2):
                            ev_fst = P.op(
                                "tensor",
                                lambda e, dd=dd, nn=nn: e.matmul(
                                    pu[0][0:1, nn * 512:(nn + 1) * 512],
                                    g["ones_col_f32"][:],
                                    X1[:, dd * 1024 + nn * 512:
                                       dd * 1024 + (nn + 1) * 512],
                                    start=(dd == 0), stop=(dd == 15)),
                                waits=(ev_fr[dd], sA_now()) if nn == 0 else (),
                                inc="sP" if (dd == 15 and nn == 1) else None)
                    e_mu = P.op("vector", lambda e: e.tensor_scalar(
                        gfm[0:1, 0:1024], pu[0][0:1, 0:1024], 1.0 / 2048.0, None,
                        ALU.mult), waits=(ev_fst,))
                    e_mb = P.op("tensor", lambda e: e.matmul(
                        pu[0][:, 0:512], g["ones_row_f32"][:], gfm[0:1, 0:512],
                        start=True, stop=True), waits=(e_mu,), inc="sP")
                    e_mb2 = P.op("tensor", lambda e: e.matmul(
                        pu[0][:, 512:1024], g["ones_row_f32"][:],
                        gfm[0:1, 512:1024], start=True, stop=True), inc="sP")
                    e_mbv = P.op("vector", lambda e: e.tensor_copy(
                        wbc_sb[:], pu[0][:, 0:1024]), waits=(e_mb2,))
                    ev_vst = None
                    for dd in range(16):
                        exc = P.op("vector", lambda e, dd=dd: e.tensor_tensor(
                            X1[:, dd * 1024:(dd + 1) * 1024],
                            X1[:, dd * 1024:(dd + 1) * 1024], wbc_sb[:],
                            ALU.subtract), waits=(e_mbv,))
                        esq = P.op("scalar", lambda e, dd=dd: e.activation(
                            sqs[:], X1[:, dd * 1024:(dd + 1) * 1024], AF.Square),
                            waits=(exc, ev_vst))
                        for nn in range(2):
                            ev_vst = P.op(
                                "tensor",
                                lambda e, dd=dd, nn=nn: e.matmul(
                                    pu[1][0:1, nn * 512:(nn + 1) * 512],
                                    g["ones_col_f32"][:],
                                    sqs[:, nn * 512:(nn + 1) * 512],
                                    start=(dd == 0), stop=(dd == 15)),
                                waits=(esq,) if nn == 0 else (),
                                inc="sP" if nn == 1 else None)
                    e_v2 = P.op("vector", lambda e: e.tensor_scalar(
                        gfm[0:1, 0:1024], pu[1][0:1, 0:1024], 1.0 / 2048.0, None,
                        ALU.mult), waits=(ev_vst,))
                    e_ln3 = P.op("scalar", lambda e: e.activation(
                        gfm[0:1, 0:1024], gfm[0:1, 0:1024], AF.Ln,
                        bias=g["eps11"][0:1, 0:1]), waits=(e_v2,))
                    e_rs3 = P.op("scalar", lambda e: e.activation(
                        gfm[0:1, 0:1024], gfm[0:1, 0:1024], AF.Exp, scale=-0.5))
                    e_rb = P.op("tensor", lambda e: e.matmul(
                        pu[1][:, 0:512], g["ones_row_f32"][:], gfm[0:1, 0:512],
                        start=True, stop=True), waits=(e_rs3,), inc="sP")
                    e_rb2 = P.op("tensor", lambda e: e.matmul(
                        pu[1][:, 512:1024], g["ones_row_f32"][:],
                        gfm[0:1, 512:1024], start=True, stop=True), inc="sP")
                    e_rbv = P.op("vector", lambda e: e.tensor_copy(
                        wbc_sb[:], pu[1][:, 0:1024]), waits=(e_rb2,))
                    ev_odma = {}
                    for dd in range(16):
                        eo = P.op("vector", lambda e, dd=dd: e.tensor_tensor(
                            x1s[dd % 2][:], X1[:, dd * 1024:(dd + 1) * 1024],
                            wbc_sb[:], ALU.mult),
                            waits=(e_rbv, ev_odma.get(dd - 2)))
                        ev_odma[dd] = P.dma("sync", d_out[dd, :, tsl],
                                            x1s[dd % 2][:], waits=(eo,),
                                            sem="dOUT")
        P.final_wait()
        P.flush("p4")

    nc._prog = P
    return nc



def _host_arrays(inp):
    a = {}
    a["w_inT"] = np.ascontiguousarray(inp["L0_attn_W_in"].T)
    a["w_outT"] = np.ascontiguousarray(inp["L0_attn_W_out"].T)
    a["conv_w_pm"] = np.ascontiguousarray(
        inp["L0_attn_conv_w"].reshape(34, 128, 4).transpose(1, 0, 2).reshape(128, 136))
    a["conv_b_pm"] = np.ascontiguousarray(inp["L0_attn_conv_b"].reshape(34, 128).T)
    a["dtb_col"] = np.ascontiguousarray(inp["L0_attn_dt_bias"].reshape(NH, 1))
    a["alog_col"] = np.ascontiguousarray(inp["L0_attn_A_log"].reshape(NH, 1))
    a["dp_pm"] = np.ascontiguousarray(
        np.repeat(inp["L0_attn_Dp"], HP).reshape(32, 128).T)
    a["nw_pm"] = np.ascontiguousarray(inp["L0_attn_norm_w"].reshape(32, 128).T)
    a["wg"] = np.ascontiguousarray(inp["L0_ffn_Wg"])
    a["bg_col"] = np.ascontiguousarray(inp["L0_ffn_bg"].reshape(2, 8, 1))
    a["wup"] = np.ascontiguousarray(inp["L0_ffn_Wup"])
    a["bup_pm"] = np.ascontiguousarray(
        inp["L0_ffn_bup"].reshape(NEXP, 8, 128).transpose(0, 2, 1))
    a["wsw"] = np.ascontiguousarray(inp["L0_ffn_Wsw"])
    a["bsw_pm"] = np.ascontiguousarray(
        inp["L0_ffn_bsw"].reshape(NEXP, 8, 128).transpose(0, 2, 1))
    a["wdn"] = np.ascontiguousarray(inp["L0_ffn_Wdn"])
    a["bdn"] = np.ascontiguousarray(inp["L0_ffn_bdn"])
    a["wsu"] = np.ascontiguousarray(inp["L0_ffn_Wsu"])
    a["bsu_pm"] = np.ascontiguousarray(inp["L0_ffn_bsu"].reshape(16, 128).T)
    a["wsa"] = np.ascontiguousarray(inp["L0_ffn_Wsa"])
    a["bsa_pm"] = np.ascontiguousarray(inp["L0_ffn_bsa"].reshape(8, 128).T)
    a["wsd"] = np.ascontiguousarray(inp["L0_ffn_Wsd"])
    a["bsd_pm"] = np.ascontiguousarray(inp["L0_ffn_bsd"].reshape(16, 128).T)
    j = np.arange(128)[:, None]
    i = np.arange(256)[None, :]
    a["m1a"] = (i >= j).astype(np.float32)
    a["m1b"] = (i >= j + 128).astype(np.float32)
    a["u01_bf"] = np.concatenate([(i < j), (i < j + 128)], axis=1).astype(BF)
    a["negdiag_bf"] = (-BIGM * np.eye(128)).astype(BF)
    sel = np.zeros((128, NH, 128), np.float32)
    for h in range(NH):
        sel[h, h, :] = 1.0
        sel[64 + h, h, :] = 1.0
    a["sel_bf"] = sel.reshape(128, NH * 128).astype(BF)
    sel8 = np.zeros((8, 8, 128), np.float32)
    for n in range(8):
        sel8[n, n, :] = 1.0
    a["sel8"] = sel8.reshape(8, 8 * 128)
    a["id_bf"] = np.eye(128).astype(BF)
    a["id_f32"] = np.eye(128, dtype=np.float32)
    a["ones_col_bf"] = np.ones((128, 1), BF)
    a["i4096_col_bf"] = np.full((128, 1), 1.0 / 4096.0, BF)
    a["i2048_col_bf"] = np.full((128, 1), 1.0 / 2048.0, BF)
    a["eps11"] = np.full((1, 1), 1e-5, np.float32)
    a["bg2"] = np.ascontiguousarray(inp["L0_ffn_bg"].T)
    a["ones_col_f32"] = np.ones((128, 1), np.float32)
    a["ones_row_f32"] = np.ones((1, 128), np.float32)
    for k in a:
        if a[k].dtype == np.float64:
            a[k] = a[k].astype(np.float32)
    return a


def kernel(**inputs):
    shared = _host_arrays(inputs)
    u = np.asarray(inputs["L0_attn_u"], np.float32)
    nc = build(debug=False)
    in_maps = []
    for b in range(8):
        m = dict(shared)
        m["u_fm"] = np.ascontiguousarray(u[b].T)
        in_maps.append(m)
    res = run_bass_kernel_spmd(nc, in_maps, core_ids=list(range(8)))
    out = np.empty((8, 2048, 2048), np.float32)
    for b, r in enumerate(res.results):
        out[b] = np.asarray(r["out"]).reshape(2048, 2048).T
    return out


# version 06
# speedup vs baseline: 1.0283x; 1.0283x over previous
import numpy as np
import ml_dtypes
from contextlib import ExitStack
import concourse.bass as bass
import concourse.mybir as mybir
from concourse.bass_utils import run_bass_kernel_spmd

F32 = mybir.dt.float32
BF16 = mybir.dt.bfloat16
AF = mybir.ActivationFunctionType
ALU = mybir.AluOpType
AX = mybir.AxisListType
BF = ml_dtypes.bfloat16

D = 2048
L = 2048
DIN = 4096
NH = 64
HP = 64
DS = 128
DCONV = 4
CHUNK = 256
NCH = L // CHUNK
CONV_DIM = DIN + 2 * DS
DPROJ = 2 * DIN + 2 * DS + NH
NOC = DPROJ // 128
EPS = 1e-5
BIGM = 32768.0
NEXP = 6
EE = 1024
TFF = 1024
TEST_NO_YOFF = False


class Prog:
    ENG_SELF = {"vector": "sV", "scalar": "sA", "gpsimd": "sG"}
    ENGS = ("tensor", "vector", "scalar", "gpsimd", "sync")

    def __init__(self, nc):
        self.nc = nc
        self.sem = {}
        self.count = {}
        self.waited = {e: {} for e in self.ENGS}
        for s in ("sV", "sA", "sG", "sP", "dS", "dG", "dC", "dCg", "dU",
                  "dUa", "dUb", "dWa", "dWb", "dXa", "dXb", "dZOa", "dZOb",
                  "dGNa", "dGNb", "dOUT"):
            self.sem[s] = nc.alloc_semaphore(s)
            self.count[s] = 0
        self.q = None

    def begin(self):
        self.q = {e: [] for e in self.ENGS}

    def flush(self, name=None):
        with self.nc.Block(name) as blk:
            for e in self.ENGS:
                fns = self.q[e]
                if not fns:
                    continue
                def body(eng, fns=fns):
                    for f in fns:
                        f(eng)
                getattr(blk, e)(body)
        self.q = None

    def op(self, eng, fn, waits=(), inc="SELF", inc_n=1):
        if inc == "SELF":
            inc = self.ENG_SELF.get(eng)
        wl = []
        se = self.ENG_SELF.get(eng)
        if se is not None:
            wl.append((se, self.count[se]))
        for wv in waits:
            if wv is not None:
                wl.append(wv)
        W = self.waited[eng]
        wl2 = []
        for s, v in wl:
            if v > 0 and W.get(s, 0) < v:
                wl2.append((s, v))
                W[s] = v
        ev = None
        if inc:
            self.count[inc] += inc_n
            ev = (inc, self.count[inc])
        sems = self.sem
        def closure(e, wl2=wl2, fn=fn, inc=inc, inc_n=inc_n):
            for s, v in wl2:
                e.wait_ge(sems[s], v)
            inst = fn(e)
            if inc:
                inst.then_inc(sems[inc], inc_n)
        self.q[eng].append(closure)
        return ev

    DMA_SEMS = ("dS", "dG", "dC", "dCg", "dU", "dUa", "dUb", "dWa", "dWb",
                "dXa", "dXb", "dZOa", "dZOb", "dGNa", "dGNb", "dOUT")

    def dma(self, queue, out, in_, waits=(), sem=None):
        semn = sem or ("dS" if queue == "sync" else "dG")
        return self.op(queue, lambda e, o=out, i=in_: e.dma_start(o, i),
                       waits=waits, inc=semn, inc_n=16)

    def dma_snapshot(self):
        return tuple((s, self.count[s]) for s in self.DMA_SEMS
                     if self.count[s] > 0)

    def final_wait(self):
        snap = self.dma_snapshot()
        def closure(e):
            for s, v in snap:
                e.wait_ge(self.sem[s], v)
        self.q["sync"].append(closure)


def build(debug=False):
    nc = bass.Bass("TRN2", target_bir_lowering=False)
    P = Prog(nc)
    t = {}

    def din(name, shape, dt=F32):
        t[name] = nc.dram_tensor(name, list(shape), dt, kind="ExternalInput").ap()

    din("u_fm", [D, L])
    din("w_inT", [D, DPROJ])
    din("w_outT", [DIN, D])
    din("conv_w_pm", [128, 34 * 4])
    din("conv_b_pm", [128, 34])
    din("dtb_col", [NH, 1])
    din("alog_col", [NH, 1])
    din("dp_pm", [128, 32])
    din("nw_pm", [128, 32])
    din("wg", [2, D, 8])
    din("bg_col", [2, 8, 1])
    din("wup", [NEXP, D, EE])
    din("bup_pm", [NEXP, 128, 8])
    din("wsw", [NEXP, EE, EE])
    din("bsw_pm", [NEXP, 128, 8])
    din("wdn", [NEXP, EE, D])
    din("bdn", [NEXP, D])
    din("wsu", [D, 2 * EE])
    din("bsu_pm", [128, 16])
    din("wsa", [EE, EE])
    din("bsa_pm", [128, 8])
    din("wsd", [2 * EE, D])
    din("bsd_pm", [128, 16])
    din("m1a", [128, 256])
    din("m1b", [128, 256])
    din("u01_bf", [128, 512], BF16)
    din("negdiag_bf", [128, 128], BF16)
    din("sel_bf", [128, NH * 128], BF16)
    din("sel8", [8, 8 * 128])
    din("id_bf", [128, 128], BF16)
    din("id_f32", [128, 128])
    din("ones_col_bf", [128, 1], BF16)
    din("i4096_col_bf", [128, 1], BF16)
    din("i2048_col_bf", [128, 1], BF16)
    din("eps11", [1, 1])
    din("bg2", [8, 2])
    din("ones_col_f32", [128, 1])
    din("ones_row_f32", [1, 128])

    skind = "ExternalOutput" if debug else "Internal"
    d_zbf = nc.dram_tensor("d_zbf", [32, 128, L], BF16, kind=skind).ap()
    d_xbc = nc.dram_tensor("d_xbc", [34, 128, L], BF16, kind=skind).ap()
    d_gn = nc.dram_tensor("d_gn", [32, 128, L], BF16, kind=skind).ap()
    d_x1 = nc.dram_tensor("d_x1", [16, 128, L], F32, kind=skind).ap()
    d_x1bf = nc.dram_tensor("d_x1bf", [16, 128, L], BF16, kind=skind).ap()
    d_x2bf = nc.dram_tensor("d_x2bf", [16, 128, L], BF16, kind=skind).ap()
    d_out = nc.dram_tensor("out", [16, 128, L], F32, kind="ExternalOutput").ap()
    if debug:
        d_dt = nc.dram_tensor("d_dt", [NH, L], F32, kind="ExternalOutput").ap()
        d_y = nc.dram_tensor("d_y", [NCH, 128, 32 * 256], F32,
                             kind="ExternalOutput").ap()
        d_dbgE = nc.dram_tensor("d_dbgE", [6, 128, 384], BF16,
                                kind="ExternalOutput").ap()
        d_dbgS = nc.dram_tensor("d_dbgS", [6, 128, 384], BF16,
                                kind="ExternalOutput").ap()
        d_dbgX = nc.dram_tensor("d_dbgX", [3, 128, 256], BF16,
                                kind="ExternalOutput").ap()
        d_dbgP = nc.dram_tensor("d_dbgP", [2, 128, 256], F32,
                                kind="ExternalOutput").ap()
        d_x3 = nc.dram_tensor("d_x3", [16, 128, L], F32,
                              kind="ExternalOutput").ap()
        d_dbgG = nc.dram_tensor("d_dbgG", [2, 8, 1024], F32,
                                kind="ExternalOutput").ap()
        d_dbgWf = nc.dram_tensor("d_dbgWf", [2, 8, 1024], F32,
                                 kind="ExternalOutput").ap()
        d_dbgXt = nc.dram_tensor("d_dbgXt", [16, 128, 1024], BF16,
                                 kind="ExternalOutput").ap()
        d_ms = nc.dram_tensor("d_ms", [NCH, 1, 256], F32,
                              kind="ExternalOutput").ap()

    g = {}
    def gal(name, shape, dt=F32):
        g[name] = nc.alloc_sbuf_tensor("g_" + name, list(shape), dt)
        return g[name]
    for name, shape, dt in [
        ("conv_w_pm", [128, 136], F32), ("conv_b_pm", [128, 34], F32),
        ("dtb_col", [NH, 1], F32), ("alog_col", [NH, 1], F32),
        ("dp_pm", [128, 32], F32), ("nw_pm", [128, 32], F32),
        ("m1a", [128, 256], F32), ("m1b", [128, 256], F32),
        ("u01_bf", [128, 512], BF16),
        ("negdiag_bf", [128, 128], BF16),
        ("id_bf", [128, 128], BF16), ("id_f32", [128, 128], F32),
        ("ones_col_bf", [128, 1], BF16), ("ones_col_f32", [128, 1], F32),
        ("i4096_col_bf", [128, 1], BF16), ("eps11", [1, 1], F32),
        ("i2048_col_bf", [128, 1], BF16), ("bg2", [8, 2], F32),
        ("ones_row_f32", [1, 128], F32),
        ("sel8", [8, 8 * 128], F32),
        ("bg_col", [2 * 8, 1], F32),
        ("bsu_pm", [128, 16], F32), ("bsa_pm", [128, 8], F32),
        ("bsd_pm", [128, 16], F32),
        ("bup_pm", [128, NEXP * 8], F32), ("bsw_pm", [128, NEXP * 8], F32),
        ("wg_bf", [128, 2 * 16 * 8], BF16),
        ("acol", [NH, 1], F32),
    ]:
        gal(name, shape, dt)

    P.begin()
    for name in ("conv_w_pm", "conv_b_pm", "dtb_col", "alog_col", "dp_pm",
                 "nw_pm", "m1a", "m1b", "u01_bf", "negdiag_bf",
                 "id_bf", "id_f32", "ones_col_bf", "ones_col_f32",
                 "i4096_col_bf", "eps11", "i2048_col_bf", "bg2",
                 "ones_row_f32", "sel8", "bsu_pm", "bsa_pm", "bsd_pm"):
        P.dma("sync", g[name][:], t[name], sem="dC")
    P.dma("sync", g["bg_col"][:], t["bg_col"].rearrange("r n one -> (r n) one"),
          sem="dC")
    for n in range(NEXP):
        P.dma("sync", g["bup_pm"][:, n * 8:(n + 1) * 8], t["bup_pm"][n], sem="dC")
        P.dma("sync", g["bsw_pm"][:, n * 8:(n + 1) * 8], t["bsw_pm"][n], sem="dC")
    for r in range(2):
        for k in range(16):
            P.dma("gpsimd", g["wg_bf"][:, (r * 16 + k) * 8:(r * 16 + k + 1) * 8],
                  t["wg"][r, k * 128:(k + 1) * 128, :], sem="dCg")
    P.flush("p0")
    ev_const = ("dC", P.count["dC"])
    ev_constg = ("dCg", P.count["dCg"])

    _mix_stack = ExitStack()
    g["dt_sb"] = _mix_stack.enter_context(
        nc.sbuf_tensor("g_dt_sb", [NH, L], F32))

    with ExitStack() as st:
        def al(name, shape, dt=F32):
            return st.enter_context(nc.sbuf_tensor(name, list(shape), dt))
        ubuf = al("ubuf", [128, 16 * L], BF16)
        wbuf = [al(f"wbuf{i}", [128, 16 * 512], BF16) for i in range(2)]
        ebuf = [al(f"ebuf{i}", [128, 2051], F32) for i in range(2)]
        xsb = [al(f"xsb{i}", [128, L], F32) for i in range(2)]
        stg = [al(f"stg{i}", [128, L], BF16) for i in range(2)]
        sp1 = al("sp1", [NH, L], F32)
        sp2 = al("sp2", [NH, L], F32)
        pA = st.enter_context(nc.psum_tensor("pA", [128, L], F32))
        pB = st.enter_context(nc.psum_tensor("pB", [128, L], F32))

        P.begin()
        ev_u = None
        for k in range(16):
            ev_u = P.dma("gpsimd", ubuf[:, k * L:(k + 1) * L],
                         t["u_fm"][k * 128:(k + 1) * 128, :], waits=(ev_constg,),
                         sem="dU")
        ev_pad = [P.op("gpsimd", lambda e, i=i: e.memset(ebuf[i][:, 0:3], 0.0))
                  for i in range(2)]

        ogs = []
        o = 0
        while o < 66:
            ogs.append(list(range(o, min(o + 4, 66))))
            o += 4
        ogs.append([66])

        ev_wdma = {}
        ev_pe_og = {}
        ev_evac = {}
        ev_outd = {}
        ev_conv = {}
        ev_silu = {}

        for gi, og in enumerate(ogs):
            c0 = og[0] * 128
            c1 = min(og[-1] * 128 + 128, DPROJ)
            wcols = c1 - c0
            wb = wbuf[gi % 2]
            wdep = [ev_constg]
            if gi >= 2:
                wdep.append(ev_pe_og.get(gi - 2))
            ev = None
            for k in range(16):
                ev = P.dma("gpsimd", wb[:, k * 512:k * 512 + wcols],
                           t["w_inT"][k * 128:(k + 1) * 128, c0:c1],
                           waits=wdep, sem="dWa" if gi % 2 == 0 else "dWb")
            ev_wdma[gi] = ev

            for o in og:
                rows = 64 if o == 66 else 128
                ps = pA if (o % 2 == 0) else pB
                off = o * 128 - c0
                mmdep = [ev_wdma[gi], ev_u]
                if o >= 2:
                    mmdep.append(ev_evac.get(o - 2))
                evp = None
                for k in range(16):
                    for tt_ in range(4):
                        evp = P.op(
                            "tensor",
                            lambda e, ps=ps, rows=rows, k=k, tt_=tt_, wb=wb, off=off: e.matmul(
                                ps[0:rows, tt_ * 512:(tt_ + 1) * 512],
                                wb[:, k * 512 + off:k * 512 + off + rows],
                                ubuf[:, k * L + tt_ * 512:k * L + (tt_ + 1) * 512],
                                start=(k == 0), stop=(k == 15)),
                            waits=mmdep if (k == 0 and tt_ == 0) else (),
                            inc="sP" if (k == 15 and tt_ == 3) else None)
                ev_pe_og[gi] = evp

                par = o % 2
                if o < 32:
                    eva = P.op("scalar",
                               lambda e, ps=ps, par=par: e.activation(
                                   stg[par][:], ps[:], AF.Silu),
                               waits=(evp, ev_outd.get(par)))
                    ev_evac[o] = eva
                    ev_outd[par] = P.dma("sync", d_zbf[o], stg[par][:],
                                         waits=(eva,),
                                         sem="dZOa" if par == 0 else "dZOb")
                elif o < 66:
                    cx = o - 32
                    eva = P.op("scalar",
                               lambda e, ps=ps, par=par: e.activation(
                                   ebuf[par][:, 3:2051], ps[:], AF.Copy),
                               waits=(evp, ev_evac.get(o - 2), ev_pad[par],
                                      ev_conv.get(o - 2)))
                    ev_evac[o] = eva
                    e0 = P.op("vector",
                              lambda e, par=par, cx=cx: e.tensor_scalar(
                                  xsb[par][:], ebuf[par][:, 0:2048],
                                  g["conv_w_pm"][:, cx * 4:cx * 4 + 1], None,
                                  ALU.mult),
                              waits=(eva, ev_silu.get(o - 2)))
                    for kk in range(1, 4):
                        e0 = P.op("vector",
                                  lambda e, par=par, cx=cx, kk=kk: e.scalar_tensor_tensor(
                                      xsb[par][:], ebuf[par][:, kk:kk + 2048],
                                      g["conv_w_pm"][:, cx * 4 + kk:cx * 4 + kk + 1],
                                      xsb[par][:], ALU.mult, ALU.add))
                    ev_conv[o] = e0
                    eva2 = P.op("scalar",
                                lambda e, par=par, cx=cx: e.activation(
                                    stg[par][:], xsb[par][:], AF.Silu,
                                    bias=g["conv_b_pm"][:, cx:cx + 1]),
                                waits=(e0, ev_outd.get(par)))
                    ev_silu[o] = eva2
                    ev_outd[par] = P.dma("sync", d_xbc[cx], stg[par][:],
                                         waits=(eva2,),
                                         sem="dZOa" if par == 0 else "dZOb")
                else:
                    e1 = P.op("scalar",
                              lambda e, ps=ps: e.activation(
                                  sp1[:], ps[0:64, :], AF.Abs,
                                  bias=g["dtb_col"][:, 0:1]),
                              waits=(evp,))
                    e2 = P.op("scalar",
                              lambda e: e.activation(sp2[:], sp1[:], AF.Exp,
                                                     scale=-1.0))
                    e3 = P.op("scalar",
                              lambda e: e.activation(sp1[:], sp2[:], AF.Ln,
                                                     bias=1.0))
                    e4 = P.op("scalar",
                              lambda e, ps=ps: e.activation(
                                  sp2[:], ps[0:64, :], AF.Relu,
                                  bias=g["dtb_col"][:, 0:1]))
                    ev_evac[o] = e4
                    e5 = P.op("vector",
                              lambda e: e.tensor_tensor(g["dt_sb"][:], sp1[:],
                                                        sp2[:], ALU.add),
                              waits=(e3, e4))
                    e6 = P.op("scalar",
                              lambda e: e.activation(sp1[:, 0:1],
                                                     g["alog_col"][:, 0:1],
                                                     AF.Exp),
                              waits=(ev_const,))
                    e7 = P.op("vector",
                              lambda e: e.tensor_scalar(g["acol"][:, 0:1],
                                                        sp1[:, 0:1], -1.0,
                                                        None, ALU.mult),
                              waits=(e6,))
                    if debug:
                        P.dma("sync", d_dt, g["dt_sb"][:], waits=(e5,))
        P.final_wait() if debug else None
        P.flush("p1")
    with ExitStack() as st:
        def al(name, shape, dt=F32):
            return st.enter_context(nc.sbuf_tensor(name, list(shape), dt))
        xbc_sb = [al(f"xbc{i}", [128, 34 * 256], BF16) for i in range(2)]
        z_sb = [al(f"zsb{i}", [128, 32 * 256], BF16) for i in range(2)]
        y_sb = al("y_sb", [128, 32 * 256], F32)
        gn_sb = al("gn_sb", [128, 32 * 256], BF16)
        sq_sb = al("sq_sb", [128, 32 * 256], BF16)
        h_f = al("h_f", [128, NH * HP], F32)
        h_b = al("h_b", [128, NH * HP], BF16)
        sel_sb = al("sel_sb", [128, NH * 128], BF16)
        dA_f = al("dA_f", [NH, 256], F32)
        dA_t = al("dA_t", [128, 128], F32)
        cs_fm = al("cs_fm", [NH, 256], F32)
        cs_t = al("cs_t", [128, 128], F32)

        hilo = al("hilo", [128, 256], BF16)
        b2 = al("b2", [128, 128], F32)
        ldt = al("ldt", [128, 128], F32)
        ecs_bf = al("ecs_bf", [NH, 256], BF16)
        wt1 = al("wt1", [NH, 256], F32)
        wt2 = al("wt2", [NH, 256], F32)
        w_t = al("w_t", [128, 128], F32)
        dt_t = al("dt_t", [128, 128], F32)
        cd_col = al("cd_col", [NH, 1], F32)
        cd_row = al("cd_row", [1, NH], F32)
        cd_all = al("cd_all", [128, NH], F32)

        cbt_bf = al("cbt_bf", [128, 384], BF16)
        bt_bf = al("bt_bf", [128, 256], BF16)
        xt_bf = [al(f"xt_bf{i}", [128, 256], BF16) for i in range(2)]
        E_bf = [al(f"E_bf{i}", [128, 384], BF16) for i in range(2)]
        ce_bf = [al(f"ce_bf{i}", [128, 256], BF16) for i in range(2)]
        sc_bf = [al(f"sc_bf{i}", [128, 384], BF16) for i in range(2)]
        xw_bf = [al(f"xw_bf{i}", [128, 128], BF16) for i in range(2)]
        ms_row = al("ms_row", [1, 256], F32)
        zero_bf = al("zero_bf", [128, 256], BF16)
        dbgP_sb = al("dbgP_sb", [128, 256], F32)
        rstd_row = al("rstd_row", [1, 256], F32)

        pseg = [st.enter_context(nc.psum_tensor(f"pseg{i}", [128, 512], F32))
                for i in range(2)]
        pyp_ = [st.enter_context(nc.psum_tensor(f"pyp{i}", [128, 512], F32))
                for i in range(2)]
        ptr0 = st.enter_context(nc.psum_tensor("ptr0", [128, 1024], BF16))
        ptr = [ptr0, ptr0]
        pcum = st.enter_context(nc.psum_tensor("pcum", [128, 512], F32))
        pcbt = st.enter_context(nc.psum_tensor("pcbt", [128, 512], F32))
        pce = st.enter_context(nc.psum_tensor("pce", [128, 512], F32))

        P.begin()
        P.op("vector", lambda e: e.memset(h_f[:], 0.0))
        P.op("vector", lambda e: e.memset(zero_bf[:], 0.0))
        ev_hb0 = P.op("gpsimd", lambda e: e.memset(h_b[:], 0.0))
        for hh in range(NH):
            P.dma("sync", sel_sb[:, hh * 128:(hh + 1) * 128],
                  t["sel_bf"][:, hh * 128:(hh + 1) * 128], sem="dC")

        M = {}
        ev_gn_prev = [None]
        snap = lambda: {s_: P.count[s_] for s_ in ("sP", "sV", "sA", "sG", "dS", "dG")}
        M[-1] = M[-2] = {s_: 0 for s_ in ("sP", "sV", "sA", "sG", "dS", "dG")}
        ev_gn_dma = {}

        def emit_chunk(c):
            par_c = c % 2
            xb = xbc_sb[par_c]
            zb = z_sb[par_c]
            csl = slice(c * 256, (c + 1) * 256)
            ddep = (("sP", M[c - 1]["sP"]), ("sV", M[c - 1]["sV"]),
                    ("sA", M[c - 1]["sA"]))
            if c == 0:
                ddep = ddep + (("dZOa", P.count["dZOa"]),
                               ("dZOb", P.count["dZOb"]))
            dxs = "dXa" if par_c == 0 else "dXb"
            ev_x = P.dma("sync", xb[:].rearrange("p (t j) -> p t j", j=256),
                         d_xbc[:, :, csl].rearrange("t p j -> p t j"), waits=ddep,
                         sem=dxs)
            ev_z = P.dma("sync", zb[:].rearrange("p (t j) -> p t j", j=256),
                         d_zbf[:, :, csl].rearrange("t p j -> p t j"), waits=ddep,
                         sem=dxs)

            adep = (("sP", M[c - 1]["sP"]), ("sV", M[c - 1]["sV"]),
                    ("sA", M[c - 1]["sA"]), ("sG", M[c - 1]["sG"]))
            e_dA = P.op("vector", lambda e: e.tensor_scalar(
                dA_f[:], g["dt_sb"][:, csl], g["acol"][:, 0:1], None, ALU.mult),
                waits=adep)
            e_trA = P.op("tensor", lambda e: e.transpose(
                pcum[0:128, 0:64], dA_f[:, 0:128], g["id_f32"][0:64, 0:64]),
                waits=adep + (e_dA,), inc="sP")
            e_trA2 = P.op("tensor", lambda e: e.transpose(
                pcum[0:128, 64:128], dA_f[:, 128:256], g["id_f32"][0:64, 0:64]),
                inc="sP")
            e_vA = P.op("vector", lambda e: e.tensor_copy(dA_t[:], pcum[:, 0:128]),
                        waits=(e_trA2,))
            e_trD = P.op("tensor", lambda e: e.transpose(
                pcum[0:128, 128:192], g["dt_sb"][:, csl.start:csl.start + 128],
                g["id_f32"][0:64, 0:64]), waits=(e_vA,), inc="sP")
            e_trD2 = P.op("tensor", lambda e: e.transpose(
                pcum[0:128, 192:256], g["dt_sb"][:, csl.start + 128:csl.stop],
                g["id_f32"][0:64, 0:64]), inc="sP")
            e_vD = P.op("vector", lambda e: e.tensor_copy(dt_t[:], pcum[:, 128:256]),
                        waits=(e_trD2,))
            e_cf1 = P.op("tensor", lambda e: e.matmul(
                pcum[0:64, 256:512], dA_t[:, 0:64], g["m1a"][:], start=True,
                stop=False), waits=(e_vD,), inc="sP")
            e_cf2 = P.op("tensor", lambda e: e.matmul(
                pcum[0:64, 256:512], dA_t[:, 64:128], g["m1b"][:], start=False,
                stop=True), inc="sP")
            e_csfm = P.op("vector", lambda e: e.tensor_copy(
                cs_fm[:], pcum[0:64, 256:512]), waits=(e_cf2,))
            e_ct1 = P.op("tensor", lambda e: e.matmul(
                pcum[0:128, 0:64], g["m1a"][:, 0:128], dA_t[:, 0:64],
                start=True, stop=True), waits=(e_csfm,), inc="sP")
            e_ct2 = P.op("tensor", lambda e: e.matmul(
                pcum[0:128, 64:128], g["m1a"][:, 128:256], dA_t[:, 0:64],
                start=True, stop=False), inc="sP")
            e_ct3 = P.op("tensor", lambda e: e.matmul(
                pcum[0:128, 64:128], g["m1b"][:, 128:256], dA_t[:, 64:128],
                start=False, stop=True), inc="sP")
            e_cst = P.op("vector", lambda e: e.tensor_copy(cs_t[:], pcum[:, 0:128]),
                         waits=(e_ct3,))
            e_ldt = P.op("scalar", lambda e: e.activation(
                ldt[:], dt_t[:], AF.Ln), waits=(e_vD,))
            e_b2 = P.op("vector", lambda e: e.tensor_tensor(
                b2[:], ldt[:], cs_t[:], ALU.subtract), waits=(e_ldt, e_cst))
            e_hi = P.op("scalar", lambda e: e.activation(
                hilo[0:64, :], cs_fm[:], AF.Copy), waits=(e_csfm,))
            e_lo = P.op("vector", lambda e: e.tensor_tensor(
                hilo[64:128, :], cs_fm[:], hilo[0:64, :], ALU.subtract),
                waits=(e_hi,))
            e_ecs = P.op("scalar", lambda e: e.activation(
                ecs_bf[:], cs_fm[:], AF.Exp))
            e_w1 = P.op("vector", lambda e: e.tensor_scalar(
                wt1[:], cs_fm[:], cs_fm[:, 255:256], None, ALU.subtract))
            e_w2 = P.op("scalar", lambda e: e.activation(
                wt2[:], wt1[:], AF.Exp, scale=-1.0), waits=(e_w1,))
            e_w3 = P.op("vector", lambda e: e.tensor_tensor(
                wt1[:], wt2[:], g["dt_sb"][:, csl], ALU.mult), waits=(e_w2,))
            e_tw1 = P.op("tensor", lambda e: e.transpose(
                pcum[0:128, 128:192], wt1[:, 0:128], g["id_f32"][0:64, 0:64]),
                waits=(e_w3, e_cst), inc="sP")
            e_tw2 = P.op("tensor", lambda e: e.transpose(
                pcum[0:128, 192:256], wt1[:, 128:256], g["id_f32"][0:64, 0:64]),
                inc="sP")
            e_vw = P.op("vector", lambda e: e.tensor_copy(w_t[:], pcum[:, 128:256]),
                        waits=(e_tw2,))
            e_cd1 = P.op("scalar", lambda e: e.activation(
                cd_col[:, 0:1], cs_fm[:, 255:256], AF.Exp))
            e_cd2 = P.op("tensor", lambda e: e.transpose(
                pcum[0:1, 256:320], cd_col[:, 0:1], g["id_f32"][0:64, 0:64]),
                waits=(("sA", e_cd1[1]), e_vw), inc="sP")
            e_cd3 = P.op("vector", lambda e: e.tensor_copy(
                cd_row[:], pcum[0:1, 256:320]), waits=(e_cd2,))
            e_cd4 = P.op("tensor", lambda e: e.matmul(
                pcum[0:128, 320:384], g["ones_row_f32"][:], cd_row[:],
                start=True, stop=True), waits=(e_cd3,), inc="sP")
            e_cd5 = P.op("vector", lambda e: e.tensor_copy(
                cd_all[:], pcum[:, 320:384]), waits=(e_cd4,))
            e_cb1 = P.op("tensor", lambda e: e.matmul(
                pcbt[:, 0:256], xb[:, 32 * 256:32 * 256 + 128],
                xb[:, 33 * 256:34 * 256], start=True, stop=True),
                waits=(ev_x,), inc="sP")
            e_cb2 = P.op("tensor", lambda e: e.matmul(
                pcbt[:, 256:512], xb[:, 32 * 256 + 128:33 * 256],
                xb[:, 33 * 256:34 * 256], start=True, stop=True), inc="sP")
            e_cbt0 = P.op("scalar", lambda e: e.activation(
                cbt_bf[:, 0:256], pcbt[:, 0:256], AF.Copy), waits=(e_cb2,))
            e_cbt = P.op("scalar", lambda e: e.activation(
                cbt_bf[:, 256:384], pcbt[:, 384:512], AF.Copy))
            e_bt1 = P.op("tensor", lambda e: e.transpose(
                ptr[0][:, 0:128], xb[:, 32 * 256:32 * 256 + 128], g["id_bf"][:]),
                waits=(ev_x,), inc="sP")
            e_bt2 = P.op("tensor", lambda e: e.transpose(
                ptr[0][:, 128:256], xb[:, 32 * 256 + 128:33 * 256], g["id_bf"][:]),
                inc="sP")
            e_btv = P.op("scalar", lambda e: e.activation(
                bt_bf[:], ptr[0][:, 0:256], AF.Copy), waits=(e_bt2,))

            dbg_ev = {}
            ev_gn_last = [ev_gn_prev[0]]
            ev_xw_d = {}
            ev_seg = {}
            ev_brcE = {}
            ev_E = {}
            ev_sc = {}
            ev_ce = {}
            ev_xw = {}
            ev_ydiag = {}
            ev_yoff = {}
            ev_states = {}
            ev_hupd = {}
            ev_hbf = {}
            ev_xtv = {}
            ev_evac = {}
            ev_g = {}
            ev_sq = {}

            def emit_xtr(f):
                pb = ptr[f % 2]
                reg = 256 if f % 2 == 0 else 512
                w = [ev_x, ev_xtv.get(f - 1), ev_xtv.get(f - 2)]
                if f <= 1:
                    w.append(e_btv)
                e1 = P.op("tensor", lambda e, reg=reg: e.transpose(
                    pb[:, reg:reg + 128], xb[:, f * 256:f * 256 + 128],
                    g["id_bf"][:]), waits=w, inc="sP")
                e2 = P.op("tensor", lambda e, reg=reg: e.transpose(
                    pb[:, reg + 128:reg + 256],
                    xb[:, f * 256 + 128:(f + 1) * 256],
                    g["id_bf"][:]), inc="sP")
                ev_xtv[f] = P.op("scalar", lambda e, reg=reg: e.activation(
                    xt_bf[f % 2][:], pb[:, reg:reg + 256], AF.Copy),
                    waits=(e2, ev_ydiag.get(2 * (f - 2) + 1),
                           ev_xw.get(2 * (f - 2) + 1),
                           ev_xw_d.get(2 * (f - 2) + 1)))
                if debug and c == 0 and f < 3:
                    dbg_ev[("X", f)] = P.dma("sync", d_dbgX[f], xt_bf[f % 2][:],
                                             waits=(ev_xtv[f],))

            def emit_seg(h):
                par = h % 2
                ps = pseg[par]
                w = [e_lo, ev_E.get(h - 2)]
                rhs2 = hilo[:].rearrange("p (r c) -> p r c", r=1).broadcast_to(
                    [128, 2, 256])
                P.op("tensor", lambda e, rhs2=rhs2: e.matmul(
                    ps[:, 0:512], sel_sb[:, h * 128:(h + 1) * 128], rhs2,
                    start=True, stop=False), waits=w, inc=None)
                ev_seg[h] = P.op("tensor", lambda e: e.matmul(
                    ps[:, 0:512], g["negdiag_bf"][:], g["u01_bf"][:],
                    start=False, stop=True), inc="sP")

            def emit_brcE(h):
                par = h % 2
                pc = pce if par == 0 else pcum
                w = [e_ecs, ev_ce.get(h - 2), ev_hupd.get(h - 2)]
                if par == 1:
                    w.append(e_cd5)
                ev_brcE[h] = P.op("tensor", lambda e, pc=pc: e.matmul(
                    pc[:, 0:256], sel_sb[0:64, h * 128:(h + 1) * 128], ecs_bf[:],
                    start=True, stop=True), waits=w, inc="sP")

            def emit_E(h):
                par = h % 2
                ps = pseg[par]
                w = [ev_seg[h], ev_sc.get(h - 2), e_b2,
                     dbg_ev.get(("E", h - 2))]
                P.op("scalar", lambda e: e.activation(
                    E_bf[par][:, 0:256], ps[:, 0:256], AF.Exp,
                    bias=b2[:, h:h + 1]), waits=w)
                ev_E[h] = P.op("scalar", lambda e: e.activation(
                    E_bf[par][:, 256:384], ps[:, 384:512], AF.Exp,
                    bias=b2[:, 64 + h:64 + h + 1]))
                if debug and c == 0 and h < 6:
                    dbg_ev[("E", h)] = P.dma("sync", d_dbgE[h], E_bf[par][:],
                                             waits=(ev_E[h],))

            def emit_ce(h):
                par = h % 2
                pc = pce if par == 0 else pcum
                w = [ev_brcE[h], ev_yoff.get(h - 2)]
                ev_ce[h] = P.op("vector", lambda e, pc=pc: e.tensor_tensor(
                    ce_bf[par][:], xb[:, 33 * 256:34 * 256], pc[:, 0:256],
                    ALU.mult), waits=w)

            def emit_scores(h):
                par = h % 2
                w = [ev_E[h], e_cbt, ev_ydiag.get(h - 2),
                     dbg_ev.get(("S", h - 2))]
                ev_sc[h] = P.op("vector", lambda e: e.tensor_tensor(
                    sc_bf[par][:, 0:384], E_bf[par][:, 0:384],
                    cbt_bf[:, 0:384], ALU.mult), waits=w)
                if debug and c == 0 and h < 6:
                    dbg_ev[("S", h)] = P.dma("sync", d_dbgS[h], sc_bf[par][:],
                                             waits=(ev_sc[h],))

            def emit_xw(h):
                par = h % 2
                f = h // 2
                s = (h % 2) * 64
                w = [ev_xtv.get(f), e_vw, ev_states.get(h - 2)]
                ev_xw_d[h] = P.op("vector", lambda e: e.tensor_scalar(
                    xw_bf[par][:, 0:64], xt_bf[f % 2][:, s:s + 64],
                    w_t[:, h:h + 1], None, ALU.mult), waits=w)
                ev_xw[h] = P.op("gpsimd", lambda e: e.tensor_scalar(
                    xw_bf[par][:, 64:128], xt_bf[f % 2][:, 128 + s:128 + s + 64],
                    w_t[:, 64 + h:64 + h + 1], None, ALU.mult), waits=w)

            def emit_compute(h):
                par = h % 2
                scpar = h % 2
                f = h // 2
                s = (h % 2) * 64
                yp = pyp_[f % 2]
                w = [ev_sc[h], ev_xtv.get(f)]
                if h % 2 == 0:
                    w.append(ev_evac.get(f - 2))
                    if f < 2:
                        w.append(ev_gn_last[0])
                P.op("tensor", lambda e, yp=yp: e.matmul(
                    yp[s:s + 64, 0:256], xt_bf[f % 2][:, s:s + 64],
                    sc_bf[scpar][:, 0:256], start=True, stop=False), waits=w,
                    inc=None)
                ev_ydiag[h] = P.op("tensor", lambda e, yp=yp: e.matmul(
                    yp[s:s + 64, 128:256], xt_bf[f % 2][:, 128 + s:128 + s + 64],
                    sc_bf[scpar][:, 256:384], start=False, stop=False), inc="sP")
                ev_yoff[h] = P.op("tensor", lambda e, yp=yp: e.matmul(
                    yp[s:s + 64, 0:256], h_b[:, h * 64:(h + 1) * 64],
                    ce_bf[par][:], start=False, stop=(h % 2 == 1)),
                    waits=(ev_ce[h], ev_hb0), inc="sP") if not TEST_NO_YOFF else P.op(
                    "tensor", lambda e, tp=tp: e.matmul(
                    pyp[s:s + 64, 0:256], g["id_bf"][:, 0:64], zero_bf[:],
                    start=False, stop=(h % 2 == 1), tile_position=tp),
                    waits=(ev_ce[h],), inc="sP")
                pc = pce if par == 0 else pcum
                w3 = [ev_xw[h], ev_xw_d[h], e_btv, ev_ce[h], ev_hupd.get(h - 2)]
                P.op("tensor", lambda e, pc=pc: e.matmul(
                    pc[:, 256:320], bt_bf[:, 0:128], xw_bf[par][:, 0:64],
                    start=False, stop=False), waits=w3, inc=None)
                ev_states[h] = P.op("tensor", lambda e, pc=pc: e.matmul(
                    pc[:, 256:320], bt_bf[:, 128:256], xw_bf[par][:, 64:128],
                    start=False, stop=True), inc="sP")

            def emit_hupd(h):
                par = h % 2
                pc = pce if par == 0 else pcum
                w = [ev_states[h], ev_yoff[h], e_cd5]
                ev_hupd[h] = P.op("vector", lambda e, pc=pc: e.scalar_tensor_tensor(
                    h_f[:, h * 64:(h + 1) * 64], h_f[:, h * 64:(h + 1) * 64],
                    cd_all[:, h:h + 1], pc[:, 256:320], ALU.mult, ALU.add),
                    waits=w)
                ev_hbf[h] = P.op("gpsimd", lambda e: e.tensor_copy(
                    h_b[:, h * 64:(h + 1) * 64], h_f[:, h * 64:(h + 1) * 64]),
                    waits=(ev_hupd[h],))

            def emit_evac(f):
                if debug and c == 0 and f < 2:
                    epv = P.op("scalar", lambda e: e.activation(
                        dbgP_sb[:], pyp[:, 0:256], AF.Copy),
                        waits=(ev_yoff[2 * f + 1],))
                    dbg_ev[("P", f)] = P.dma("sync", d_dbgP[f], dbgP_sb[:],
                                             waits=(epv,))
                w = [ev_yoff[2 * f + 1], ("sV", M[c - 1]["sV"]),
                     ("sA", M[c - 1]["sA"]), ev_x, dbg_ev.get(("P", f))]
                ev_evac[f] = P.op("vector", lambda e: e.scalar_tensor_tensor(
                    y_sb[:, f * 256:(f + 1) * 256], xb[:, f * 256:(f + 1) * 256],
                    g["dp_pm"][:, f:f + 1], pyp_[f % 2][:, 0:256],
                    ALU.mult, ALU.add), waits=w)
                ev_g[f] = P.op("vector", lambda e: e.tensor_tensor(
                    y_sb[:, f * 256:(f + 1) * 256], y_sb[:, f * 256:(f + 1) * 256],
                    zb[:, f * 256:(f + 1) * 256], ALU.mult), waits=(ev_z,))
                ev_sq[f] = P.op("scalar", lambda e: e.activation(
                    sq_sb[:, f * 256:(f + 1) * 256], y_sb[:, f * 256:(f + 1) * 256],
                    AF.Square), waits=(ev_g[f],))

            emit_xtr(0)
            emit_xtr(1)
            for h in range(NH + 1):
                if h < NH:
                    emit_seg(h)
                    emit_brcE(h)
                    emit_E(h)
                    emit_ce(h)
                    emit_scores(h)
                    emit_xw(h)
                if h >= 1:
                    hh = h - 1
                    emit_compute(hh)
                    emit_hupd(hh)
                    if hh % 2 == 1:
                        emit_evac(hh // 2)
                        if (hh // 2 + 2) <= 31:
                            emit_xtr(hh // 2 + 2)

            e_st = None
            for f in range(32):
                e_st = P.op("tensor", lambda e, f=f: e.matmul(
                    pcbt[0:1, 0:256], g["i4096_col_bf"][:],
                    sq_sb[:, f * 256:(f + 1) * 256],
                    start=(f == 0), stop=(f == 31)),
                    waits=(ev_sq[f],), inc="sP" if f == 31 else None)
            e_ln = P.op("scalar", lambda e: e.activation(
                ms_row[:], pcbt[0:1, 0:256], AF.Ln, bias=g["eps11"][0:1, 0:1]),
                waits=(e_st,))
            e_rstd = P.op("scalar", lambda e: e.activation(
                rstd_row[:], ms_row[:], AF.Exp, scale=-0.5))
            e_rs = P.op("tensor", lambda e: e.matmul(
                pcbt[:, 256:512], g["ones_row_f32"][:], rstd_row[:],
                start=True, stop=True), waits=(e_rstd,), inc="sP")
            evdy = None
            if debug:
                evdy = P.dma("sync", d_y[c], y_sb[:],
                             waits=tuple(ev_g[f] for f in range(32)))
                P.dma("sync", d_ms[c], ms_row[:], waits=(("sA", e_rstd[1]),))
            e_gnl = None
            for f in range(32):
                e_gnl = P.op("vector", lambda e, f=f: e.scalar_tensor_tensor(
                    gn_sb[:, f * 256:(f + 1) * 256], y_sb[:, f * 256:(f + 1) * 256],
                    g["nw_pm"][:, f:f + 1], pcbt[:, 256:512], ALU.mult, ALU.mult),
                    waits=(e_rs, ev_g[f], ev_gn_dma.get(c - 1)))
            ev_gn_prev[0] = e_gnl
            ev_gn_dma[c] = P.dma("sync",
                                 d_gn[:, :, csl].rearrange("t p j -> p t j"),
                                 gn_sb[:].rearrange("p (t j) -> p t j", j=256),
                                 waits=(e_gnl,),
                                 sem="dGNa" if par_c == 0 else "dGNb")
            M[c] = snap()

        for c in range(NCH):
            emit_chunk(c)
        if debug:
            P.final_wait()
        P.flush("p2a")
    _mix_stack.close()

    with ExitStack() as st:
        def al(name, shape, dt=F32):
            return st.enter_context(nc.sbuf_tensor(name, list(shape), dt))
        gn_t = al("gn_t", [128, 32 * 1024], BF16)
        wo_bf = [al(f"wo_bf{i}", [128, 32 * 128], BF16) for i in range(2)]
        u_st = [al(f"u_st{i}", [128, 1024], F32) for i in range(2)]
        r_sb = al("r_sb", [128, 16 * 1024], F32)
        sqb = al("sqb", [128, 1024], BF16)
        murow = al("murow", [1, 1024], F32)
        rstd2 = al("rstd2", [1, 1024], F32)
        mubc = al("mubc", [128, 1024], F32)
        rsbc = mubc
        xst = [al(f"xst{i}", [128, 1024], F32) for i in range(2)]
        pd = [st.enter_context(nc.psum_tensor(f"pd{i}", [128, 1024], F32))
              for i in range(2)]
        pms2 = st.enter_context(nc.psum_tensor("pms2", [128, 1024], F32))

        P.begin()
        for tt_ in range(2):
            tsl = slice(tt_ * 1024, (tt_ + 1) * 1024)
            tdep = (("sP", P.count["sP"]), ("sV", P.count["sV"]),
                    ("sA", P.count["sA"]), ("sG", P.count["sG"])
                    ) + P.dma_snapshot()
            ev_gn = P.dma("sync", gn_t[:].rearrange("p (f j) -> p f j", j=1024),
                          d_gn[:, :, tsl].rearrange("f p j -> p f j"), waits=tdep,
                          sem="dXa" if tt_ == 0 else "dXb")
            ev_wo = {}
            ev_wo[0] = P.dma("gpsimd",
                             wo_bf[0][:].rearrange("p (f j) -> p f j", j=128),
                             t["w_outT"][:, 0:128].rearrange("(f p) c -> p f c", p=128),
                             waits=tdep, sem="dWa")
            ev_u = {}
            ev_mm = {}
            ev_r = {}
            ev_stat = {}
            for dp in range(8):
                for dd in (2 * dp, 2 * dp + 1):
                    if dd + 1 < 16:
                        ev_wo[dd + 1] = P.dma(
                            "gpsimd",
                            wo_bf[(dd + 1) % 2][:].rearrange("p (f j) -> p f j", j=128),
                            t["w_outT"][:, (dd + 1) * 128:(dd + 2) * 128].rearrange(
                                "(f p) c -> p f c", p=128),
                            waits=(ev_mm.get(dd - 1),),
                            sem="dWa" if (dd + 1) % 2 == 0 else "dWb")
                    ev_u[dd] = P.dma("sync", u_st[dd % 2][:],
                                     t["u_fm"][dd * 128:(dd + 1) * 128, tsl],
                                     waits=(ev_r.get(dd - 2),),
                                     sem="dUa" if dd % 2 == 0 else "dUb")
                    ps = pd[dd % 2]
                    evm = None
                    for fi in range(32):
                        for nn in range(2):
                            evm = P.op(
                                "tensor",
                                lambda e, ps=ps, dd=dd, fi=fi, nn=nn: e.matmul(
                                    ps[:, nn * 512:(nn + 1) * 512],
                                    wo_bf[dd % 2][:, fi * 128:(fi + 1) * 128],
                                    gn_t[:, fi * 1024 + nn * 512:
                                         fi * 1024 + (nn + 1) * 512],
                                    start=(fi == 0), stop=(fi == 31)),
                                waits=(ev_wo[dd], ev_gn, ev_r.get(dd - 2))
                                if fi == 0 and nn == 0 else (),
                                inc="sP" if (fi == 31 and nn == 1) else None)
                    ev_mm[dd] = evm
                    ev_r[dd] = P.op("vector", lambda e, ps=ps, dd=dd: e.tensor_tensor(
                        r_sb[:, dd * 1024:(dd + 1) * 1024], ps[:, 0:1024],
                        u_st[dd % 2][:], ALU.add), waits=(evm, ev_u[dd]))
                    for nn in range(2):
                        ev_stat[dd] = P.op(
                            "tensor",
                            lambda e, dd=dd, nn=nn: e.matmul(
                                pms2[0:1, nn * 512:(nn + 1) * 512],
                                g["ones_col_f32"][:],
                                r_sb[:, dd * 1024 + nn * 512:dd * 1024 + (nn + 1) * 512],
                                start=(dd == 0), stop=(dd == 15)),
                            waits=(ev_r[dd],) if nn == 0 else (),
                            inc="sP" if (dd == 15 and nn == 1) else None)
            e_mu = P.op("vector", lambda e: e.tensor_scalar(
                murow[:], pms2[0:1, 0:1024], 1.0 / 2048.0, None, ALU.mult),
                waits=(ev_stat[15],))
            e_mb = P.op("tensor", lambda e: e.matmul(
                pms2[:, 0:512], g["ones_row_f32"][:], murow[:, 0:512],
                start=True, stop=True), waits=(e_mu,), inc="sP")
            e_mb2 = P.op("tensor", lambda e: e.matmul(
                pms2[:, 512:1024], g["ones_row_f32"][:], murow[:, 512:1024],
                start=True, stop=True), inc="sP")
            e_mbv = P.op("vector", lambda e: e.tensor_copy(mubc[:], pms2[:, 0:1024]),
                         waits=(e_mb2,))
            ev_xc = {}
            ev_sq2 = {}
            ev_vst = None
            for dd in range(16):
                ev_xc[dd] = P.op("vector", lambda e, dd=dd: e.tensor_tensor(
                    r_sb[:, dd * 1024:(dd + 1) * 1024],
                    r_sb[:, dd * 1024:(dd + 1) * 1024], mubc[:], ALU.subtract),
                    waits=(e_mbv,))
                ev_sq2[dd] = P.op("scalar", lambda e, dd=dd: e.activation(
                    sqb[:], r_sb[:, dd * 1024:(dd + 1) * 1024], AF.Square),
                    waits=(ev_xc[dd], ev_vst))
                for nn in range(2):
                    ev_vst = P.op(
                        "tensor",
                        lambda e, dd=dd, nn=nn: e.matmul(
                            pms2[0:1, nn * 512:(nn + 1) * 512],
                            g["i2048_col_bf"][:],
                            sqb[:, nn * 512:(nn + 1) * 512],
                            start=(dd == 0), stop=(dd == 15)),
                        waits=(ev_sq2[dd], e_mbv) if nn == 0 else (),
                        inc="sP" if nn == 1 else None)
            e_ln2 = P.op("scalar", lambda e: e.activation(
                rstd2[:], pms2[0:1, 0:1024], AF.Ln, bias=g["eps11"][0:1, 0:1]),
                waits=(ev_vst,))
            e_rst2 = P.op("scalar", lambda e: e.activation(
                rstd2[:], rstd2[:], AF.Exp, scale=-0.5))
            e_rb = P.op("tensor", lambda e: e.matmul(
                pms2[:, 0:512], g["ones_row_f32"][:], rstd2[:, 0:512],
                start=True, stop=True), waits=(e_rst2,), inc="sP")
            e_rb2 = P.op("tensor", lambda e: e.matmul(
                pms2[:, 512:1024], g["ones_row_f32"][:], rstd2[:, 512:1024],
                start=True, stop=True), inc="sP")
            e_rbv = P.op("vector", lambda e: e.tensor_copy(rsbc[:], pms2[:, 0:1024]),
                         waits=(e_rb2,))
            ev_od = {}
            for dd in range(16):
                ex = P.op("vector", lambda e, dd=dd: e.tensor_tensor(
                    xst[dd % 2][:], r_sb[:, dd * 1024:(dd + 1) * 1024], rsbc[:],
                    ALU.mult), waits=(e_rbv, ev_od.get(dd - 2)))
                e1 = P.dma("sync", d_x1[dd, :, tsl], xst[dd % 2][:], waits=(ex,),
                           sem="dZOa" if dd % 2 == 0 else "dZOb")
                ev_od[dd] = P.dma("gpsimd", d_x1bf[dd, :, tsl], xst[dd % 2][:],
                                  waits=(ex,),
                                  sem="dZOa" if dd % 2 == 0 else "dZOb")
        P.flush("p2b")

    with ExitStack() as st:
        def al(name, shape, dt=F32):
            return st.enter_context(nc.sbuf_tensor(name, list(shape), dt))
        X1 = al("X1", [128, 16 * 1024], F32)
        xtb = al("xtb", [128, 16 * 1024], BF16)
        wsb = al("wsb", [128, 16 * 1024], BF16)
        hbf = al("hbf", [128, 8 * 1024], BF16)
        h2w = al("h2w", [128, 8 * 1024], BF16)
        wbc_sb = al("wbc_sb", [128, 1024], F32)
        x1s = [al(f"x1s{i}", [128, 1024], F32) for i in range(2)]
        sqs = al("sqs", [128, 1024], F32)
        gfm = al("gfm", [8, 1024], F32)
        gt_t = al("gt_t", [128, 64], F32)
        w_tt = al("w_tt", [128, 64], F32)
        w_fm = al("w_fm", [8, 1024], F32)
        mcol = al("mcol", [128, 4], F32)
        bdn_sb = al("bdn_sb", [NEXP, 2048], F32)
        pu = [st.enter_context(nc.psum_tensor(f"pu{i}", [128, 1024], F32))
              for i in range(2)]
        pdn = [st.enter_context(nc.psum_tensor(f"pdn{i}", [128, 1024], F32))
               for i in range(2)]

        P.begin()
        ev_bdn = None
        for n in range(NEXP):
            ev_bdn = P.dma("sync", bdn_sb[n:n + 1, :], t["bdn"][n:n + 1, :],
                           sem="dC")
        carry_mml = [{}]

        def sA_now():
            return ("sA", P.count["sA"])

        def sV_now():
            return ("sV", P.count["sV"])

        for rr in range(2):
            for tt_ in range(2):
                tsl = slice(tt_ * 1024, (tt_ + 1) * 1024)
                tdep = (("sP", P.count["sP"]), ("sV", P.count["sV"]),
                        ("sA", P.count["sA"]), ("sG", P.count["sG"])
                        ) + P.dma_snapshot()
                src = d_x1bf if rr == 0 else d_x2bf
                ev_xt = P.dma("sync",
                              xtb[:].rearrange("p (d j) -> p d j", j=1024),
                              src[:, :, tsl].rearrange("d p j -> p d j"),
                              waits=tdep, sem="dXa" if tt_ == 0 else "dXb")
                evg = None
                for k in range(16):
                    for nn in range(2):
                        evg = P.op("tensor", lambda e, k=k, nn=nn, rr=rr: e.matmul(
                            pu[0][0:8, nn * 512:(nn + 1) * 512],
                            g["wg_bf"][:, (rr * 16 + k) * 8:(rr * 16 + k + 1) * 8],
                            xtb[:, k * 1024 + nn * 512:k * 1024 + (nn + 1) * 512],
                            start=(k == 0), stop=(k == 15)),
                            waits=(ev_xt,) + tdep if (k == 0 and nn == 0) else (),
                            inc="sP" if (k == 15 and nn == 1) else None)
                e_gf = P.op("scalar", lambda e, rr=rr: e.activation(
                    gfm[:], pu[0][0:8, 0:1024], AF.Identity,
                    bias=g["bg2"][:, rr:rr + 1]), waits=(evg,))
                ev_gt = None
                for j in range(8):
                    ev_gt = P.op("tensor", lambda e, j=j: e.transpose(
                        pdn[0][0:128, j * 8:(j + 1) * 8],
                        gfm[:, j * 128:(j + 1) * 128], g["id_f32"][0:8, 0:8]),
                        waits=(e_gf,) if j == 0 else (), inc="sP")
                e_gtv = P.op("vector", lambda e: e.tensor_copy(
                    gt_t[:], pdn[0][:, 0:64]), waits=(ev_gt,))
                e_wt = None
                for j in range(8):
                    gsl = gt_t[:, j * 8:(j + 1) * 8]
                    wsl = w_tt[:, j * 8:(j + 1) * 8]
                    P.op("vector", lambda e, gsl=gsl: e.tensor_reduce(
                        mcol[:, 0:1], gsl, AX.X, ALU.min), waits=(e_gtv,))
                    P.op("vector", lambda e, gsl=gsl, wsl=wsl: e.tensor_scalar(
                        wsl, gsl, mcol[:, 0:1], None, ALU.is_equal))
                    P.op("vector", lambda e, gsl=gsl, wsl=wsl: e.scalar_tensor_tensor(
                        wsl, wsl, 32768.0, gsl, ALU.mult, ALU.add))
                    P.op("vector", lambda e, wsl=wsl: e.tensor_reduce(
                        mcol[:, 1:2], wsl, AX.X, ALU.min))
                    P.op("vector", lambda e, gsl=gsl, wsl=wsl: e.tensor_scalar(
                        wsl, gsl, mcol[:, 1:2], None, ALU.is_gt))
                    e6 = P.op("vector", lambda e, gsl=gsl: e.tensor_reduce(
                        mcol[:, 2:3], gsl, AX.X, ALU.max, negate=True))
                    e7 = P.op("scalar", lambda e, gsl=gsl, j=j: e.activation(
                        sqs[:, j * 8:(j + 1) * 8], gsl, AF.Exp,
                        bias=mcol[:, 2:3]), waits=(e6,))
                    P.op("vector", lambda e, wsl=wsl, j=j: e.tensor_tensor(
                        wsl, wsl, sqs[:, j * 8:(j + 1) * 8], ALU.mult),
                        waits=(e7,))
                    P.op("vector", lambda e, wsl=wsl: e.tensor_reduce(
                        mcol[:, 3:4], wsl, AX.X, ALU.add))
                    P.op("vector", lambda e: e.reciprocal(
                        mcol[:, 3:4], mcol[:, 3:4]))
                    e_wt = P.op("vector", lambda e, wsl=wsl: e.tensor_scalar(
                        wsl, wsl, mcol[:, 3:4], None, ALU.mult))
                ev_wf = None
                for j in range(8):
                    ev_wf = P.op("tensor", lambda e, j=j: e.transpose(
                        pdn[0][0:8, j * 128:(j + 1) * 128],
                        w_tt[:, j * 8:(j + 1) * 8], g["id_f32"][:]),
                        waits=(e_wt, e_gtv) if j == 0 else (), inc="sP")
                e_wfv = P.op("vector", lambda e: e.tensor_copy(
                    w_fm[:], pdn[0][0:8, 0:1024]), waits=(ev_wf,))
                if debug and tt_ == 0:
                    P.dma("sync", d_dbgG[rr], gfm[:], waits=(e_gf,))
                    ev_dwf = P.dma("sync", d_dbgWf[rr], w_fm[:], waits=(e_wfv,))
                    if rr == 1:
                        P.dma("sync",
                              d_dbgXt.rearrange("d p j -> p d j"),
                              xtb[:].rearrange("p (d j) -> p d j", j=1024),
                              waits=(ev_xt,))

                R = [wsb[:, 0:8192], wsb[:, 8192:16384]]
                ev_evp = {}
                ev_hb = {}
                ev_h2 = {}
                ev_wbcv = {}
                ev_x1dma = {}
                phases = []

                def ph_up(tag, src_fn, bias_fn, half):
                    def streams(reg_base):
                        return [(reg_base + k * 512, 512,
                                 src_fn(k, half)) for k in range(16)]
                    def mms(Rv, evs, sA0):
                        if half == 0 and isinstance(tag, int):
                            e_wb1 = P.op("tensor", lambda e, n=tag: e.matmul(
                                pdn[1][:, 0:512],
                                g["sel8"][:, n * 128:(n + 1) * 128],
                                w_fm[0:8, 0:512], start=True, stop=True),
                                waits=(e_wfv, ev_evp.get(("prev15",)),
                                       ), inc="sP")
                            e_wb2 = P.op("tensor", lambda e, n=tag: e.matmul(
                                pdn[1][:, 512:1024],
                                g["sel8"][:, n * 128:(n + 1) * 128],
                                w_fm[0:8, 512:1024], start=True, stop=True),
                                inc="sP")
                            ev_wbcv[tag] = P.op("vector", lambda e: e.tensor_copy(
                                wbc_sb[:], pdn[1][:, 0:1024]),
                                waits=(e_wb2, ev_h2.get(("wbc_last",))))
                        evm = None
                        for o in range(half * 4, half * 4 + 4):
                            for k in range(16):
                                for nn in range(2):
                                    evm = P.op(
                                        "tensor",
                                        lambda e, o=o, k=k, nn=nn, Rv=Rv, half=half: e.matmul(
                                            pu[o % 2][:, nn * 512:(nn + 1) * 512],
                                            Rv[:, k * 512 + (o - half * 4) * 128:
                                               k * 512 + (o - half * 4) * 128 + 128],
                                            xtb[:, k * 1024 + nn * 512:
                                                k * 1024 + (nn + 1) * 512],
                                            start=(k == 0), stop=(k == 15)),
                                        waits=(evs, ev_xt,
                                               ("sA", ev_hb[(tag, o - 2)][1])
                                               if (tag, o - 2) in ev_hb else sA0)
                                        if (k == 0 and nn == 0) else (),
                                        inc="sP" if (k == 15 and nn == 1) else None)
                            ev_hb[(tag, o)] = P.op(
                                "scalar", lambda e, o=o: e.activation(
                                    hbf[:, o * 1024:(o + 1) * 1024],
                                    pu[o % 2][:, 0:1024], AF.Identity,
                                    bias=bias_fn(o)),
                                waits=(evm,))
                        return evm
                    return streams, mms

                def ph_sw(tag, src_fn, bias_fn, use_wbc):
                    def streams(reg_base):
                        return [(reg_base + k2 * 1024, 1024, src_fn(k2))
                                for k2 in range(8)]
                    def mms(Rv, evs, sA0):
                        evm = None
                        for o2 in range(8):
                            for k2 in range(8):
                                for nn in range(2):
                                    evm = P.op(
                                        "tensor",
                                        lambda e, o2=o2, k2=k2, nn=nn, Rv=Rv: e.matmul(
                                            pu[o2 % 2][:, nn * 512:(nn + 1) * 512],
                                            Rv[:, k2 * 1024 + o2 * 128:
                                               k2 * 1024 + o2 * 128 + 128],
                                            hbf[:, k2 * 1024 + nn * 512:
                                                k2 * 1024 + (nn + 1) * 512],
                                            start=(k2 == 0), stop=(k2 == 7)),
                                        waits=(evs, ("sA", ev_hb[(tag, 7)][1]),
                                               ev_h2.get((tag, o2 - 2)))
                                        if (k2 == 0 and nn == 0) else (),
                                        inc="sP" if (k2 == 7 and nn == 1) else None)
                            e_si = P.op("scalar", lambda e, o2=o2: e.activation(
                                sqs[:], pu[o2 % 2][:, 0:1024], AF.Silu,
                                bias=bias_fn(o2)),
                                waits=(evm, ev_h2.get((tag, o2 - 1))))
                            if use_wbc:
                                e_t1 = P.op("vector", lambda e, o2=o2: e.tensor_tensor(
                                    sqs[:], sqs[:], hbf[:, o2 * 1024:(o2 + 1) * 1024],
                                    ALU.mult), waits=(e_si,))
                                ev_h2[(tag, o2)] = P.op(
                                    "vector", lambda e, o2=o2: e.tensor_tensor(
                                        h2w[:, o2 * 1024:(o2 + 1) * 1024], sqs[:],
                                        wbc_sb[:], ALU.mult),
                                    waits=(ev_wbcv.get(tag),))
                                if o2 == 7:
                                    ev_h2[("wbc_last",)] = ev_h2[(tag, 7)]
                            else:
                                ev_h2[(tag, o2)] = P.op(
                                    "vector", lambda e, o2=o2: e.tensor_tensor(
                                        h2w[:, o2 * 1024:(o2 + 1) * 1024], sqs[:],
                                        hbf[:, o2 * 1024:(o2 + 1) * 1024], ALU.mult),
                                    waits=(e_si,))
                        return evm
                    return streams, mms

                def ph_dn(tag, src_fn, half, first, bias_close, sd_half):
                    def streams(reg_base):
                        return [(reg_base + k3 * 1024, 1024, src_fn(k3, half))
                                for k3 in range(8)]
                    def mms(Rv, evs, sA0):
                        evm = None
                        for dd in range(half * 8, half * 8 + 8):
                            for k3 in range(8):
                                for nn in range(2):
                                    is_last = (k3 == 7 and nn == 1)
                                    stp = is_last and not bias_close
                                    evm = P.op(
                                        "tensor",
                                        lambda e, dd=dd, k3=k3, nn=nn, Rv=Rv,
                                        half=half, stp=stp: e.matmul(
                                            pdn[dd % 2][:, nn * 512:(nn + 1) * 512],
                                            Rv[:, k3 * 1024 + (dd - half * 8) * 128:
                                               k3 * 1024 + (dd - half * 8) * 128 + 128],
                                            h2w[:, k3 * 1024 + nn * 512:
                                                k3 * 1024 + (nn + 1) * 512],
                                            start=(k3 == 0), stop=stp),
                                        waits=(evs, ("sV", ev_h2[(tag, 7)][1]),
                                               ev_evp.get((tag, dd - 2))
                                               if dd - 2 >= half * 8 else
                                               ev_evp.get(("prev", 14 + dd)),
                                               ev_evp.get(("prev", dd)))
                                        if (k3 == 0 and nn == 0) else (),
                                        inc="sP" if (is_last and not bias_close)
                                        else None)
                            if bias_close:
                                for nn in range(2):
                                    evm = P.op("tensor", lambda e, dd=dd, nn=nn: e.matmul(
                                        pdn[dd % 2][:, nn * 512:(nn + 1) * 512],
                                        bdn_sb[:, dd * 128:(dd + 1) * 128],
                                        w_fm[0:6, nn * 512:(nn + 1) * 512],
                                        start=False, stop=(nn == 1)),
                                        waits=(e_wfv, ev_bdn) if nn == 0 else (),
                                        inc="sP" if nn == 1 else None)
                            if first:
                                ev_x1dma[dd] = P.dma(
                                    "sync", x1s[dd % 2][:], d_x1[dd, :, tsl],
                                    waits=(ev_evp.get((tag, dd - 2)),),
                                    sem="dUa" if dd % 2 == 0 else "dUb")
                                ev_evp[(tag, dd)] = P.op(
                                    "vector", lambda e, dd=dd: e.tensor_tensor(
                                        X1[:, dd * 1024:(dd + 1) * 1024],
                                        pdn[dd % 2][:, 0:1024], x1s[dd % 2][:],
                                        ALU.add),
                                    waits=(evm, ev_x1dma[dd]))
                            elif sd_half == 1:
                                ev_evp[(tag, dd)] = P.op(
                                    "vector", lambda e, dd=dd: e.scalar_tensor_tensor(
                                        X1[:, dd * 1024:(dd + 1) * 1024],
                                        pdn[dd % 2][:, 0:1024],
                                        g["bsd_pm"][:, dd:dd + 1],
                                        X1[:, dd * 1024:(dd + 1) * 1024],
                                        ALU.add, ALU.add),
                                    waits=(evm, ev_evp.get(("prev", dd))))
                            else:
                                ev_evp[(tag, dd)] = P.op(
                                    "vector", lambda e, dd=dd: e.tensor_tensor(
                                        X1[:, dd * 1024:(dd + 1) * 1024],
                                        X1[:, dd * 1024:(dd + 1) * 1024],
                                        pdn[dd % 2][:, 0:1024], ALU.add),
                                    waits=(evm, ev_evp.get(("prev", dd))))
                        return evm
                    return streams, mms

                def close_tag(tag):
                    for dd in range(16):
                        ev_evp[("prev", dd)] = ev_evp[(tag, dd)]
                    ev_evp[("prev15",)] = ev_evp[(tag, 15)]

                for n in range(NEXP):
                    up_src = (lambda n: lambda k, half: t["wup"][
                        n, k * 128:(k + 1) * 128, half * 512:(half + 1) * 512])(n)
                    sw_src = (lambda n: lambda k2: t["wsw"][
                        n, k2 * 128:(k2 + 1) * 128, :])(n)
                    dn_src = (lambda n: lambda k3, half: t["wdn"][
                        n, k3 * 128:(k3 + 1) * 128,
                        half * 1024:(half + 1) * 1024])(n)
                    upb = (lambda n: lambda o: g["bup_pm"][
                        :, n * 8 + o:n * 8 + o + 1])(n)
                    swb = (lambda n: lambda o2: g["bsw_pm"][
                        :, n * 8 + o2:n * 8 + o2 + 1])(n)
                    phases.append((ph_up(n, up_src, upb, 0), n))
                    phases.append((ph_up(n, up_src, upb, 1), n))
                    phases.append((ph_sw(n, sw_src, swb, True), n))
                    phases.append((ph_dn(n, dn_src, 0, n == 0, False, None), n))
                    phases.append((ph_dn(n, dn_src, 1, n == 0, False, None), n))
                for sh_ in range(2):
                    tag = ("s", sh_)
                    su_src = (lambda sh_: lambda k, half: t["wsu"][
                        k * 128:(k + 1) * 128,
                        sh_ * 1024 + half * 512:sh_ * 1024 + (half + 1) * 512])(sh_)
                    sa_src = lambda k2: t["wsa"][k2 * 128:(k2 + 1) * 128, :]
                    sd_src = (lambda sh_: lambda k3, half: t["wsd"][
                        (sh_ * 8 + k3) * 128:(sh_ * 8 + k3 + 1) * 128,
                        half * 1024:(half + 1) * 1024])(sh_)
                    sub = (lambda sh_: lambda o: g["bsu_pm"][
                        :, sh_ * 8 + o:sh_ * 8 + o + 1])(sh_)
                    sab = lambda o2: g["bsa_pm"][:, o2:o2 + 1]
                    phases.append((ph_up(tag, su_src, sub, 0), tag))
                    phases.append((ph_up(tag, su_src, sub, 1), tag))
                    phases.append((ph_sw(tag, sa_src, sab, False), tag))
                    phases.append((ph_dn(tag, sd_src, 0, False,
                                          sh_ == 1, sh_), tag))
                    phases.append((ph_dn(tag, sd_src, 1, False,
                                          sh_ == 1, sh_), tag))

                nph = len(phases)
                ev_stream = {}
                ev_mml = dict(carry_mml[0])

                def emit_stream(i):
                    (streams, _), _tg = phases[i]
                    reg = i % 2
                    evd = None
                    w0 = (ev_mml.get(i - 2),) + tdep
                    for j, (off, width, src) in enumerate(streams(reg * 8192)):
                        evd = P.dma("gpsimd", wsb[:, off:off + width], src,
                                    waits=w0 if j == 0 else (),
                                    sem="dWa" if reg == 0 else "dWb")
                    ev_stream[i] = evd

                emit_stream(0)
                emit_stream(1)
                tag_done = set()
                for i in range(nph):
                    (streams, mms), tg = phases[i]
                    sA0 = ("sA", P.count["sA"])
                    ev_mml[i] = mms(R[i % 2], ev_stream[i], sA0)
                    if i + 2 < nph:
                        emit_stream(i + 2)
                    if (i % 5) == 4:
                        close_tag(tg)
                carry_mml[0] = {-2: ev_mml[nph - 2], -1: ev_mml[nph - 1]}

                if rr == 0:
                    for dd in range(16):
                        P.dma("gpsimd", d_x2bf[dd, :, tsl],
                              X1[:, dd * 1024:(dd + 1) * 1024],
                              waits=(ev_evp[(("s", 1), dd)],),
                              sem="dZOa" if dd % 2 == 0 else "dZOb")
                if rr == 1:
                    if debug:
                        for dd in range(16):
                            P.dma("sync", d_x3[dd, :, tsl],
                                  X1[:, dd * 1024:(dd + 1) * 1024],
                                  waits=(ev_evp[(("s", 1), dd)],))
                    ev_fx = {}
                    ev_fr = {}
                    ev_fst = None
                    for dd in range(16):
                        ev_fx[dd] = P.dma("sync", x1s[dd % 2][:],
                                          d_x1[dd, :, tsl],
                                          waits=(ev_fr.get(dd - 2),
                                                 ev_evp[(0, 14 + dd)] if dd < 2
                                                 else None),
                                          sem="dUa" if dd % 2 == 0 else "dUb")
                        ev_fr[dd] = P.op("vector", lambda e, dd=dd: e.tensor_tensor(
                            X1[:, dd * 1024:(dd + 1) * 1024],
                            X1[:, dd * 1024:(dd + 1) * 1024], x1s[dd % 2][:],
                            ALU.add), waits=(ev_fx[dd], ev_evp[(("s", 1), dd)],
                                             ("dS", P.count["dS"]) if debug else None))
                        for nn in range(2):
                            ev_fst = P.op(
                                "tensor",
                                lambda e, dd=dd, nn=nn: e.matmul(
                                    pu[0][0:1, nn * 512:(nn + 1) * 512],
                                    g["ones_col_f32"][:],
                                    X1[:, dd * 1024 + nn * 512:
                                       dd * 1024 + (nn + 1) * 512],
                                    start=(dd == 0), stop=(dd == 15)),
                                waits=(ev_fr[dd], sA_now()) if nn == 0 else (),
                                inc="sP" if (dd == 15 and nn == 1) else None)
                    e_mu = P.op("vector", lambda e: e.tensor_scalar(
                        gfm[0:1, 0:1024], pu[0][0:1, 0:1024], 1.0 / 2048.0, None,
                        ALU.mult), waits=(ev_fst,))
                    e_mb = P.op("tensor", lambda e: e.matmul(
                        pu[0][:, 0:512], g["ones_row_f32"][:], gfm[0:1, 0:512],
                        start=True, stop=True), waits=(e_mu,), inc="sP")
                    e_mb2 = P.op("tensor", lambda e: e.matmul(
                        pu[0][:, 512:1024], g["ones_row_f32"][:],
                        gfm[0:1, 512:1024], start=True, stop=True), inc="sP")
                    e_mbv = P.op("vector", lambda e: e.tensor_copy(
                        wbc_sb[:], pu[0][:, 0:1024]), waits=(e_mb2,))
                    ev_vst = None
                    for dd in range(16):
                        exc = P.op("vector", lambda e, dd=dd: e.tensor_tensor(
                            X1[:, dd * 1024:(dd + 1) * 1024],
                            X1[:, dd * 1024:(dd + 1) * 1024], wbc_sb[:],
                            ALU.subtract), waits=(e_mbv,))
                        esq = P.op("scalar", lambda e, dd=dd: e.activation(
                            sqs[:], X1[:, dd * 1024:(dd + 1) * 1024], AF.Square),
                            waits=(exc, ev_vst))
                        for nn in range(2):
                            ev_vst = P.op(
                                "tensor",
                                lambda e, dd=dd, nn=nn: e.matmul(
                                    pu[1][0:1, nn * 512:(nn + 1) * 512],
                                    g["ones_col_f32"][:],
                                    sqs[:, nn * 512:(nn + 1) * 512],
                                    start=(dd == 0), stop=(dd == 15)),
                                waits=(esq,) if nn == 0 else (),
                                inc="sP" if nn == 1 else None)
                    e_v2 = P.op("vector", lambda e: e.tensor_scalar(
                        gfm[0:1, 0:1024], pu[1][0:1, 0:1024], 1.0 / 2048.0, None,
                        ALU.mult), waits=(ev_vst,))
                    e_ln3 = P.op("scalar", lambda e: e.activation(
                        gfm[0:1, 0:1024], gfm[0:1, 0:1024], AF.Ln,
                        bias=g["eps11"][0:1, 0:1]), waits=(e_v2,))
                    e_rs3 = P.op("scalar", lambda e: e.activation(
                        gfm[0:1, 0:1024], gfm[0:1, 0:1024], AF.Exp, scale=-0.5))
                    e_rb = P.op("tensor", lambda e: e.matmul(
                        pu[1][:, 0:512], g["ones_row_f32"][:], gfm[0:1, 0:512],
                        start=True, stop=True), waits=(e_rs3,), inc="sP")
                    e_rb2 = P.op("tensor", lambda e: e.matmul(
                        pu[1][:, 512:1024], g["ones_row_f32"][:],
                        gfm[0:1, 512:1024], start=True, stop=True), inc="sP")
                    e_rbv = P.op("vector", lambda e: e.tensor_copy(
                        wbc_sb[:], pu[1][:, 0:1024]), waits=(e_rb2,))
                    ev_odma = {}
                    for dd in range(16):
                        eo = P.op("vector", lambda e, dd=dd: e.tensor_tensor(
                            x1s[dd % 2][:], X1[:, dd * 1024:(dd + 1) * 1024],
                            wbc_sb[:], ALU.mult),
                            waits=(e_rbv, ev_odma.get(dd - 2)))
                        ev_odma[dd] = P.dma("sync", d_out[dd, :, tsl],
                                            x1s[dd % 2][:], waits=(eo,),
                                            sem="dOUT")
        P.final_wait()
        P.flush("p4")

    nc._prog = P
    return nc



def _host_arrays(inp):
    a = {}
    a["w_inT"] = np.ascontiguousarray(inp["L0_attn_W_in"].T)
    a["w_outT"] = np.ascontiguousarray(inp["L0_attn_W_out"].T)
    a["conv_w_pm"] = np.ascontiguousarray(
        inp["L0_attn_conv_w"].reshape(34, 128, 4).transpose(1, 0, 2).reshape(128, 136))
    a["conv_b_pm"] = np.ascontiguousarray(inp["L0_attn_conv_b"].reshape(34, 128).T)
    a["dtb_col"] = np.ascontiguousarray(inp["L0_attn_dt_bias"].reshape(NH, 1))
    a["alog_col"] = np.ascontiguousarray(inp["L0_attn_A_log"].reshape(NH, 1))
    a["dp_pm"] = np.ascontiguousarray(
        np.repeat(inp["L0_attn_Dp"], HP).reshape(32, 128).T)
    a["nw_pm"] = np.ascontiguousarray(inp["L0_attn_norm_w"].reshape(32, 128).T)
    a["wg"] = np.ascontiguousarray(inp["L0_ffn_Wg"])
    a["bg_col"] = np.ascontiguousarray(inp["L0_ffn_bg"].reshape(2, 8, 1))
    a["wup"] = np.ascontiguousarray(inp["L0_ffn_Wup"])
    a["bup_pm"] = np.ascontiguousarray(
        inp["L0_ffn_bup"].reshape(NEXP, 8, 128).transpose(0, 2, 1))
    a["wsw"] = np.ascontiguousarray(inp["L0_ffn_Wsw"])
    a["bsw_pm"] = np.ascontiguousarray(
        inp["L0_ffn_bsw"].reshape(NEXP, 8, 128).transpose(0, 2, 1))
    a["wdn"] = np.ascontiguousarray(inp["L0_ffn_Wdn"])
    a["bdn"] = np.ascontiguousarray(inp["L0_ffn_bdn"])
    a["wsu"] = np.ascontiguousarray(inp["L0_ffn_Wsu"])
    a["bsu_pm"] = np.ascontiguousarray(inp["L0_ffn_bsu"].reshape(16, 128).T)
    a["wsa"] = np.ascontiguousarray(inp["L0_ffn_Wsa"])
    a["bsa_pm"] = np.ascontiguousarray(inp["L0_ffn_bsa"].reshape(8, 128).T)
    a["wsd"] = np.ascontiguousarray(inp["L0_ffn_Wsd"])
    a["bsd_pm"] = np.ascontiguousarray(inp["L0_ffn_bsd"].reshape(16, 128).T)
    j = np.arange(128)[:, None]
    i = np.arange(256)[None, :]
    a["m1a"] = (i >= j).astype(np.float32)
    a["m1b"] = (i >= j + 128).astype(np.float32)
    a["u01_bf"] = np.concatenate([(i < j), (i < j + 128)], axis=1).astype(BF)
    a["negdiag_bf"] = (-BIGM * np.eye(128)).astype(BF)
    sel = np.zeros((128, NH, 128), np.float32)
    for h in range(NH):
        sel[h, h, :] = 1.0
        sel[64 + h, h, :] = 1.0
    a["sel_bf"] = sel.reshape(128, NH * 128).astype(BF)
    sel8 = np.zeros((8, 8, 128), np.float32)
    for n in range(8):
        sel8[n, n, :] = 1.0
    a["sel8"] = sel8.reshape(8, 8 * 128)
    a["id_bf"] = np.eye(128).astype(BF)
    a["id_f32"] = np.eye(128, dtype=np.float32)
    a["ones_col_bf"] = np.ones((128, 1), BF)
    a["i4096_col_bf"] = np.full((128, 1), 1.0 / 4096.0, BF)
    a["i2048_col_bf"] = np.full((128, 1), 1.0 / 2048.0, BF)
    a["eps11"] = np.full((1, 1), 1e-5, np.float32)
    a["bg2"] = np.ascontiguousarray(inp["L0_ffn_bg"].T)
    a["ones_col_f32"] = np.ones((128, 1), np.float32)
    a["ones_row_f32"] = np.ones((1, 128), np.float32)
    for k in a:
        if a[k].dtype == np.float64:
            a[k] = a[k].astype(np.float32)
    return a


def kernel(**inputs):
    shared = _host_arrays(inputs)
    u = np.asarray(inputs["L0_attn_u"], np.float32)
    nc = build(debug=False)
    in_maps = []
    for b in range(8):
        m = dict(shared)
        m["u_fm"] = np.ascontiguousarray(u[b].T)
        in_maps.append(m)
    res = run_bass_kernel_spmd(nc, in_maps, core_ids=list(range(8)))
    out = np.empty((8, 2048, 2048), np.float32)
    for b, r in enumerate(res.results):
        out[b] = np.asarray(r["out"]).reshape(2048, 2048).T
    return out


# version 07 (final)
# speedup vs baseline: 1.0574x; 1.0283x over previous
import numpy as np
import ml_dtypes
from contextlib import ExitStack
import concourse.bass as bass
import concourse.mybir as mybir
from concourse.bass_utils import run_bass_kernel_spmd

F32 = mybir.dt.float32
BF16 = mybir.dt.bfloat16
AF = mybir.ActivationFunctionType
ALU = mybir.AluOpType
AX = mybir.AxisListType
BF = ml_dtypes.bfloat16

D = 2048
L = 2048
DIN = 4096
NH = 64
HP = 64
DS = 128
DCONV = 4
CHUNK = 256
NCH = L // CHUNK
CONV_DIM = DIN + 2 * DS
DPROJ = 2 * DIN + 2 * DS + NH
NOC = DPROJ // 128
EPS = 1e-5
BIGM = 32768.0
NEXP = 6
EE = 1024
TFF = 1024
TEST_NO_YOFF = False


class Prog:
    ENG_SELF = {"vector": "sV", "scalar": "sA", "gpsimd": "sG"}
    ENGS = ("tensor", "vector", "scalar", "gpsimd", "sync")

    def __init__(self, nc):
        self.nc = nc
        self.sem = {}
        self.count = {}
        self.waited = {e: {} for e in self.ENGS}
        for s in ("sV", "sA", "sG", "sP", "dS", "dG", "dC", "dCg", "dU",
                  "dUa", "dUb", "dWa", "dWb", "dXa", "dXb", "dZOa", "dZOb",
                  "dGNa", "dGNb", "dOUT"):
            self.sem[s] = nc.alloc_semaphore(s)
            self.count[s] = 0
        self.q = None

    def begin(self):
        self.q = {e: [] for e in self.ENGS}

    def flush(self, name=None):
        with self.nc.Block(name) as blk:
            for e in self.ENGS:
                fns = self.q[e]
                if not fns:
                    continue
                def body(eng, fns=fns):
                    for f in fns:
                        f(eng)
                getattr(blk, e)(body)
        self.q = None

    def op(self, eng, fn, waits=(), inc="SELF", inc_n=1):
        if inc == "SELF":
            inc = self.ENG_SELF.get(eng)
        wl = []
        se = self.ENG_SELF.get(eng)
        if se is not None:
            wl.append((se, self.count[se]))
        for wv in waits:
            if wv is not None:
                wl.append(wv)
        W = self.waited[eng]
        wl2 = []
        for s, v in wl:
            if v > 0 and W.get(s, 0) < v:
                wl2.append((s, v))
                W[s] = v
        ev = None
        if inc:
            self.count[inc] += inc_n
            ev = (inc, self.count[inc])
        sems = self.sem
        def closure(e, wl2=wl2, fn=fn, inc=inc, inc_n=inc_n):
            for s, v in wl2:
                e.wait_ge(sems[s], v)
            inst = fn(e)
            if inc:
                inst.then_inc(sems[inc], inc_n)
        self.q[eng].append(closure)
        return ev

    DMA_SEMS = ("dS", "dG", "dC", "dCg", "dU", "dUa", "dUb", "dWa", "dWb",
                "dXa", "dXb", "dZOa", "dZOb", "dGNa", "dGNb", "dOUT")

    def dma(self, queue, out, in_, waits=(), sem=None):
        semn = sem or ("dS" if queue == "sync" else "dG")
        return self.op(queue, lambda e, o=out, i=in_: e.dma_start(o, i),
                       waits=waits, inc=semn, inc_n=16)

    def dma_snapshot(self):
        return tuple((s, self.count[s]) for s in self.DMA_SEMS
                     if self.count[s] > 0)

    def final_wait(self):
        snap = self.dma_snapshot()
        def closure(e):
            for s, v in snap:
                e.wait_ge(self.sem[s], v)
        self.q["sync"].append(closure)


def build(debug=False):
    nc = bass.Bass("TRN2", target_bir_lowering=False)
    P = Prog(nc)
    t = {}

    def din(name, shape, dt=F32):
        t[name] = nc.dram_tensor(name, list(shape), dt, kind="ExternalInput").ap()

    din("u_fm", [D, L])
    din("w_inT", [D, DPROJ])
    din("w_outT", [DIN, D])
    din("conv_w_pm", [128, 34 * 4])
    din("conv_b_pm", [128, 34])
    din("dtb_col", [NH, 1])
    din("alog_col", [NH, 1])
    din("dp_pm", [128, 32])
    din("nw_pm", [128, 32])
    din("wg", [2, D, 8])
    din("bg_col", [2, 8, 1])
    din("wup", [NEXP, D, EE])
    din("bup_pm", [NEXP, 128, 8])
    din("wsw", [NEXP, EE, EE])
    din("bsw_pm", [NEXP, 128, 8])
    din("wdn", [NEXP, EE, D])
    din("bdn", [NEXP, D])
    din("wsu", [D, 2 * EE])
    din("bsu_pm", [128, 16])
    din("wsa", [EE, EE])
    din("bsa_pm", [128, 8])
    din("wsd", [2 * EE, D])
    din("bsd_pm", [128, 16])
    din("m1a", [128, 256])
    din("m1b", [128, 256])
    din("u01_bf", [128, 512], BF16)
    din("negdiag_bf", [128, 128], BF16)
    din("sel_bf", [128, NH * 128], BF16)
    din("sel8", [8, 8 * 128])
    din("id_bf", [128, 128], BF16)
    din("id_f32", [128, 128])
    din("ones_col_bf", [128, 1], BF16)
    din("i4096_col_bf", [128, 1], BF16)
    din("i2048_col_bf", [128, 1], BF16)
    din("eps11", [1, 1])
    din("bg2", [8, 2])
    din("ones_col_f32", [128, 1])
    din("ones_row_f32", [1, 128])

    skind = "ExternalOutput" if debug else "Internal"
    d_zbf = nc.dram_tensor("d_zbf", [32, 128, L], BF16, kind=skind).ap()
    d_xbc = nc.dram_tensor("d_xbc", [34, 128, L], BF16, kind=skind).ap()
    d_gn = nc.dram_tensor("d_gn", [32, 128, L], BF16, kind=skind).ap()
    d_x1 = nc.dram_tensor("d_x1", [16, 128, L], F32, kind=skind).ap()
    d_x1bf = nc.dram_tensor("d_x1bf", [16, 128, L], BF16, kind=skind).ap()
    d_x2bf = nc.dram_tensor("d_x2bf", [16, 128, L], BF16, kind=skind).ap()
    d_out = nc.dram_tensor("out", [16, 128, L], F32, kind="ExternalOutput").ap()
    if debug:
        d_dt = nc.dram_tensor("d_dt", [NH, L], F32, kind="ExternalOutput").ap()
        d_y = nc.dram_tensor("d_y", [NCH, 128, 32 * 256], F32,
                             kind="ExternalOutput").ap()
        d_dbgE = nc.dram_tensor("d_dbgE", [6, 128, 384], BF16,
                                kind="ExternalOutput").ap()
        d_dbgS = nc.dram_tensor("d_dbgS", [6, 128, 384], BF16,
                                kind="ExternalOutput").ap()
        d_dbgX = nc.dram_tensor("d_dbgX", [3, 128, 256], BF16,
                                kind="ExternalOutput").ap()
        d_dbgP = nc.dram_tensor("d_dbgP", [2, 128, 256], F32,
                                kind="ExternalOutput").ap()
        d_x3 = nc.dram_tensor("d_x3", [16, 128, L], F32,
                              kind="ExternalOutput").ap()
        d_dbgG = nc.dram_tensor("d_dbgG", [2, 8, 1024], F32,
                                kind="ExternalOutput").ap()
        d_dbgWf = nc.dram_tensor("d_dbgWf", [2, 8, 1024], F32,
                                 kind="ExternalOutput").ap()
        d_dbgXt = nc.dram_tensor("d_dbgXt", [16, 128, 1024], BF16,
                                 kind="ExternalOutput").ap()
        d_ms = nc.dram_tensor("d_ms", [NCH, 1, 256], F32,
                              kind="ExternalOutput").ap()

    g = {}
    def gal(name, shape, dt=F32):
        g[name] = nc.alloc_sbuf_tensor("g_" + name, list(shape), dt)
        return g[name]
    for name, shape, dt in [
        ("conv_w_pm", [128, 136], F32), ("conv_b_pm", [128, 34], F32),
        ("dtb_col", [NH, 1], F32), ("alog_col", [NH, 1], F32),
        ("dp_pm", [128, 32], F32), ("nw_pm", [128, 32], F32),
        ("m1a", [128, 256], F32), ("m1b", [128, 256], F32),
        ("u01_bf", [128, 512], BF16),
        ("negdiag_bf", [128, 128], BF16),
        ("id_bf", [128, 128], BF16), ("id_f32", [128, 128], F32),
        ("ones_col_bf", [128, 1], BF16), ("ones_col_f32", [128, 1], F32),
        ("i4096_col_bf", [128, 1], BF16), ("eps11", [1, 1], F32),
        ("i2048_col_bf", [128, 1], BF16), ("bg2", [8, 2], F32),
        ("ones_row_f32", [1, 128], F32),
        ("sel8", [8, 8 * 128], F32),
        ("bg_col", [2 * 8, 1], F32),
        ("bsu_pm", [128, 16], F32), ("bsa_pm", [128, 8], F32),
        ("bsd_pm", [128, 16], F32),
        ("bup_pm", [128, NEXP * 8], F32), ("bsw_pm", [128, NEXP * 8], F32),
        ("wg_bf", [128, 2 * 16 * 8], BF16),
        ("acol", [NH, 1], F32),
    ]:
        gal(name, shape, dt)

    P.begin()
    for name in ("conv_w_pm", "conv_b_pm", "dtb_col", "alog_col", "dp_pm",
                 "nw_pm", "m1a", "m1b", "u01_bf", "negdiag_bf",
                 "id_bf", "id_f32", "ones_col_bf", "ones_col_f32",
                 "i4096_col_bf", "eps11", "i2048_col_bf", "bg2",
                 "ones_row_f32", "sel8", "bsu_pm", "bsa_pm", "bsd_pm"):
        P.dma("sync", g[name][:], t[name], sem="dC")
    P.dma("sync", g["bg_col"][:], t["bg_col"].rearrange("r n one -> (r n) one"),
          sem="dC")
    for n in range(NEXP):
        P.dma("sync", g["bup_pm"][:, n * 8:(n + 1) * 8], t["bup_pm"][n], sem="dC")
        P.dma("sync", g["bsw_pm"][:, n * 8:(n + 1) * 8], t["bsw_pm"][n], sem="dC")
    for r in range(2):
        for k in range(16):
            P.dma("gpsimd", g["wg_bf"][:, (r * 16 + k) * 8:(r * 16 + k + 1) * 8],
                  t["wg"][r, k * 128:(k + 1) * 128, :], sem="dCg")
    P.flush("p0")
    ev_const = ("dC", P.count["dC"])
    ev_constg = ("dCg", P.count["dCg"])

    _mix_stack = ExitStack()
    g["dt_sb"] = _mix_stack.enter_context(
        nc.sbuf_tensor("g_dt_sb", [NH, L], F32))

    with ExitStack() as st:
        def al(name, shape, dt=F32):
            return st.enter_context(nc.sbuf_tensor(name, list(shape), dt))
        ubuf = al("ubuf", [128, 16 * L], BF16)
        wbuf = [al(f"wbuf{i}", [128, 16 * 512], BF16) for i in range(2)]
        ebuf = [al(f"ebuf{i}", [128, 2051], F32) for i in range(2)]
        xsb = [al(f"xsb{i}", [128, L], F32) for i in range(2)]
        stg = [al(f"stg{i}", [128, L], BF16) for i in range(2)]
        sp1 = al("sp1", [NH, L], F32)
        sp2 = al("sp2", [NH, L], F32)
        pA = st.enter_context(nc.psum_tensor("pA", [128, L], F32))
        pB = st.enter_context(nc.psum_tensor("pB", [128, L], F32))

        P.begin()
        ev_u = None
        for k in range(16):
            ev_u = P.dma("gpsimd", ubuf[:, k * L:(k + 1) * L],
                         t["u_fm"][k * 128:(k + 1) * 128, :], waits=(ev_constg,),
                         sem="dU")
        ev_pad = [P.op("gpsimd", lambda e, i=i: e.memset(ebuf[i][:, 0:3], 0.0))
                  for i in range(2)]

        ogs = []
        o = 0
        while o < 66:
            ogs.append(list(range(o, min(o + 4, 66))))
            o += 4
        ogs.append([66])

        ev_wdma = {}
        ev_pe_og = {}
        ev_evac = {}
        ev_outd = {}
        ev_conv = {}
        ev_silu = {}

        for gi, og in enumerate(ogs):
            c0 = og[0] * 128
            c1 = min(og[-1] * 128 + 128, DPROJ)
            wcols = c1 - c0
            wb = wbuf[gi % 2]
            wdep = [ev_constg]
            if gi >= 2:
                wdep.append(ev_pe_og.get(gi - 2))
            ev = None
            for k in range(16):
                ev = P.dma("gpsimd", wb[:, k * 512:k * 512 + wcols],
                           t["w_inT"][k * 128:(k + 1) * 128, c0:c1],
                           waits=wdep, sem="dWa" if gi % 2 == 0 else "dWb")
            ev_wdma[gi] = ev

            for o in og:
                rows = 64 if o == 66 else 128
                ps = pA if (o % 2 == 0) else pB
                off = o * 128 - c0
                mmdep = [ev_wdma[gi], ev_u]
                if o >= 2:
                    mmdep.append(ev_evac.get(o - 2))
                evp = None
                for k in range(16):
                    for tt_ in range(4):
                        evp = P.op(
                            "tensor",
                            lambda e, ps=ps, rows=rows, k=k, tt_=tt_, wb=wb, off=off: e.matmul(
                                ps[0:rows, tt_ * 512:(tt_ + 1) * 512],
                                wb[:, k * 512 + off:k * 512 + off + rows],
                                ubuf[:, k * L + tt_ * 512:k * L + (tt_ + 1) * 512],
                                start=(k == 0), stop=(k == 15)),
                            waits=mmdep if (k == 0 and tt_ == 0) else (),
                            inc="sP" if (k == 15 and tt_ == 3) else None)
                ev_pe_og[gi] = evp

                par = o % 2
                if o < 32:
                    eva = P.op("scalar",
                               lambda e, ps=ps, par=par: e.activation(
                                   stg[par][:], ps[:], AF.Silu),
                               waits=(evp, ev_outd.get(par)))
                    ev_evac[o] = eva
                    ev_outd[par] = P.dma("sync", d_zbf[o], stg[par][:],
                                         waits=(eva,),
                                         sem="dZOa" if par == 0 else "dZOb")
                elif o < 66:
                    cx = o - 32
                    eva = P.op("scalar",
                               lambda e, ps=ps, par=par: e.activation(
                                   ebuf[par][:, 3:2051], ps[:], AF.Copy),
                               waits=(evp, ev_evac.get(o - 2), ev_pad[par],
                                      ev_conv.get(o - 2)))
                    ev_evac[o] = eva
                    e0 = P.op("vector",
                              lambda e, par=par, cx=cx: e.tensor_scalar(
                                  xsb[par][:], ebuf[par][:, 0:2048],
                                  g["conv_w_pm"][:, cx * 4:cx * 4 + 1], None,
                                  ALU.mult),
                              waits=(eva, ev_silu.get(o - 2)))
                    for kk in range(1, 4):
                        e0 = P.op("vector",
                                  lambda e, par=par, cx=cx, kk=kk: e.scalar_tensor_tensor(
                                      xsb[par][:], ebuf[par][:, kk:kk + 2048],
                                      g["conv_w_pm"][:, cx * 4 + kk:cx * 4 + kk + 1],
                                      xsb[par][:], ALU.mult, ALU.add))
                    ev_conv[o] = e0
                    eva2 = P.op("scalar",
                                lambda e, par=par, cx=cx: e.activation(
                                    stg[par][:], xsb[par][:], AF.Silu,
                                    bias=g["conv_b_pm"][:, cx:cx + 1]),
                                waits=(e0, ev_outd.get(par)))
                    ev_silu[o] = eva2
                    ev_outd[par] = P.dma("sync", d_xbc[cx], stg[par][:],
                                         waits=(eva2,),
                                         sem="dZOa" if par == 0 else "dZOb")
                else:
                    e1 = P.op("scalar",
                              lambda e, ps=ps: e.activation(
                                  sp1[:], ps[0:64, :], AF.Abs,
                                  bias=g["dtb_col"][:, 0:1]),
                              waits=(evp,))
                    e2 = P.op("scalar",
                              lambda e: e.activation(sp2[:], sp1[:], AF.Exp,
                                                     scale=-1.0))
                    e3 = P.op("scalar",
                              lambda e: e.activation(sp1[:], sp2[:], AF.Ln,
                                                     bias=1.0))
                    e4 = P.op("scalar",
                              lambda e, ps=ps: e.activation(
                                  sp2[:], ps[0:64, :], AF.Relu,
                                  bias=g["dtb_col"][:, 0:1]))
                    ev_evac[o] = e4
                    e5 = P.op("vector",
                              lambda e: e.tensor_tensor(g["dt_sb"][:], sp1[:],
                                                        sp2[:], ALU.add),
                              waits=(e3, e4))
                    e6 = P.op("scalar",
                              lambda e: e.activation(sp1[:, 0:1],
                                                     g["alog_col"][:, 0:1],
                                                     AF.Exp),
                              waits=(ev_const,))
                    e7 = P.op("vector",
                              lambda e: e.tensor_scalar(g["acol"][:, 0:1],
                                                        sp1[:, 0:1], -1.0,
                                                        None, ALU.mult),
                              waits=(e6,))
                    if debug:
                        P.dma("sync", d_dt, g["dt_sb"][:], waits=(e5,))
        P.final_wait() if debug else None
        P.flush("p1")
    with ExitStack() as st:
        def al(name, shape, dt=F32):
            return st.enter_context(nc.sbuf_tensor(name, list(shape), dt))
        xbc_sb = [al(f"xbc{i}", [128, 34 * 256], BF16) for i in range(2)]
        z_sb = [al(f"zsb{i}", [128, 32 * 256], BF16) for i in range(2)]
        y_sb = al("y_sb", [128, 32 * 256], F32)
        gn_sb = al("gn_sb", [128, 32 * 256], BF16)
        sq_sb = al("sq_sb", [128, 32 * 256], BF16)
        h_f = al("h_f", [128, NH * HP], F32)
        h_b = al("h_b", [128, NH * HP], BF16)
        sel_sb = al("sel_sb", [128, NH * 128], BF16)
        dA_f = al("dA_f", [NH, 256], F32)
        dA_t = al("dA_t", [128, 128], F32)
        cs_fm = al("cs_fm", [NH, 256], F32)
        cs_t = al("cs_t", [128, 128], F32)

        hilo = al("hilo", [128, 256], BF16)
        b2 = al("b2", [128, 128], F32)
        ldt = al("ldt", [128, 128], F32)
        ecs_bf = al("ecs_bf", [NH, 256], BF16)
        wt1 = al("wt1", [NH, 256], F32)
        wt2 = al("wt2", [NH, 256], F32)
        w_t = al("w_t", [128, 128], F32)
        dt_t = al("dt_t", [128, 128], F32)
        cd_col = al("cd_col", [NH, 1], F32)
        cd_row = al("cd_row", [1, NH], F32)
        cd_all = al("cd_all", [128, NH], F32)

        cbt_bf = al("cbt_bf", [128, 384], BF16)
        bt_bf = al("bt_bf", [128, 256], BF16)
        xt_bf = [al(f"xt_bf{i}", [128, 256], BF16) for i in range(2)]
        E_bf = [al(f"E_bf{i}", [128, 384], BF16) for i in range(2)]
        ce_bf = [al(f"ce_bf{i}", [128, 256], BF16) for i in range(2)]
        sc_bf = [al(f"sc_bf{i}", [128, 384], BF16) for i in range(2)]
        xw_bf = [al(f"xw_bf{i}", [128, 128], BF16) for i in range(2)]
        ms_row = al("ms_row", [1, 256], F32)
        zero_bf = al("zero_bf", [128, 256], BF16)
        dbgP_sb = al("dbgP_sb", [128, 256], F32)
        rstd_row = al("rstd_row", [1, 256], F32)

        pseg = [st.enter_context(nc.psum_tensor(f"pseg{i}", [128, 512], F32))
                for i in range(2)]
        pyp_ = [st.enter_context(nc.psum_tensor(f"pyp{i}", [128, 512], F32))
                for i in range(2)]
        ptr0 = st.enter_context(nc.psum_tensor("ptr0", [128, 1024], BF16))
        ptr = [ptr0, ptr0]
        pcum = st.enter_context(nc.psum_tensor("pcum", [128, 512], F32))
        pcbt = st.enter_context(nc.psum_tensor("pcbt", [128, 512], F32))
        pce = st.enter_context(nc.psum_tensor("pce", [128, 512], F32))

        P.begin()
        P.op("vector", lambda e: e.memset(h_f[:], 0.0))
        P.op("vector", lambda e: e.memset(zero_bf[:], 0.0))
        ev_hb0 = P.op("gpsimd", lambda e: e.memset(h_b[:], 0.0))
        for hh in range(NH):
            P.dma("sync", sel_sb[:, hh * 128:(hh + 1) * 128],
                  t["sel_bf"][:, hh * 128:(hh + 1) * 128], sem="dC")

        M = {}
        ev_gn_prev = [None]
        snap = lambda: {s_: P.count[s_] for s_ in ("sP", "sV", "sA", "sG", "dS", "dG")}
        M[-1] = M[-2] = {s_: 0 for s_ in ("sP", "sV", "sA", "sG", "dS", "dG")}
        ev_gn_dma = {}

        def emit_chunk(c):
            par_c = c % 2
            xb = xbc_sb[par_c]
            zb = z_sb[par_c]
            csl = slice(c * 256, (c + 1) * 256)
            ddep = (("sP", M[c - 1]["sP"]), ("sV", M[c - 1]["sV"]),
                    ("sA", M[c - 1]["sA"]))
            if c == 0:
                ddep = ddep + (("dZOa", P.count["dZOa"]),
                               ("dZOb", P.count["dZOb"]))
            dxs = "dXa" if par_c == 0 else "dXb"
            ev_x = P.dma("sync", xb[:].rearrange("p (t j) -> p t j", j=256),
                         d_xbc[:, :, csl].rearrange("t p j -> p t j"), waits=ddep,
                         sem=dxs)
            ev_z = P.dma("sync", zb[:].rearrange("p (t j) -> p t j", j=256),
                         d_zbf[:, :, csl].rearrange("t p j -> p t j"), waits=ddep,
                         sem=dxs)

            adep = (("sP", M[c - 1]["sP"]), ("sV", M[c - 1]["sV"]),
                    ("sA", M[c - 1]["sA"]), ("sG", M[c - 1]["sG"]))
            e_dA = P.op("vector", lambda e: e.tensor_scalar(
                dA_f[:], g["dt_sb"][:, csl], g["acol"][:, 0:1], None, ALU.mult),
                waits=adep)
            e_trA = P.op("tensor", lambda e: e.transpose(
                pcum[0:128, 0:64], dA_f[:, 0:128], g["id_f32"][0:64, 0:64]),
                waits=adep + (e_dA,), inc="sP")
            e_trA2 = P.op("tensor", lambda e: e.transpose(
                pcum[0:128, 64:128], dA_f[:, 128:256], g["id_f32"][0:64, 0:64]),
                inc="sP")
            e_vA = P.op("vector", lambda e: e.tensor_copy(dA_t[:], pcum[:, 0:128]),
                        waits=(e_trA2,))
            e_trD = P.op("tensor", lambda e: e.transpose(
                pcum[0:128, 128:192], g["dt_sb"][:, csl.start:csl.start + 128],
                g["id_f32"][0:64, 0:64]), waits=(e_vA,), inc="sP")
            e_trD2 = P.op("tensor", lambda e: e.transpose(
                pcum[0:128, 192:256], g["dt_sb"][:, csl.start + 128:csl.stop],
                g["id_f32"][0:64, 0:64]), inc="sP")
            e_vD = P.op("vector", lambda e: e.tensor_copy(dt_t[:], pcum[:, 128:256]),
                        waits=(e_trD2,))
            e_cf1 = P.op("tensor", lambda e: e.matmul(
                pcum[0:64, 256:512], dA_t[:, 0:64], g["m1a"][:], start=True,
                stop=False), waits=(e_vD,), inc="sP")
            e_cf2 = P.op("tensor", lambda e: e.matmul(
                pcum[0:64, 256:512], dA_t[:, 64:128], g["m1b"][:], start=False,
                stop=True), inc="sP")
            e_csfm = P.op("vector", lambda e: e.tensor_copy(
                cs_fm[:], pcum[0:64, 256:512]), waits=(e_cf2,))
            e_ct1 = P.op("tensor", lambda e: e.matmul(
                pcum[0:128, 0:64], g["m1a"][:, 0:128], dA_t[:, 0:64],
                start=True, stop=True), waits=(e_csfm,), inc="sP")
            e_ct2 = P.op("tensor", lambda e: e.matmul(
                pcum[0:128, 64:128], g["m1a"][:, 128:256], dA_t[:, 0:64],
                start=True, stop=False), inc="sP")
            e_ct3 = P.op("tensor", lambda e: e.matmul(
                pcum[0:128, 64:128], g["m1b"][:, 128:256], dA_t[:, 64:128],
                start=False, stop=True), inc="sP")
            e_cst = P.op("vector", lambda e: e.tensor_copy(cs_t[:], pcum[:, 0:128]),
                         waits=(e_ct3,))
            e_ldt = P.op("scalar", lambda e: e.activation(
                ldt[:], dt_t[:], AF.Ln), waits=(e_vD,))
            e_b2 = P.op("vector", lambda e: e.tensor_tensor(
                b2[:], ldt[:], cs_t[:], ALU.subtract), waits=(e_ldt, e_cst))
            e_hi = P.op("scalar", lambda e: e.activation(
                hilo[0:64, :], cs_fm[:], AF.Copy), waits=(e_csfm,))
            e_lo = P.op("vector", lambda e: e.tensor_tensor(
                hilo[64:128, :], cs_fm[:], hilo[0:64, :], ALU.subtract),
                waits=(e_hi,))
            e_ecs = P.op("scalar", lambda e: e.activation(
                ecs_bf[:], cs_fm[:], AF.Exp))
            e_w1 = P.op("vector", lambda e: e.tensor_scalar(
                wt1[:], cs_fm[:], cs_fm[:, 255:256], None, ALU.subtract))
            e_w2 = P.op("scalar", lambda e: e.activation(
                wt2[:], wt1[:], AF.Exp, scale=-1.0), waits=(e_w1,))
            e_w3 = P.op("vector", lambda e: e.tensor_tensor(
                wt1[:], wt2[:], g["dt_sb"][:, csl], ALU.mult), waits=(e_w2,))
            e_tw1 = P.op("tensor", lambda e: e.transpose(
                pcum[0:128, 128:192], wt1[:, 0:128], g["id_f32"][0:64, 0:64]),
                waits=(e_w3, e_cst), inc="sP")
            e_tw2 = P.op("tensor", lambda e: e.transpose(
                pcum[0:128, 192:256], wt1[:, 128:256], g["id_f32"][0:64, 0:64]),
                inc="sP")
            e_vw = P.op("vector", lambda e: e.tensor_copy(w_t[:], pcum[:, 128:256]),
                        waits=(e_tw2,))
            e_cd1 = P.op("scalar", lambda e: e.activation(
                cd_col[:, 0:1], cs_fm[:, 255:256], AF.Exp))
            e_cd2 = P.op("tensor", lambda e: e.transpose(
                pcum[0:1, 256:320], cd_col[:, 0:1], g["id_f32"][0:64, 0:64]),
                waits=(("sA", e_cd1[1]), e_vw), inc="sP")
            e_cd3 = P.op("vector", lambda e: e.tensor_copy(
                cd_row[:], pcum[0:1, 256:320]), waits=(e_cd2,))
            e_cd4 = P.op("tensor", lambda e: e.matmul(
                pcum[0:128, 320:384], g["ones_row_f32"][:], cd_row[:],
                start=True, stop=True), waits=(e_cd3,), inc="sP")
            e_cd5 = P.op("vector", lambda e: e.tensor_copy(
                cd_all[:], pcum[:, 320:384]), waits=(e_cd4,))
            e_cb1 = P.op("tensor", lambda e: e.matmul(
                pcbt[:, 0:256], xb[:, 32 * 256:32 * 256 + 128],
                xb[:, 33 * 256:34 * 256], start=True, stop=True),
                waits=(ev_x,), inc="sP")
            e_cb2 = P.op("tensor", lambda e: e.matmul(
                pcbt[:, 256:512], xb[:, 32 * 256 + 128:33 * 256],
                xb[:, 33 * 256:34 * 256], start=True, stop=True), inc="sP")
            e_cbt0 = P.op("scalar", lambda e: e.activation(
                cbt_bf[:, 0:256], pcbt[:, 0:256], AF.Copy), waits=(e_cb2,))
            e_cbt = P.op("scalar", lambda e: e.activation(
                cbt_bf[:, 256:384], pcbt[:, 384:512], AF.Copy))
            e_bt1 = P.op("tensor", lambda e: e.transpose(
                ptr[0][:, 0:128], xb[:, 32 * 256:32 * 256 + 128], g["id_bf"][:]),
                waits=(ev_x,), inc="sP")
            e_bt2 = P.op("tensor", lambda e: e.transpose(
                ptr[0][:, 128:256], xb[:, 32 * 256 + 128:33 * 256], g["id_bf"][:]),
                inc="sP")
            e_btv = P.op("scalar", lambda e: e.activation(
                bt_bf[:], ptr[0][:, 0:256], AF.Copy), waits=(e_bt2,))

            dbg_ev = {}
            ev_gn_last = [ev_gn_prev[0]]
            ev_xw_d = {}
            ev_seg = {}
            ev_brcE = {}
            ev_E = {}
            ev_sc = {}
            ev_ce = {}
            ev_xw = {}
            ev_ydiag = {}
            ev_yoff = {}
            ev_states = {}
            ev_hupd = {}
            ev_hbf = {}
            ev_xtv = {}
            ev_evac = {}
            ev_g = {}
            ev_sq = {}

            def emit_xtr(f):
                pb = ptr[f % 2]
                reg = 256 if f % 2 == 0 else 512
                w = [ev_x, ev_xtv.get(f - 1), ev_xtv.get(f - 2)]
                if f <= 1:
                    w.append(e_btv)
                e1 = P.op("tensor", lambda e, reg=reg: e.transpose(
                    pb[:, reg:reg + 128], xb[:, f * 256:f * 256 + 128],
                    g["id_bf"][:]), waits=w, inc="sP")
                e2 = P.op("tensor", lambda e, reg=reg: e.transpose(
                    pb[:, reg + 128:reg + 256],
                    xb[:, f * 256 + 128:(f + 1) * 256],
                    g["id_bf"][:]), inc="sP")
                ev_xtv[f] = P.op("scalar", lambda e, reg=reg: e.activation(
                    xt_bf[f % 2][:], pb[:, reg:reg + 256], AF.Copy),
                    waits=(e2, ev_ydiag.get(2 * (f - 2) + 1),
                           ev_xw.get(2 * (f - 2) + 1),
                           ev_xw_d.get(2 * (f - 2) + 1)))
                if debug and c == 0 and f < 3:
                    dbg_ev[("X", f)] = P.dma("sync", d_dbgX[f], xt_bf[f % 2][:],
                                             waits=(ev_xtv[f],))

            def emit_seg(h):
                par = h % 2
                ps = pseg[par]
                w = [e_lo, ev_E.get(h - 2)]
                rhs2 = hilo[:].rearrange("p (r c) -> p r c", r=1).broadcast_to(
                    [128, 2, 256])
                P.op("tensor", lambda e, rhs2=rhs2: e.matmul(
                    ps[:, 0:512], sel_sb[:, h * 128:(h + 1) * 128], rhs2,
                    start=True, stop=False), waits=w, inc=None)
                ev_seg[h] = P.op("tensor", lambda e: e.matmul(
                    ps[:, 0:512], g["negdiag_bf"][:], g["u01_bf"][:],
                    start=False, stop=True), inc="sP")

            def emit_brcE(h):
                par = h % 2
                pc = pce if par == 0 else pcum
                w = [e_ecs, ev_ce.get(h - 2), ev_hupd.get(h - 2)]
                if par == 1:
                    w.append(e_cd5)
                ev_brcE[h] = P.op("tensor", lambda e, pc=pc: e.matmul(
                    pc[:, 0:256], sel_sb[0:64, h * 128:(h + 1) * 128], ecs_bf[:],
                    start=True, stop=True), waits=w, inc="sP")

            def emit_E(h):
                par = h % 2
                ps = pseg[par]
                w = [ev_seg[h], ev_sc.get(h - 2), e_b2,
                     dbg_ev.get(("E", h - 2))]
                P.op("scalar", lambda e: e.activation(
                    E_bf[par][:, 0:256], ps[:, 0:256], AF.Exp,
                    bias=b2[:, h:h + 1]), waits=w)
                ev_E[h] = P.op("scalar", lambda e: e.activation(
                    E_bf[par][:, 256:384], ps[:, 384:512], AF.Exp,
                    bias=b2[:, 64 + h:64 + h + 1]))
                if debug and c == 0 and h < 6:
                    dbg_ev[("E", h)] = P.dma("sync", d_dbgE[h], E_bf[par][:],
                                             waits=(ev_E[h],))

            def emit_ce(h):
                par = h % 2
                pc = pce if par == 0 else pcum
                w = [ev_brcE[h], ev_yoff.get(h - 2)]
                ev_ce[h] = P.op("vector", lambda e, pc=pc: e.tensor_tensor(
                    ce_bf[par][:], xb[:, 33 * 256:34 * 256], pc[:, 0:256],
                    ALU.mult), waits=w)

            def emit_scores(h):
                par = h % 2
                w = [ev_E[h], e_cbt, ev_ydiag.get(h - 2),
                     dbg_ev.get(("S", h - 2))]
                ev_sc[h] = P.op("vector", lambda e: e.tensor_tensor(
                    sc_bf[par][:, 0:384], E_bf[par][:, 0:384],
                    cbt_bf[:, 0:384], ALU.mult), waits=w)
                if debug and c == 0 and h < 6:
                    dbg_ev[("S", h)] = P.dma("sync", d_dbgS[h], sc_bf[par][:],
                                             waits=(ev_sc[h],))

            def emit_xw(h):
                par = h % 2
                f = h // 2
                s = (h % 2) * 64
                w = [ev_xtv.get(f), e_vw, ev_states.get(h - 2)]
                ev_xw_d[h] = P.op("vector", lambda e: e.tensor_scalar(
                    xw_bf[par][:, 0:64], xt_bf[f % 2][:, s:s + 64],
                    w_t[:, h:h + 1], None, ALU.mult), waits=w)
                ev_xw[h] = P.op("gpsimd", lambda e: e.tensor_scalar(
                    xw_bf[par][:, 64:128], xt_bf[f % 2][:, 128 + s:128 + s + 64],
                    w_t[:, 64 + h:64 + h + 1], None, ALU.mult), waits=w)

            def emit_compute(h):
                par = h % 2
                scpar = h % 2
                f = h // 2
                s = (h % 2) * 64
                yp = pyp_[f % 2]
                w = [ev_sc[h], ev_xtv.get(f)]
                if h % 2 == 0:
                    w.append(ev_evac.get(f - 2))
                    if f < 2:
                        w.append(ev_gn_last[0])
                P.op("tensor", lambda e, yp=yp: e.matmul(
                    yp[s:s + 64, 0:256], xt_bf[f % 2][:, s:s + 64],
                    sc_bf[scpar][:, 0:256], start=True, stop=False), waits=w,
                    inc=None)
                ev_ydiag[h] = P.op("tensor", lambda e, yp=yp: e.matmul(
                    yp[s:s + 64, 128:256], xt_bf[f % 2][:, 128 + s:128 + s + 64],
                    sc_bf[scpar][:, 256:384], start=False, stop=False), inc="sP")
                ev_yoff[h] = P.op("tensor", lambda e, yp=yp: e.matmul(
                    yp[s:s + 64, 0:256], h_b[:, h * 64:(h + 1) * 64],
                    ce_bf[par][:], start=False, stop=(h % 2 == 1)),
                    waits=(ev_ce[h], ev_hb0), inc="sP") if not TEST_NO_YOFF else P.op(
                    "tensor", lambda e, tp=tp: e.matmul(
                    pyp[s:s + 64, 0:256], g["id_bf"][:, 0:64], zero_bf[:],
                    start=False, stop=(h % 2 == 1), tile_position=tp),
                    waits=(ev_ce[h],), inc="sP")
                pc = pce if par == 0 else pcum
                w3 = [ev_xw[h], ev_xw_d[h], e_btv, ev_ce[h], ev_hupd.get(h - 2)]
                P.op("tensor", lambda e, pc=pc: e.matmul(
                    pc[:, 256:320], bt_bf[:, 0:128], xw_bf[par][:, 0:64],
                    start=False, stop=False), waits=w3, inc=None)
                ev_states[h] = P.op("tensor", lambda e, pc=pc: e.matmul(
                    pc[:, 256:320], bt_bf[:, 128:256], xw_bf[par][:, 64:128],
                    start=False, stop=True), inc="sP")

            def emit_hupd(h):
                par = h % 2
                pc = pce if par == 0 else pcum
                w = [ev_states[h], ev_yoff[h], e_cd5]
                ev_hupd[h] = P.op("vector", lambda e, pc=pc: e.scalar_tensor_tensor(
                    h_f[:, h * 64:(h + 1) * 64], h_f[:, h * 64:(h + 1) * 64],
                    cd_all[:, h:h + 1], pc[:, 256:320], ALU.mult, ALU.add),
                    waits=w)
                ev_hbf[h] = P.op("gpsimd", lambda e: e.tensor_copy(
                    h_b[:, h * 64:(h + 1) * 64], h_f[:, h * 64:(h + 1) * 64]),
                    waits=(ev_hupd[h],))

            def emit_evac(f):
                if debug and c == 0 and f < 2:
                    epv = P.op("scalar", lambda e: e.activation(
                        dbgP_sb[:], pyp[:, 0:256], AF.Copy),
                        waits=(ev_yoff[2 * f + 1],))
                    dbg_ev[("P", f)] = P.dma("sync", d_dbgP[f], dbgP_sb[:],
                                             waits=(epv,))
                w = [ev_yoff[2 * f + 1], ("sV", M[c - 1]["sV"]),
                     ("sA", M[c - 1]["sA"]), ev_x, dbg_ev.get(("P", f))]
                ev_evac[f] = P.op("vector", lambda e: e.scalar_tensor_tensor(
                    y_sb[:, f * 256:(f + 1) * 256], xb[:, f * 256:(f + 1) * 256],
                    g["dp_pm"][:, f:f + 1], pyp_[f % 2][:, 0:256],
                    ALU.mult, ALU.add), waits=w)
                ev_g[f] = P.op("vector", lambda e: e.tensor_tensor(
                    y_sb[:, f * 256:(f + 1) * 256], y_sb[:, f * 256:(f + 1) * 256],
                    zb[:, f * 256:(f + 1) * 256], ALU.mult), waits=(ev_z,))
                ev_sq[f] = P.op("scalar", lambda e: e.activation(
                    sq_sb[:, f * 256:(f + 1) * 256], y_sb[:, f * 256:(f + 1) * 256],
                    AF.Square), waits=(ev_g[f],))

            emit_xtr(0)
            emit_xtr(1)
            for h in range(NH + 1):
                if h < NH:
                    emit_seg(h)
                    emit_brcE(h)
                    emit_E(h)
                    emit_ce(h)
                    emit_scores(h)
                    emit_xw(h)
                if h >= 1:
                    hh = h - 1
                    emit_compute(hh)
                    emit_hupd(hh)
                    if hh % 2 == 1:
                        emit_evac(hh // 2)
                        if (hh // 2 + 2) <= 31:
                            emit_xtr(hh // 2 + 2)

            e_st = None
            for f in range(32):
                e_st = P.op("tensor", lambda e, f=f: e.matmul(
                    pcbt[0:1, 0:256], g["i4096_col_bf"][:],
                    sq_sb[:, f * 256:(f + 1) * 256],
                    start=(f == 0), stop=(f == 31)),
                    waits=(ev_sq[f],), inc="sP" if f == 31 else None)
            e_ln = P.op("scalar", lambda e: e.activation(
                ms_row[:], pcbt[0:1, 0:256], AF.Ln, bias=g["eps11"][0:1, 0:1]),
                waits=(e_st,))
            e_rstd = P.op("scalar", lambda e: e.activation(
                rstd_row[:], ms_row[:], AF.Exp, scale=-0.5))
            e_rs = P.op("tensor", lambda e: e.matmul(
                pcbt[:, 256:512], g["ones_row_f32"][:], rstd_row[:],
                start=True, stop=True), waits=(e_rstd,), inc="sP")
            evdy = None
            if debug:
                evdy = P.dma("sync", d_y[c], y_sb[:],
                             waits=tuple(ev_g[f] for f in range(32)))
                P.dma("sync", d_ms[c], ms_row[:], waits=(("sA", e_rstd[1]),))
            e_gnl = None
            for f in range(32):
                e_gnl = P.op("vector", lambda e, f=f: e.scalar_tensor_tensor(
                    gn_sb[:, f * 256:(f + 1) * 256], y_sb[:, f * 256:(f + 1) * 256],
                    g["nw_pm"][:, f:f + 1], pcbt[:, 256:512], ALU.mult, ALU.mult),
                    waits=(e_rs, ev_g[f], ev_gn_dma.get(c - 1)))
            ev_gn_prev[0] = e_gnl
            ev_gn_dma[c] = P.dma("sync",
                                 d_gn[:, :, csl].rearrange("t p j -> p t j"),
                                 gn_sb[:].rearrange("p (t j) -> p t j", j=256),
                                 waits=(e_gnl,),
                                 sem="dGNa" if par_c == 0 else "dGNb")
            M[c] = snap()

        for c in range(NCH):
            emit_chunk(c)
        if debug:
            P.final_wait()
        P.flush("p2a")
    _mix_stack.close()

    with ExitStack() as st:
        def al(name, shape, dt=F32):
            return st.enter_context(nc.sbuf_tensor(name, list(shape), dt))
        gn_t = al("gn_t", [128, 32 * 1024], BF16)
        wo_bf = [al(f"wo_bf{i}", [128, 32 * 128], BF16) for i in range(2)]
        u_st = [al(f"u_st{i}", [128, 1024], F32) for i in range(2)]
        r_sb = al("r_sb", [128, 16 * 1024], F32)
        sqb = al("sqb", [128, 1024], BF16)
        murow = al("murow", [1, 1024], F32)
        rstd2 = al("rstd2", [1, 1024], F32)
        mubc = al("mubc", [128, 1024], F32)
        rsbc = mubc
        xst = [al(f"xst{i}", [128, 1024], F32) for i in range(2)]
        pd = [st.enter_context(nc.psum_tensor(f"pd{i}", [128, 1024], F32))
              for i in range(2)]
        pms2 = st.enter_context(nc.psum_tensor("pms2", [128, 1024], F32))

        P.begin()
        for tt_ in range(2):
            tsl = slice(tt_ * 1024, (tt_ + 1) * 1024)
            tdep = (("sP", P.count["sP"]), ("sV", P.count["sV"]),
                    ("sA", P.count["sA"]), ("sG", P.count["sG"])
                    ) + P.dma_snapshot()
            ev_gn = P.dma("sync", gn_t[:].rearrange("p (f j) -> p f j", j=1024),
                          d_gn[:, :, tsl].rearrange("f p j -> p f j"), waits=tdep,
                          sem="dXa" if tt_ == 0 else "dXb")
            ev_wo = {}
            ev_wo[0] = P.dma("gpsimd",
                             wo_bf[0][:].rearrange("p (f j) -> p f j", j=128),
                             t["w_outT"][:, 0:128].rearrange("(f p) c -> p f c", p=128),
                             waits=tdep, sem="dWa")
            ev_u = {}
            ev_mm = {}
            ev_r = {}
            ev_stat = {}
            for dp in range(8):
                for dd in (2 * dp, 2 * dp + 1):
                    if dd + 1 < 16:
                        ev_wo[dd + 1] = P.dma(
                            "gpsimd",
                            wo_bf[(dd + 1) % 2][:].rearrange("p (f j) -> p f j", j=128),
                            t["w_outT"][:, (dd + 1) * 128:(dd + 2) * 128].rearrange(
                                "(f p) c -> p f c", p=128),
                            waits=(ev_mm.get(dd - 1),),
                            sem="dWa" if (dd + 1) % 2 == 0 else "dWb")
                    ev_u[dd] = P.dma("sync", u_st[dd % 2][:],
                                     t["u_fm"][dd * 128:(dd + 1) * 128, tsl],
                                     waits=(ev_r.get(dd - 2),),
                                     sem="dUa" if dd % 2 == 0 else "dUb")
                    ps = pd[dd % 2]
                    evm = None
                    for fi in range(32):
                        for nn in range(2):
                            evm = P.op(
                                "tensor",
                                lambda e, ps=ps, dd=dd, fi=fi, nn=nn: e.matmul(
                                    ps[:, nn * 512:(nn + 1) * 512],
                                    wo_bf[dd % 2][:, fi * 128:(fi + 1) * 128],
                                    gn_t[:, fi * 1024 + nn * 512:
                                         fi * 1024 + (nn + 1) * 512],
                                    start=(fi == 0), stop=(fi == 31)),
                                waits=(ev_wo[dd], ev_gn, ev_r.get(dd - 2))
                                if fi == 0 and nn == 0 else (),
                                inc="sP" if (fi == 31 and nn == 1) else None)
                    ev_mm[dd] = evm
                    ev_r[dd] = P.op("vector", lambda e, ps=ps, dd=dd: e.tensor_tensor(
                        r_sb[:, dd * 1024:(dd + 1) * 1024], ps[:, 0:1024],
                        u_st[dd % 2][:], ALU.add), waits=(evm, ev_u[dd]))
                    for nn in range(2):
                        ev_stat[dd] = P.op(
                            "tensor",
                            lambda e, dd=dd, nn=nn: e.matmul(
                                pms2[0:1, nn * 512:(nn + 1) * 512],
                                g["ones_col_f32"][:],
                                r_sb[:, dd * 1024 + nn * 512:dd * 1024 + (nn + 1) * 512],
                                start=(dd == 0), stop=(dd == 15)),
                            waits=(ev_r[dd],) if nn == 0 else (),
                            inc="sP" if (dd == 15 and nn == 1) else None)
            e_mu = P.op("vector", lambda e: e.tensor_scalar(
                murow[:], pms2[0:1, 0:1024], 1.0 / 2048.0, None, ALU.mult),
                waits=(ev_stat[15],))
            e_mb = P.op("tensor", lambda e: e.matmul(
                pms2[:, 0:512], g["ones_row_f32"][:], murow[:, 0:512],
                start=True, stop=True), waits=(e_mu,), inc="sP")
            e_mb2 = P.op("tensor", lambda e: e.matmul(
                pms2[:, 512:1024], g["ones_row_f32"][:], murow[:, 512:1024],
                start=True, stop=True), inc="sP")
            e_mbv = P.op("vector", lambda e: e.tensor_copy(mubc[:], pms2[:, 0:1024]),
                         waits=(e_mb2,))
            ev_xc = {}
            ev_sq2 = {}
            ev_vst = None
            for dd in range(16):
                ev_xc[dd] = P.op("vector", lambda e, dd=dd: e.tensor_tensor(
                    r_sb[:, dd * 1024:(dd + 1) * 1024],
                    r_sb[:, dd * 1024:(dd + 1) * 1024], mubc[:], ALU.subtract),
                    waits=(e_mbv,))
                ev_sq2[dd] = P.op("scalar", lambda e, dd=dd: e.activation(
                    sqb[:], r_sb[:, dd * 1024:(dd + 1) * 1024], AF.Square),
                    waits=(ev_xc[dd], ev_vst))
                for nn in range(2):
                    ev_vst = P.op(
                        "tensor",
                        lambda e, dd=dd, nn=nn: e.matmul(
                            pms2[0:1, nn * 512:(nn + 1) * 512],
                            g["i2048_col_bf"][:],
                            sqb[:, nn * 512:(nn + 1) * 512],
                            start=(dd == 0), stop=(dd == 15)),
                        waits=(ev_sq2[dd], e_mbv) if nn == 0 else (),
                        inc="sP" if nn == 1 else None)
            e_ln2 = P.op("scalar", lambda e: e.activation(
                rstd2[:], pms2[0:1, 0:1024], AF.Ln, bias=g["eps11"][0:1, 0:1]),
                waits=(ev_vst,))
            e_rst2 = P.op("scalar", lambda e: e.activation(
                rstd2[:], rstd2[:], AF.Exp, scale=-0.5))
            e_rb = P.op("tensor", lambda e: e.matmul(
                pms2[:, 0:512], g["ones_row_f32"][:], rstd2[:, 0:512],
                start=True, stop=True), waits=(e_rst2,), inc="sP")
            e_rb2 = P.op("tensor", lambda e: e.matmul(
                pms2[:, 512:1024], g["ones_row_f32"][:], rstd2[:, 512:1024],
                start=True, stop=True), inc="sP")
            e_rbv = P.op("vector", lambda e: e.tensor_copy(rsbc[:], pms2[:, 0:1024]),
                         waits=(e_rb2,))
            ev_od = {}
            for dd in range(16):
                ex = P.op("vector", lambda e, dd=dd: e.tensor_tensor(
                    xst[dd % 2][:], r_sb[:, dd * 1024:(dd + 1) * 1024], rsbc[:],
                    ALU.mult), waits=(e_rbv, ev_od.get(dd - 2)))
                e1 = P.dma("sync", d_x1[dd, :, tsl], xst[dd % 2][:], waits=(ex,),
                           sem="dZOa" if dd % 2 == 0 else "dZOb")
                ev_od[dd] = P.dma("gpsimd", d_x1bf[dd, :, tsl], xst[dd % 2][:],
                                  waits=(ex,),
                                  sem="dZOa" if dd % 2 == 0 else "dZOb")
        P.flush("p2b")

    with ExitStack() as st:
        def al(name, shape, dt=F32):
            return st.enter_context(nc.sbuf_tensor(name, list(shape), dt))
        X1 = al("X1", [128, 16 * 1024], F32)
        xtb = al("xtb", [128, 16 * 1024], BF16)
        wsb = al("wsb", [128, 16 * 1024], BF16)
        hbf = al("hbf", [128, 8 * 1024], BF16)
        h2w = al("h2w", [128, 8 * 1024], BF16)
        wbc_sb = al("wbc_sb", [128, 1024], F32)
        x1s = [al(f"x1s{i}", [128, 1024], F32) for i in range(2)]
        sqs2 = [al(f"sqs{i}", [128, 1024], F32) for i in range(2)]
        sqs = sqs2[0]
        gfm = al("gfm", [8, 1024], F32)
        gt_t = al("gt_t", [128, 64], F32)
        w_tt = al("w_tt", [128, 64], F32)
        w_fm = al("w_fm", [8, 1024], F32)
        mcol = al("mcol", [128, 4], F32)
        bdn_sb = al("bdn_sb", [NEXP, 2048], F32)
        pu = [st.enter_context(nc.psum_tensor(f"pu{i}", [128, 1024], F32))
              for i in range(2)]
        pdn = [st.enter_context(nc.psum_tensor(f"pdn{i}", [128, 1024], F32))
               for i in range(2)]

        P.begin()
        ev_bdn = None
        for n in range(NEXP):
            ev_bdn = P.dma("sync", bdn_sb[n:n + 1, :], t["bdn"][n:n + 1, :],
                           sem="dC")
        carry_mml = [{}]

        def sA_now():
            return ("sA", P.count["sA"])

        def sV_now():
            return ("sV", P.count["sV"])

        for rr in range(2):
            for tt_ in range(2):
                tsl = slice(tt_ * 1024, (tt_ + 1) * 1024)
                tdep = (("sP", P.count["sP"]), ("sV", P.count["sV"]),
                        ("sA", P.count["sA"]), ("sG", P.count["sG"])
                        ) + P.dma_snapshot()
                src = d_x1bf if rr == 0 else d_x2bf
                ev_xt = P.dma("sync",
                              xtb[:].rearrange("p (d j) -> p d j", j=1024),
                              src[:, :, tsl].rearrange("d p j -> p d j"),
                              waits=tdep, sem="dXa" if tt_ == 0 else "dXb")
                evg = None
                for k in range(16):
                    for nn in range(2):
                        evg = P.op("tensor", lambda e, k=k, nn=nn, rr=rr: e.matmul(
                            pu[0][0:8, nn * 512:(nn + 1) * 512],
                            g["wg_bf"][:, (rr * 16 + k) * 8:(rr * 16 + k + 1) * 8],
                            xtb[:, k * 1024 + nn * 512:k * 1024 + (nn + 1) * 512],
                            start=(k == 0), stop=(k == 15)),
                            waits=(ev_xt,) + tdep if (k == 0 and nn == 0) else (),
                            inc="sP" if (k == 15 and nn == 1) else None)
                e_gf = P.op("scalar", lambda e, rr=rr: e.activation(
                    gfm[:], pu[0][0:8, 0:1024], AF.Identity,
                    bias=g["bg2"][:, rr:rr + 1]), waits=(evg,))
                ev_gt = None
                for j in range(8):
                    ev_gt = P.op("tensor", lambda e, j=j: e.transpose(
                        pdn[0][0:128, j * 8:(j + 1) * 8],
                        gfm[:, j * 128:(j + 1) * 128], g["id_f32"][0:8, 0:8]),
                        waits=(e_gf,) if j == 0 else (), inc="sP")
                e_gtv = P.op("vector", lambda e: e.tensor_copy(
                    gt_t[:], pdn[0][:, 0:64]), waits=(ev_gt,))
                e_wt = None
                for j in range(8):
                    gsl = gt_t[:, j * 8:(j + 1) * 8]
                    wsl = w_tt[:, j * 8:(j + 1) * 8]
                    P.op("vector", lambda e, gsl=gsl: e.tensor_reduce(
                        mcol[:, 0:1], gsl, AX.X, ALU.min), waits=(e_gtv,))
                    P.op("vector", lambda e, gsl=gsl, wsl=wsl: e.tensor_scalar(
                        wsl, gsl, mcol[:, 0:1], None, ALU.is_equal))
                    P.op("vector", lambda e, gsl=gsl, wsl=wsl: e.scalar_tensor_tensor(
                        wsl, wsl, 32768.0, gsl, ALU.mult, ALU.add))
                    P.op("vector", lambda e, wsl=wsl: e.tensor_reduce(
                        mcol[:, 1:2], wsl, AX.X, ALU.min))
                    P.op("vector", lambda e, gsl=gsl, wsl=wsl: e.tensor_scalar(
                        wsl, gsl, mcol[:, 1:2], None, ALU.is_gt))
                    e6 = P.op("vector", lambda e, gsl=gsl: e.tensor_reduce(
                        mcol[:, 2:3], gsl, AX.X, ALU.max, negate=True))
                    e7 = P.op("scalar", lambda e, gsl=gsl, j=j: e.activation(
                        sqs[:, j * 8:(j + 1) * 8], gsl, AF.Exp,
                        bias=mcol[:, 2:3]), waits=(e6,))
                    P.op("vector", lambda e, wsl=wsl, j=j: e.tensor_tensor(
                        wsl, wsl, sqs[:, j * 8:(j + 1) * 8], ALU.mult),
                        waits=(e7,))
                    P.op("vector", lambda e, wsl=wsl: e.tensor_reduce(
                        mcol[:, 3:4], wsl, AX.X, ALU.add))
                    P.op("vector", lambda e: e.reciprocal(
                        mcol[:, 3:4], mcol[:, 3:4]))
                    e_wt = P.op("vector", lambda e, wsl=wsl: e.tensor_scalar(
                        wsl, wsl, mcol[:, 3:4], None, ALU.mult))
                wfm_state = [None]
                if debug and tt_ == 0:
                    P.dma("sync", d_dbgG[rr], gfm[:], waits=(e_gf,))
                    ev_dwf = P.dma("sync", d_dbgWf[rr], w_fm[:], waits=(e_wfv,))
                    if rr == 1:
                        P.dma("sync",
                              d_dbgXt.rearrange("d p j -> p d j"),
                              xtb[:].rearrange("p (d j) -> p d j", j=1024),
                              waits=(ev_xt,))

                R = [wsb[:, 0:8192], wsb[:, 8192:16384]]
                ev_evp = {}
                ev_hb = {}
                ev_h2 = {}
                ev_wbcv = {}
                ev_x1dma = {}
                phases = []

                def ph_up(tag, src_fn, bias_fn, half):
                    def streams(reg_base):
                        return [(reg_base + k * 512, 512,
                                 src_fn(k, half)) for k in range(16)]
                    def mms(Rv, evs, sA0):
                        evm = None
                        for o in range(half * 4, half * 4 + 4):
                            for k in range(16):
                                for nn in range(2):
                                    evm = P.op(
                                        "tensor",
                                        lambda e, o=o, k=k, nn=nn, Rv=Rv, half=half: e.matmul(
                                            pu[o % 2][:, nn * 512:(nn + 1) * 512],
                                            Rv[:, k * 512 + (o - half * 4) * 128:
                                               k * 512 + (o - half * 4) * 128 + 128],
                                            xtb[:, k * 1024 + nn * 512:
                                                k * 1024 + (nn + 1) * 512],
                                            start=(k == 0), stop=(k == 15)),
                                        waits=(evs, ev_xt,
                                               ("sA", ev_hb[(tag, o - 2)][1])
                                               if (tag, o - 2) in ev_hb else sA0)
                                        if (k == 0 and nn == 0) else (),
                                        inc="sP" if (k == 15 and nn == 1) else None)
                            ev_hb[(tag, o)] = P.op(
                                "scalar", lambda e, o=o: e.activation(
                                    hbf[:, o * 1024:(o + 1) * 1024],
                                    pu[o % 2][:, 0:1024], AF.Identity,
                                    bias=bias_fn(o)),
                                waits=(evm,))
                        return evm
                    return streams, mms

                def ph_sw(tag, src_fn, bias_fn, use_wbc):
                    def streams(reg_base):
                        return [(reg_base + k2 * 1024, 1024, src_fn(k2))
                                for k2 in range(8)]
                    def mms(Rv, evs, sA0):
                        if use_wbc:
                            if wfm_state[0] is None:
                                ev_wf = None
                                for j in range(8):
                                    ev_wf = P.op("tensor", lambda e, j=j: e.transpose(
                                        pdn[0][0:8, j * 128:(j + 1) * 128],
                                        w_tt[:, j * 8:(j + 1) * 8],
                                        g["id_f32"][:]),
                                        waits=(e_wt, e_gtv) if j == 0 else (),
                                        inc="sP")
                                wfm_state[0] = P.op("vector", lambda e: e.tensor_copy(
                                    w_fm[:], pdn[0][0:8, 0:1024]), waits=(ev_wf,))
                            e_wfv2 = wfm_state[0]
                            e_wb1 = P.op("tensor", lambda e, n=tag: e.matmul(
                                pdn[1][:, 0:512],
                                g["sel8"][:, n * 128:(n + 1) * 128],
                                w_fm[0:8, 0:512], start=True, stop=True),
                                waits=(e_wfv2, ev_evp.get(("prev15",))), inc="sP")
                            e_wb2 = P.op("tensor", lambda e, n=tag: e.matmul(
                                pdn[1][:, 512:1024],
                                g["sel8"][:, n * 128:(n + 1) * 128],
                                w_fm[0:8, 512:1024], start=True, stop=True),
                                inc="sP")
                            ev_wbcv[tag] = P.op("vector", lambda e: e.tensor_copy(
                                wbc_sb[:], pdn[1][:, 0:1024]),
                                waits=(e_wb2, ev_h2.get(("wbc_last",))))
                        evm = None
                        for o2 in range(8):
                            for k2 in range(8):
                                for nn in range(2):
                                    evm = P.op(
                                        "tensor",
                                        lambda e, o2=o2, k2=k2, nn=nn, Rv=Rv: e.matmul(
                                            pu[o2 % 2][:, nn * 512:(nn + 1) * 512],
                                            Rv[:, k2 * 1024 + o2 * 128:
                                               k2 * 1024 + o2 * 128 + 128],
                                            hbf[:, k2 * 1024 + nn * 512:
                                                k2 * 1024 + (nn + 1) * 512],
                                            start=(k2 == 0), stop=(k2 == 7)),
                                        waits=(evs, ("sA", ev_hb[(tag, 7)][1]),
                                               ev_h2.get((tag, o2 - 2)))
                                        if (k2 == 0 and nn == 0) else (),
                                        inc="sP" if (k2 == 7 and nn == 1) else None)
                            sq = sqs2[o2 % 2]
                            e_si = P.op("scalar", lambda e, o2=o2, sq=sq: e.activation(
                                sq[:], pu[o2 % 2][:, 0:1024], AF.Silu,
                                bias=bias_fn(o2)),
                                waits=(evm, ev_h2.get((tag, o2 - 2))))
                            if use_wbc:
                                e_t1 = P.op("vector", lambda e, o2=o2, sq=sq: e.tensor_tensor(
                                    sq[:], sq[:], hbf[:, o2 * 1024:(o2 + 1) * 1024],
                                    ALU.mult), waits=(e_si,))
                                ev_h2[(tag, o2)] = P.op(
                                    "vector", lambda e, o2=o2, sq=sq: e.tensor_tensor(
                                        h2w[:, o2 * 1024:(o2 + 1) * 1024], sq[:],
                                        wbc_sb[:], ALU.mult),
                                    waits=(ev_wbcv.get(tag),))
                                if o2 == 7:
                                    ev_h2[("wbc_last",)] = ev_h2[(tag, 7)]
                            else:
                                ev_h2[(tag, o2)] = P.op(
                                    "vector", lambda e, o2=o2, sq=sq: e.tensor_tensor(
                                        h2w[:, o2 * 1024:(o2 + 1) * 1024], sq[:],
                                        hbf[:, o2 * 1024:(o2 + 1) * 1024], ALU.mult),
                                    waits=(e_si,))
                        return evm
                    return streams, mms

                def ph_dn(tag, src_fn, half, first, bias_close, sd_half):
                    def streams(reg_base):
                        return [(reg_base + k3 * 1024, 1024, src_fn(k3, half))
                                for k3 in range(8)]
                    def mms(Rv, evs, sA0):
                        evm = None
                        for dd in range(half * 8, half * 8 + 8):
                            for k3 in range(8):
                                for nn in range(2):
                                    is_last = (k3 == 7 and nn == 1)
                                    stp = is_last and not bias_close
                                    evm = P.op(
                                        "tensor",
                                        lambda e, dd=dd, k3=k3, nn=nn, Rv=Rv,
                                        half=half, stp=stp: e.matmul(
                                            pdn[dd % 2][:, nn * 512:(nn + 1) * 512],
                                            Rv[:, k3 * 1024 + (dd - half * 8) * 128:
                                               k3 * 1024 + (dd - half * 8) * 128 + 128],
                                            h2w[:, k3 * 1024 + nn * 512:
                                                k3 * 1024 + (nn + 1) * 512],
                                            start=(k3 == 0), stop=stp),
                                        waits=(evs, ("sV", ev_h2[(tag, 7)][1]),
                                               ev_evp.get((tag, dd - 2))
                                               if dd - 2 >= half * 8 else
                                               ev_evp.get(("prev", 14 + dd)),
                                               ev_evp.get(("prev", dd)))
                                        if (k3 == 0 and nn == 0) else (),
                                        inc="sP" if (is_last and not bias_close)
                                        else None)
                            if bias_close:
                                for nn in range(2):
                                    evm = P.op("tensor", lambda e, dd=dd, nn=nn: e.matmul(
                                        pdn[dd % 2][:, nn * 512:(nn + 1) * 512],
                                        bdn_sb[:, dd * 128:(dd + 1) * 128],
                                        w_fm[0:6, nn * 512:(nn + 1) * 512],
                                        start=False, stop=(nn == 1)),
                                        waits=(wfm_state[0], ev_bdn)
                                        if nn == 0 else (),
                                        inc="sP" if nn == 1 else None)
                            if first:
                                ev_x1dma[dd] = P.dma(
                                    "sync", x1s[dd % 2][:], d_x1[dd, :, tsl],
                                    waits=(ev_evp.get((tag, dd - 2)),),
                                    sem="dUa" if dd % 2 == 0 else "dUb")
                                ev_evp[(tag, dd)] = P.op(
                                    "vector", lambda e, dd=dd: e.tensor_tensor(
                                        X1[:, dd * 1024:(dd + 1) * 1024],
                                        pdn[dd % 2][:, 0:1024], x1s[dd % 2][:],
                                        ALU.add),
                                    waits=(evm, ev_x1dma[dd]))
                            elif sd_half == 1:
                                ev_evp[(tag, dd)] = P.op(
                                    "vector", lambda e, dd=dd: e.scalar_tensor_tensor(
                                        X1[:, dd * 1024:(dd + 1) * 1024],
                                        pdn[dd % 2][:, 0:1024],
                                        g["bsd_pm"][:, dd:dd + 1],
                                        X1[:, dd * 1024:(dd + 1) * 1024],
                                        ALU.add, ALU.add),
                                    waits=(evm, ev_evp.get(("prev", dd))))
                            else:
                                ev_evp[(tag, dd)] = P.op(
                                    "vector", lambda e, dd=dd: e.tensor_tensor(
                                        X1[:, dd * 1024:(dd + 1) * 1024],
                                        X1[:, dd * 1024:(dd + 1) * 1024],
                                        pdn[dd % 2][:, 0:1024], ALU.add),
                                    waits=(evm, ev_evp.get(("prev", dd))))
                        return evm
                    return streams, mms

                def close_tag(tag):
                    for dd in range(16):
                        ev_evp[("prev", dd)] = ev_evp[(tag, dd)]
                    ev_evp[("prev15",)] = ev_evp[(tag, 15)]

                for n in range(NEXP):
                    up_src = (lambda n: lambda k, half: t["wup"][
                        n, k * 128:(k + 1) * 128, half * 512:(half + 1) * 512])(n)
                    sw_src = (lambda n: lambda k2: t["wsw"][
                        n, k2 * 128:(k2 + 1) * 128, :])(n)
                    dn_src = (lambda n: lambda k3, half: t["wdn"][
                        n, k3 * 128:(k3 + 1) * 128,
                        half * 1024:(half + 1) * 1024])(n)
                    upb = (lambda n: lambda o: g["bup_pm"][
                        :, n * 8 + o:n * 8 + o + 1])(n)
                    swb = (lambda n: lambda o2: g["bsw_pm"][
                        :, n * 8 + o2:n * 8 + o2 + 1])(n)
                    phases.append((ph_up(n, up_src, upb, 0), n))
                    phases.append((ph_up(n, up_src, upb, 1), n))
                    phases.append((ph_sw(n, sw_src, swb, True), n))
                    phases.append((ph_dn(n, dn_src, 0, n == 0, False, None), n))
                    phases.append((ph_dn(n, dn_src, 1, n == 0, False, None), n))
                for sh_ in range(2):
                    tag = ("s", sh_)
                    su_src = (lambda sh_: lambda k, half: t["wsu"][
                        k * 128:(k + 1) * 128,
                        sh_ * 1024 + half * 512:sh_ * 1024 + (half + 1) * 512])(sh_)
                    sa_src = lambda k2: t["wsa"][k2 * 128:(k2 + 1) * 128, :]
                    sd_src = (lambda sh_: lambda k3, half: t["wsd"][
                        (sh_ * 8 + k3) * 128:(sh_ * 8 + k3 + 1) * 128,
                        half * 1024:(half + 1) * 1024])(sh_)
                    sub = (lambda sh_: lambda o: g["bsu_pm"][
                        :, sh_ * 8 + o:sh_ * 8 + o + 1])(sh_)
                    sab = lambda o2: g["bsa_pm"][:, o2:o2 + 1]
                    phases.append((ph_up(tag, su_src, sub, 0), tag))
                    phases.append((ph_up(tag, su_src, sub, 1), tag))
                    phases.append((ph_sw(tag, sa_src, sab, False), tag))
                    phases.append((ph_dn(tag, sd_src, 0, False,
                                          sh_ == 1, sh_), tag))
                    phases.append((ph_dn(tag, sd_src, 1, False,
                                          sh_ == 1, sh_), tag))

                nph = len(phases)
                ev_stream = {}
                ev_mml = dict(carry_mml[0])

                def emit_stream(i):
                    (streams, _), _tg = phases[i]
                    reg = i % 2
                    evd = None
                    w0 = (ev_mml.get(i - 2),) + tdep
                    for j, (off, width, src) in enumerate(streams(reg * 8192)):
                        evd = P.dma("gpsimd", wsb[:, off:off + width], src,
                                    waits=w0 if j == 0 else (),
                                    sem="dWa" if reg == 0 else "dWb")
                    ev_stream[i] = evd

                emit_stream(0)
                emit_stream(1)
                tag_done = set()
                for i in range(nph):
                    (streams, mms), tg = phases[i]
                    sA0 = ("sA", P.count["sA"])
                    ev_mml[i] = mms(R[i % 2], ev_stream[i], sA0)
                    if i + 2 < nph:
                        emit_stream(i + 2)
                    if (i % 5) == 4:
                        close_tag(tg)
                carry_mml[0] = {-2: ev_mml[nph - 2], -1: ev_mml[nph - 1]}

                if rr == 0:
                    for dd in range(16):
                        P.dma("gpsimd", d_x2bf[dd, :, tsl],
                              X1[:, dd * 1024:(dd + 1) * 1024],
                              waits=(ev_evp[(("s", 1), dd)],),
                              sem="dZOa" if dd % 2 == 0 else "dZOb")
                if rr == 1:
                    if debug:
                        for dd in range(16):
                            P.dma("sync", d_x3[dd, :, tsl],
                                  X1[:, dd * 1024:(dd + 1) * 1024],
                                  waits=(ev_evp[(("s", 1), dd)],))
                    ev_fx = {}
                    ev_fr = {}
                    ev_fst = None
                    for dd in range(16):
                        ev_fx[dd] = P.dma("sync", x1s[dd % 2][:],
                                          d_x1[dd, :, tsl],
                                          waits=(ev_fr.get(dd - 2),
                                                 ev_evp[(0, 14 + dd)] if dd < 2
                                                 else None),
                                          sem="dUa" if dd % 2 == 0 else "dUb")
                        ev_fr[dd] = P.op("vector", lambda e, dd=dd: e.tensor_tensor(
                            X1[:, dd * 1024:(dd + 1) * 1024],
                            X1[:, dd * 1024:(dd + 1) * 1024], x1s[dd % 2][:],
                            ALU.add), waits=(ev_fx[dd], ev_evp[(("s", 1), dd)],
                                             ("dS", P.count["dS"]) if debug else None))
                        for nn in range(2):
                            ev_fst = P.op(
                                "tensor",
                                lambda e, dd=dd, nn=nn: e.matmul(
                                    pu[0][0:1, nn * 512:(nn + 1) * 512],
                                    g["ones_col_f32"][:],
                                    X1[:, dd * 1024 + nn * 512:
                                       dd * 1024 + (nn + 1) * 512],
                                    start=(dd == 0), stop=(dd == 15)),
                                waits=(ev_fr[dd], sA_now()) if nn == 0 else (),
                                inc="sP" if (dd == 15 and nn == 1) else None)
                    e_mu = P.op("vector", lambda e: e.tensor_scalar(
                        gfm[0:1, 0:1024], pu[0][0:1, 0:1024], 1.0 / 2048.0, None,
                        ALU.mult), waits=(ev_fst,))
                    e_mb = P.op("tensor", lambda e: e.matmul(
                        pu[0][:, 0:512], g["ones_row_f32"][:], gfm[0:1, 0:512],
                        start=True, stop=True), waits=(e_mu,), inc="sP")
                    e_mb2 = P.op("tensor", lambda e: e.matmul(
                        pu[0][:, 512:1024], g["ones_row_f32"][:],
                        gfm[0:1, 512:1024], start=True, stop=True), inc="sP")
                    e_mbv = P.op("vector", lambda e: e.tensor_copy(
                        wbc_sb[:], pu[0][:, 0:1024]), waits=(e_mb2,))
                    ev_vst = None
                    for dd in range(16):
                        exc = P.op("vector", lambda e, dd=dd: e.tensor_tensor(
                            X1[:, dd * 1024:(dd + 1) * 1024],
                            X1[:, dd * 1024:(dd + 1) * 1024], wbc_sb[:],
                            ALU.subtract), waits=(e_mbv,))
                        esq = P.op("scalar", lambda e, dd=dd: e.activation(
                            sqs[:], X1[:, dd * 1024:(dd + 1) * 1024], AF.Square),
                            waits=(exc, ev_vst))
                        for nn in range(2):
                            ev_vst = P.op(
                                "tensor",
                                lambda e, dd=dd, nn=nn: e.matmul(
                                    pu[1][0:1, nn * 512:(nn + 1) * 512],
                                    g["ones_col_f32"][:],
                                    sqs[:, nn * 512:(nn + 1) * 512],
                                    start=(dd == 0), stop=(dd == 15)),
                                waits=(esq,) if nn == 0 else (),
                                inc="sP" if nn == 1 else None)
                    e_v2 = P.op("vector", lambda e: e.tensor_scalar(
                        gfm[0:1, 0:1024], pu[1][0:1, 0:1024], 1.0 / 2048.0, None,
                        ALU.mult), waits=(ev_vst,))
                    e_ln3 = P.op("scalar", lambda e: e.activation(
                        gfm[0:1, 0:1024], gfm[0:1, 0:1024], AF.Ln,
                        bias=g["eps11"][0:1, 0:1]), waits=(e_v2,))
                    e_rs3 = P.op("scalar", lambda e: e.activation(
                        gfm[0:1, 0:1024], gfm[0:1, 0:1024], AF.Exp, scale=-0.5))
                    e_rb = P.op("tensor", lambda e: e.matmul(
                        pu[1][:, 0:512], g["ones_row_f32"][:], gfm[0:1, 0:512],
                        start=True, stop=True), waits=(e_rs3,), inc="sP")
                    e_rb2 = P.op("tensor", lambda e: e.matmul(
                        pu[1][:, 512:1024], g["ones_row_f32"][:],
                        gfm[0:1, 512:1024], start=True, stop=True), inc="sP")
                    e_rbv = P.op("vector", lambda e: e.tensor_copy(
                        wbc_sb[:], pu[1][:, 0:1024]), waits=(e_rb2,))
                    ev_odma = {}
                    for dd in range(16):
                        eo = P.op("vector", lambda e, dd=dd: e.tensor_tensor(
                            x1s[dd % 2][:], X1[:, dd * 1024:(dd + 1) * 1024],
                            wbc_sb[:], ALU.mult),
                            waits=(e_rbv, ev_odma.get(dd - 2)))
                        ev_odma[dd] = P.dma("sync", d_out[dd, :, tsl],
                                            x1s[dd % 2][:], waits=(eo,),
                                            sem="dOUT")
        P.final_wait()
        P.flush("p4")

    nc._prog = P
    return nc



def _host_arrays(inp):
    a = {}
    a["w_inT"] = np.ascontiguousarray(inp["L0_attn_W_in"].T)
    a["w_outT"] = np.ascontiguousarray(inp["L0_attn_W_out"].T)
    a["conv_w_pm"] = np.ascontiguousarray(
        inp["L0_attn_conv_w"].reshape(34, 128, 4).transpose(1, 0, 2).reshape(128, 136))
    a["conv_b_pm"] = np.ascontiguousarray(inp["L0_attn_conv_b"].reshape(34, 128).T)
    a["dtb_col"] = np.ascontiguousarray(inp["L0_attn_dt_bias"].reshape(NH, 1))
    a["alog_col"] = np.ascontiguousarray(inp["L0_attn_A_log"].reshape(NH, 1))
    a["dp_pm"] = np.ascontiguousarray(
        np.repeat(inp["L0_attn_Dp"], HP).reshape(32, 128).T)
    a["nw_pm"] = np.ascontiguousarray(inp["L0_attn_norm_w"].reshape(32, 128).T)
    a["wg"] = np.ascontiguousarray(inp["L0_ffn_Wg"])
    a["bg_col"] = np.ascontiguousarray(inp["L0_ffn_bg"].reshape(2, 8, 1))
    a["wup"] = np.ascontiguousarray(inp["L0_ffn_Wup"])
    a["bup_pm"] = np.ascontiguousarray(
        inp["L0_ffn_bup"].reshape(NEXP, 8, 128).transpose(0, 2, 1))
    a["wsw"] = np.ascontiguousarray(inp["L0_ffn_Wsw"])
    a["bsw_pm"] = np.ascontiguousarray(
        inp["L0_ffn_bsw"].reshape(NEXP, 8, 128).transpose(0, 2, 1))
    a["wdn"] = np.ascontiguousarray(inp["L0_ffn_Wdn"])
    a["bdn"] = np.ascontiguousarray(inp["L0_ffn_bdn"])
    a["wsu"] = np.ascontiguousarray(inp["L0_ffn_Wsu"])
    a["bsu_pm"] = np.ascontiguousarray(inp["L0_ffn_bsu"].reshape(16, 128).T)
    a["wsa"] = np.ascontiguousarray(inp["L0_ffn_Wsa"])
    a["bsa_pm"] = np.ascontiguousarray(inp["L0_ffn_bsa"].reshape(8, 128).T)
    a["wsd"] = np.ascontiguousarray(inp["L0_ffn_Wsd"])
    a["bsd_pm"] = np.ascontiguousarray(inp["L0_ffn_bsd"].reshape(16, 128).T)
    j = np.arange(128)[:, None]
    i = np.arange(256)[None, :]
    a["m1a"] = (i >= j).astype(np.float32)
    a["m1b"] = (i >= j + 128).astype(np.float32)
    a["u01_bf"] = np.concatenate([(i < j), (i < j + 128)], axis=1).astype(BF)
    a["negdiag_bf"] = (-BIGM * np.eye(128)).astype(BF)
    sel = np.zeros((128, NH, 128), np.float32)
    for h in range(NH):
        sel[h, h, :] = 1.0
        sel[64 + h, h, :] = 1.0
    a["sel_bf"] = sel.reshape(128, NH * 128).astype(BF)
    sel8 = np.zeros((8, 8, 128), np.float32)
    for n in range(8):
        sel8[n, n, :] = 1.0
    a["sel8"] = sel8.reshape(8, 8 * 128)
    a["id_bf"] = np.eye(128).astype(BF)
    a["id_f32"] = np.eye(128, dtype=np.float32)
    a["ones_col_bf"] = np.ones((128, 1), BF)
    a["i4096_col_bf"] = np.full((128, 1), 1.0 / 4096.0, BF)
    a["i2048_col_bf"] = np.full((128, 1), 1.0 / 2048.0, BF)
    a["eps11"] = np.full((1, 1), 1e-5, np.float32)
    a["bg2"] = np.ascontiguousarray(inp["L0_ffn_bg"].T)
    a["ones_col_f32"] = np.ones((128, 1), np.float32)
    a["ones_row_f32"] = np.ones((1, 128), np.float32)
    for k in a:
        if a[k].dtype == np.float64:
            a[k] = a[k].astype(np.float32)
    return a


def kernel(**inputs):
    shared = _host_arrays(inputs)
    u = np.asarray(inputs["L0_attn_u"], np.float32)
    nc = build(debug=False)
    in_maps = []
    for b in range(8):
        m = dict(shared)
        m["u_fm"] = np.ascontiguousarray(u[b].T)
        in_maps.append(m)
    res = run_bass_kernel_spmd(nc, in_maps, core_ids=list(range(8)))
    out = np.empty((8, 2048, 2048), np.float32)
    for b, r in enumerate(res.results):
        out[b] = np.asarray(r["out"]).reshape(2048, 2048).T
    return out
